# version 06
# baseline (speedup 1.0000x reference)
import os
import math
import numpy as np
import ml_dtypes
import concourse.bass as bass
import concourse.mybir as mybir
from concourse.bass_utils import run_bass_kernel_spmd
from contextlib import ExitStack

F32 = mybir.dt.float32
BF16 = mybir.dt.bfloat16
AF = mybir.ActivationFunctionType
ALU = mybir.AluOpType
AX = mybir.AxisListType

NCORE = 8
S = 8192
D = 4096
TOK = 1024
NH = 32
NPAIR = 16
QL = 1536
KVL = 512
QH = 192
NOPE = 128
VH = 128
ISQ = 1.0 / math.sqrt(QH)
EPS = 1e-5

_DEBUG = int(os.environ.get("MK_DEBUG", "0"))
_NPAIR_RUN = int(os.environ.get("MK_NPAIR", str(NPAIR)))
_PHASE = int(os.environ.get("MK_PHASE", "3"))
_TBS = int(os.environ.get("MK_TBS", "16"))
_SUB = int(os.environ.get("MK_SUB", "15"))
_TPV = int(os.environ.get("MK_TPV", "0"))
_TTS = int(os.environ.get("MK_TTS", "4"))


class Sem:
    def __init__(self, handle, name):
        self.h = handle
        self.name = name
        self.n = 0


class Sched:
    ENGINES = ("t", "v", "s", "g", "y")

    def __init__(self):
        self.ops = {e: [] for e in self.ENGINES}
        self.sems = {}

    def sem(self, name, handle):
        sm = Sem(handle, name)
        self.sems[name] = sm
        return sm

    def emit(self, e, fn, sem=None, k=1):
        if sem is None:
            self.ops[e].append(fn)
            return None
        sem.n += k
        tag = sem.n

        def run(eng, fn=fn, s=sem.h, k=k):
            fn(eng).then_inc(s, k)

        self.ops[e].append(run)
        return (sem, tag)

    def wait(self, e, tok):
        if tok is None:
            return
        sem, tag = tok
        if tag <= 0:
            return
        self.ops[e].append(lambda eng, s=sem.h, t=tag: eng.wait_ge(s, t))

    def barrier(self):
        toks = [(sm, sm.n) for sm in self.sems.values() if sm.n > 0]
        for e in self.ENGINES:
            for tok in toks:
                self.wait(e, tok)


def build_nc(debug=0):
    nc = bass.Bass("TRN2", target_bir_lowering=False)

    d_x = nc.dram_tensor("x_rot", [S, D], F32, kind="ExternalInput")
    d_wqcT = nc.dram_tensor("wqcT", [D, QL], F32, kind="ExternalInput")
    d_bqc = nc.dram_tensor("bqc_t", [128, 12], F32, kind="ExternalInput")
    d_wqup = nc.dram_tensor("wqup_til", [128, NPAIR, 12 * 384], F32, kind="ExternalInput")
    d_bqup = nc.dram_tensor("bqup_t", [128, 48], F32, kind="ExternalInput")
    d_wkvc = nc.dram_tensor("wkvc_til", [128, 32 * 576], F32, kind="ExternalInput")
    d_bkvc = nc.dram_tensor("bkvc_t", [128, 5], F32, kind="ExternalInput")
    d_wkK = nc.dram_tensor("wkK_til", [128, NPAIR, 4 * 256], F32, kind="ExternalInput")
    d_bkK = nc.dram_tensor("bkupK_t", [128, 32], F32, kind="ExternalInput")
    d_wkV = nc.dram_tensor("wkV_til", [128, NPAIR, 4 * 256], F32, kind="ExternalInput")
    d_bkV = nc.dram_tensor("bkvV_cols", [128, 32], F32, kind="ExternalInput")
    d_woT = nc.dram_tensor("woT", [VH, D], F32, kind="ExternalInput")
    d_bo = nc.dram_tensor("bo_row", [1, D], F32, kind="ExternalInput")
    d_wfT = nc.dram_tensor("wffn_til", [128, 8, 32 * 512], F32, kind="ExternalInput")
    d_fb = nc.dram_tensor("ffnb_row", [1, D], F32, kind="ExternalInput")
    d_laT = nc.dram_tensor("loraA_til", [128, 32 * 128], F32, kind="ExternalInput")
    d_lbT = nc.dram_tensor("loraBT", [128, D], F32, kind="ExternalInput")
    d_lw = nc.dram_tensor("lwrep", [128, 1], F32, kind="ExternalInput")
    d_id = nc.dram_tensor("ident", [128, 128], F32, kind="ExternalInput")
    d_ones = nc.dram_tensor("ones_row", [1, 128], F32, kind="ExternalInput")
    d_onec = nc.dram_tensor("onecol_bf", [128, 1], BF16, kind="ExternalInput")

    d_kvF = nc.dram_tensor("kvF_dram", [576, S], BF16)
    d_qcF = nc.dram_tensor("qcF_dram", [2, 128, 12 * 512], BF16)
    d_x2 = nc.dram_tensor("x2_dram", [8, TOK, 512], F32)
    d_out = nc.dram_tensor("out_c", [TOK, D], F32, kind="ExternalOutput")

    dbg = {}
    if debug >= 1:
        dbg["kvF"] = nc.dram_tensor("dbg_kvF", [576, S], BF16, kind="ExternalOutput")
        dbg["qcF"] = nc.dram_tensor("dbg_qcF", [2, 128, 12 * 512], BF16, kind="ExternalOutput")
    if debug >= 2:
        dbg["K0"] = nc.dram_tensor("dbg_K0", [NOPE, S], BF16, kind="ExternalOutput")
        dbg["K1"] = nc.dram_tensor("dbg_K1", [NOPE, S], BF16, kind="ExternalOutput")
        dbg["V0"] = nc.dram_tensor("dbg_V0", [128, 64 * 256], BF16, kind="ExternalOutput")
        dbg["Q0"] = nc.dram_tensor("dbg_Q0", [128, 3 * TOK], BF16, kind="ExternalOutput")
        dbg["osum"] = nc.dram_tensor("dbg_osum", [VH, TOK], F32, kind="ExternalOutput")
    if debug >= 3:
        dbg["x2"] = nc.dram_tensor("dbg_x2", [8, TOK, 512], F32, kind="ExternalOutput")

    sched = Sched()
    em, wt = sched.emit, sched.wait

    with ExitStack() as ctx:
        E = ctx.enter_context

        sb_id = E(nc.sbuf_tensor("sb_id", [128, 128], F32))
        sb_ones = E(nc.sbuf_tensor("sb_ones", [1, 128], F32))
        sb_onec = E(nc.sbuf_tensor("sb_onec", [128, 1], BF16))
        sb_bqc = E(nc.sbuf_tensor("sb_bqc", [128, 12], F32))
        sb_bqup = E(nc.sbuf_tensor("sb_bqup", [128, 48], F32))
        sb_bkvc = E(nc.sbuf_tensor("sb_bkvc", [128, 5], F32))
        sb_bkK = E(nc.sbuf_tensor("sb_bkK", [128, 32], F32))
        sb_bkV = E(nc.sbuf_tensor("sb_bkV", [128, 32], F32))
        sb_lw = E(nc.sbuf_tensor("sb_lw", [128, 1], F32))
        sb_coef = E(nc.sbuf_tensor("sb_coef", [128, 1], F32))
        sb_bvsum = E(nc.sbuf_tensor("sb_bvsum", [128, 1], F32))
        st_sum = E(nc.sbuf_tensor("st_sum", [128, 2], F32))
        st_sqs = E(nc.sbuf_tensor("st_sqs", [128, 2], F32))
        st_nm = E(nc.sbuf_tensor("st_nm", [128, 2], F32))
        st_m2 = E(nc.sbuf_tensor("st_m2", [128, 2], F32))
        st_var = E(nc.sbuf_tensor("st_var", [128, 2], F32))
        st_sq = E(nc.sbuf_tensor("st_sq", [128, 2], F32))
        st_rs = E(nc.sbuf_tensor("st_rs", [128, 2], F32))
        sb_stab = E(nc.sbuf_tensor("sb_stab", [128, 8], F32))
        sb_qmax = E(nc.sbuf_tensor("sb_qmax", [128, 1], F32))
        sb_kmax = E(nc.sbuf_tensor("sb_kmax", [128, 1], F32))
        sb_qm = E(nc.sbuf_tensor("sb_qm", [1, 1], F32))
        sb_km = E(nc.sbuf_tensor("sb_km", [1, 1], F32))
        sb_cqk = E(nc.sbuf_tensor("sb_cqk", [1, 1], F32))
        sb_negc_col = E(nc.sbuf_tensor("sb_negc_col", [128, 1], F32))
        sb_recip0 = E(nc.sbuf_tensor("sb_recip0", [1, 512], F32))
        sb_recip1 = E(nc.sbuf_tensor("sb_recip1", [1, 512], F32))

        AR_N = 51456
        OS_BASE = AR_N - 1536
        arena = E(nc.sbuf_tensor("arena", [128, AR_N], F32))

        class Carver:
            def __init__(self, base=0, limit=AR_N):
                self.off = base
                self.limit = limit

            def take(self, nfree, dtype):
                ef32 = nfree if dtype == F32 else (nfree + 1) // 2
                ef32 = (ef32 + 15) & ~15
                assert self.off + ef32 <= self.limit, \
                    f"arena overflow {self.off}+{ef32} > {self.limit}"
                ap = arena[:, self.off:self.off + ef32]
                self.off += ef32
                if dtype == BF16:
                    ap = ap.bitcast(BF16)[:, 0:nfree]
                return ap

        ps = E(nc.psum_tensor("ps", [128, 4096], F32))

        def bank(b):
            return ps[:, b * 512:(b + 1) * 512]

        bankfree = [None] * 8

        sm_gc = sched.sem("gc", E(nc.semaphore("sm_gc")))
        sm_t = sched.sem("t", E(nc.semaphore("sm_t")))
        sm_v = sched.sem("v", E(nc.semaphore("sm_v")))
        sm_s = sched.sem("s", E(nc.semaphore("sm_s")))
        dma_sems = {}
        for nm in ("c", "x0", "x1", "x2b", "wkvc", "wq0", "wq1", "kv0", "kv1",
                   "qc0", "qc1", "kvn", "qch0", "qch1", "wkK", "wkV", "wqup",
                   "cw", "cwy", "wf0", "wf1", "x2w", "x2r0", "x2r1", "o0", "o1", "st"):
            dma_sems[nm] = sched.sem(nm, E(nc.semaphore("sm_" + nm)))
        sm_st = dma_sems["st"]

        def dma(e, out, in_, sem):
            if isinstance(sem, str):
                sem = dma_sems[sem]
            return em(e, lambda eng, o=out, i=in_: eng.dma_start(out=o, in_=i), sem, 16)

        tok_consts = None
        for (dst, src) in [
            (sb_id[:, :], d_id.ap()), (sb_ones[:, :], d_ones.ap()),
            (sb_onec[:, :], d_onec.ap()), (sb_bqc[:, :], d_bqc.ap()),
            (sb_bqup[:, :], d_bqup.ap()), (sb_bkvc[:, :], d_bkvc.ap()),
            (sb_bkK[:, :], d_bkK.ap()), (sb_bkV[:, :], d_bkV.ap()),
            (sb_lw[:, :], d_lw.ap()),
        ]:
            tok_consts = dma("y", dst, src, "c")
        wt("v", tok_consts)
        em("v", lambda v: v.tensor_scalar(
            sb_coef[:, :], sb_lw[:, :], scalar1=2.0, scalar2=None, op0=ALU.mult), sm_v)
        tok_bvsum = em("v", lambda v: v.reduce_sum(sb_bvsum[:, :], sb_bkV[:, :], AX.X), sm_v)

        ln_rot = [0]
        slot_sum_free = [None, None]
        slot_sqs_free = [None, None]
        slot_sq_free = [None, None]

        def emit_ln(xbuf, xnbuf, xsqbuf, tok_loaded_v, tok_loaded_s,
                    extra_v_wait=None, extra_s_wait=None):
            r = ln_rot[0] % 2
            ln_rot[0] += 1
            sl = slice(r, r + 1)
            if extra_v_wait is not None:
                wt("v", extra_v_wait)
            wt("s", tok_loaded_s)
            if extra_s_wait is not None:
                wt("s", extra_s_wait)
            wt("s", slot_sum_free[r])
            t1 = em("s", lambda s, sl=sl, xb=xbuf, xo=xnbuf: s.activation(
                xo, xb, AF.Copy, accum_out=st_sum[:, sl]), sm_s)
            wt("s", slot_sqs_free[r])
            wt("s", t1)
            t2 = em("s", lambda s, sl=sl, xb=xbuf, xq=xsqbuf: s.activation(
                xq, xb, AF.Square, accum_out=st_sqs[:, sl]), sm_s)
            wt("v", t1)
            t3 = em("v", lambda v, sl=sl: v.tensor_scalar(
                st_nm[:, sl], st_sum[:, sl], scalar1=-1.0 / D, scalar2=None, op0=ALU.mult), sm_v)
            slot_sum_free[r] = t3
            wt("v", t3)
            t4 = em("v", lambda v, sl=sl: v.tensor_tensor(
                st_m2[:, sl], st_nm[:, sl], st_nm[:, sl], ALU.mult), sm_v)
            wt("v", t2)
            wt("v", t4)
            t5 = em("v", lambda v, sl=sl: v.tensor_scalar(
                st_var[:, sl], st_sqs[:, sl], scalar1=1.0 / D, scalar2=st_m2[:, sl],
                op0=ALU.mult, op1=ALU.subtract), sm_v)
            slot_sqs_free[r] = t5
            wt("v", t5)
            t6 = em("v", lambda v, sl=sl: v.tensor_scalar(
                st_var[:, sl], st_var[:, sl], scalar1=EPS, scalar2=None, op0=ALU.add), sm_v)
            wt("s", t6)
            wt("s", slot_sq_free[r])
            t7 = em("s", lambda s, sl=sl: s.activation(st_sq[:, sl], st_var[:, sl], AF.Sqrt), sm_s)
            wt("v", t7)
            t8 = em("v", lambda v, sl=sl: v.reciprocal(st_rs[:, sl], st_sq[:, sl]), sm_v)
            slot_sq_free[r] = t8
            wt("v", t8)
            wt("v", tok_loaded_v)
            t9 = em("v", lambda v, sl=sl, xb=xbuf, xo=xnbuf: v.tensor_scalar(
                xo, xb, scalar1=st_nm[:, sl], scalar2=st_rs[:, sl],
                op0=ALU.add, op1=ALU.mult), sm_v)
            return t9

        A = Carver()
        a_x = [A.take(D, F32) for _ in range(3)]
        a_xsq = A.take(D, F32)
        a_xn = [A.take(D, F32) for _ in range(2)]
        a_xnF = [A.take(512, BF16) for _ in range(32)]
        a_wkvc = A.take(32 * 576, BF16).rearrange("p (c f) -> p c f", c=32)
        a_wqct = [A.take(4 * QL, BF16).rearrange("p (c f) -> p c f", c=4) for _ in range(2)]
        a_kvst = [A.take(512, BF16) for _ in range(2)]
        a_qcst = [A.take(512, BF16) for _ in range(2)]

        tok_wkvc = dma("g", a_wkvc, d_wkvc.ap().rearrange("p (c f) -> p c f", c=32), "wkvc")

        tok_x_free = [None, None, None]
        tok_xn_cons = [None, None]
        tok_xnF_ready = [None] * 32
        tok_xnF_free = None
        tok_kvst_free = [None, None]
        tok_qcst_free = [None, None]
        qc_buf_free = [None, None]
        tok_qc_last = [None] * 6
        pend_kv = [None]

        for tb in range(min(16, _TBS)):
            for tt in range(min(4, _TTS)):
                tile = tb * 4 + tt
                bi = tile % 3
                bj = tile % 2
                wt("y", tok_x_free[bi])
                tok_ld = dma("y", a_x[bi], d_x[tile * 128:(tile + 1) * 128, :],
                             ("x0", "x1", "x2b")[bi])
                tok_xn = emit_ln(a_x[bi], a_xn[bj], a_xsq, tok_ld, tok_ld,
                                 extra_v_wait=tok_xn_cons[bj])
                tok_x_free[bi] = tok_xn
                if not (_SUB & 8):
                    continue
                wt("t", tok_xn)
                last_tp = None
                for dc in range(32):
                    b = 5 + (dc % 2)
                    wt("t", bankfree[b])
                    tok_tp = em("t", lambda t, b=b, xb=a_xn[bj], dc=dc: t.transpose(
                        bank(b)[:, 0:128], xb[:, dc * 128:dc * 128 + 128], sb_id[:, :]), sm_t)
                    last_tp = tok_tp
                    if _TPV == 1:
                        continue
                    eng = "s" if dc % 2 == 0 else "v"
                    if _TPV == 2:
                        eng = "v"
                    if _TPV == 3:
                        eng = "s"
                    wt(eng, tok_tp)
                    if tt == 0:
                        wt(eng, tok_xnF_free)
                    if eng == "s":
                        tok_ev = em(eng, lambda e, b=b, dc=dc, tt=tt: e.copy(
                            a_xnF[dc][:, tt * 128:tt * 128 + 128], bank(b)[:, 0:128]), sm_s)
                    else:
                        tok_ev = em(eng, lambda e, b=b, dc=dc, tt=tt: e.tensor_copy(
                            a_xnF[dc][:, tt * 128:tt * 128 + 128], bank(b)[:, 0:128]), sm_v)
                    bankfree[b] = tok_ev
                    if tt == 3:
                        tok_xnF_ready[dc] = tok_ev
                tok_xn_cons[bj] = last_tp
            if pend_kv[0] is not None:
                pend_kv[0]()
                pend_kv[0] = None
            wt("t", tok_wkvc)
            for dc in range(32):
                wt("t", tok_xnF_ready[dc])
            tok_kv_last = [None] * 5
            for f in range(5):
                rows = 128 if f < 4 else 64
                wt("t", bankfree[f])
                tok_mm = None
                for dc in range(32):
                    tok_mm = em("t", lambda t, f=f, dc=dc, rows=rows: t.matmul(
                        bank(f)[0:rows, 0:512],
                        a_wkvc[:, dc, f * 128:f * 128 + rows],
                        a_xnF[dc],
                        start=(dc == 0), stop=(dc == 31)),
                        sm_t if dc == 31 else None)
                tok_kv_last[f] = tok_mm
            def flush_kv(tb=tb, toks=list(tok_kv_last)):
                for f in range(5):
                    rows = 128 if f < 4 else 64
                    kb = f % 2
                    wt("v", toks[f])
                    wt("v", tok_kvst_free[kb])
                    tok_ev = em("v", lambda v, f=f, kb=kb, rows=rows: v.tensor_scalar(
                        a_kvst[kb][0:rows, :], bank(f)[0:rows, 0:512],
                        scalar1=sb_bkvc[0:rows, f:f + 1], scalar2=None, op0=ALU.add), sm_v)
                    bankfree[f] = tok_ev
                    wt("y", tok_ev)
                    tok_st = dma("y", bass.AP(d_kvF, f * 128 * S + tb * 512,
                                              [[S, rows], [1, 512]]),
                                 a_kvst[kb][0:rows, :], "kv0" if kb == 0 else "kv1")
                    if debug >= 1:
                        tok_st = dma("y", bass.AP(dbg["kvF"], f * 128 * S + tb * 512,
                                                  [[S, rows], [1, 512]]),
                                     a_kvst[kb][0:rows, :], "kv0" if kb == 0 else "kv1")
                    tok_kvst_free[kb] = tok_st
            pend_kv[0] = flush_kv
            if tb < 2:
                if pend_kv[0] is not None:
                    pend_kv[0]()
                    pend_kv[0] = None
            if tb < 2 and (_SUB & 4):
                for pas in range(2):
                    tok_mm = None
                    for dcg in range(8):
                        gb = dcg % 2
                        wt("g", qc_buf_free[gb])
                        tok_w = dma("g", a_wqct[gb],
                                    bass.AP(d_wqcT, dcg * 4 * 128 * QL,
                                            [[QL, 128], [128 * QL, 4], [1, QL]]),
                                    "wq0" if gb == 0 else "wq1")
                        wt("t", tok_w)
                        for di in range(4):
                            dc = dcg * 4 + di
                            for f6 in range(6):
                                fg = pas * 6 + f6
                                if dc == 0:
                                    wt("t", bankfree[f6])
                                tok_mm = em("t", lambda t, f6=f6, fg=fg, gb=gb, dc=dc, di=di: t.matmul(
                                    bank(f6)[:, 0:512],
                                    a_wqct[gb][:, di, fg * 128:fg * 128 + 128],
                                    a_xnF[dc],
                                    start=(dc == 0), stop=(dc == 31)),
                                    sm_t if (f6 == 5 or dc == 31) else None)
                                if dc == 31:
                                    tok_qc_last[f6] = tok_mm
                        qc_buf_free[gb] = tok_mm
                    for f6 in range(6):
                        fg = pas * 6 + f6
                        qb_ = f6 % 2
                        wt("v", tok_qc_last[f6])
                        wt("v", tok_qcst_free[qb_])
                        tok_ev = em("v", lambda v, f6=f6, fg=fg, qb_=qb_: v.tensor_scalar(
                            a_qcst[qb_], bank(f6)[:, 0:512],
                            scalar1=sb_bqc[:, fg:fg + 1], scalar2=None, op0=ALU.add), sm_v)
                        bankfree[f6] = tok_ev
                        wt("y", tok_ev)
                        tok_st = dma("y", bass.AP(d_qcF, tb * (128 * 12 * 512) + fg * 512,
                                                  [[12 * 512, 128], [1, 512]]),
                                     a_qcst[qb_], "qc0" if qb_ == 0 else "qc1")
                        if debug >= 1:
                            tok_st = dma("y", bass.AP(dbg["qcF"], tb * (128 * 12 * 512) + fg * 512,
                                                      [[12 * 512, 128], [1, 512]]),
                                         a_qcst[qb_], "qc0" if qb_ == 0 else "qc1")
                        tok_qcst_free[qb_] = tok_st
            if tb < 2 and (_SUB & 4):
                tok_xnF_free = tok_qc_last[5]
            else:
                tok_xnF_free = tok_kv_last[4]

        if pend_kv[0] is not None:
            pend_kv[0]()
            pend_kv[0] = None

        run_b = _PHASE >= 2
        run_c = _PHASE >= 3
        sched.barrier()
        B = Carver(0, OS_BASE)
        b_kvn = [B.take(S, BF16) for _ in range(4)]
        b_rope = B.take(S, BF16)
        b_K = [B.take(S, BF16) for _ in range(2)]
        b_V = B.take(64 * 256, BF16)
        b_Q = [B.take(TOK, BF16) for _ in range(3)]
        b_P = [[B.take(512, BF16) for _ in range(3)] for _ in range(2)]
        b_qch = B.take(12 * 512, BF16).rearrange("p (c f) -> p c f", c=12)
        b_wqup = B.take(12 * 384, BF16).rearrange("p (c f) -> p c f", c=12)
        b_wkK = B.take(4 * 256, BF16).rearrange("p (c f) -> p c f", c=4)
        b_wkV = B.take(4 * 256, BF16).rearrange("p (c f) -> p c f", c=4)
        b_bcast = B.take(512, F32)
        b_otmp = B.take(512, F32)
        b_Ps = [[B.take(512, BF16) for _ in range(2)] for _ in range(2)]
        b_osum = arena[:, OS_BASE:OS_BASE + 1024]
        b_osbf = arena[:, OS_BASE + 1024:OS_BASE + 1536].bitcast(BF16)

        for lat in range(4):
            dma("y", b_kvn[lat],
                bass.AP(d_kvF, lat * 128 * S, [[S, 128], [1, S]]), "kvn")
        dma("y", b_rope[0:64, :], bass.AP(d_kvF, 512 * S, [[S, 64], [1, S]]), "kvn")
        tok_kvn_all = dma("y", b_rope[64:128, :],
                          bass.AP(d_kvF, 512 * S, [[S, 64], [1, S]]), "kvn")
        tok_rope_l = tok_rope_h = tok_kvn_all
        tok_osum0 = em("v", lambda v: v.memset(b_osum, 0.0), sm_v)

        wt("v", tok_rope_h)
        wt("v", tok_rope_l)
        tok_ropemax = em("v", lambda v: v.tensor_reduce(
            sb_stab[:, 5:6], b_rope, AX.X, ALU.max, apply_absolute_value=True), sm_v)

        tok_w_free = {"q": None, "k": None, "v": None}
        tok_qch_free = [None, None]
        tok_K_free = [None, None]
        tok_V_free = None
        tok_Q_free = None
        tok_osum_last = tok_osum0
        tok_exp_prev_pair = None
        tok_gq_prev = None
        tok_gk_prev = None
        tok_tc1_prev = None
        tok_bcast_prev = None

        pend = [None]
        texp = [(None, None)] * 3
        tps_tok = [[None, None], [None, None]]
        tps_free = [[None, None], [None, None]]
        p_read = [[None] * 3, [None] * 3]
        tok_bc_prev = [None, None]
        tok_osum_chain = [tok_osum0]
        flush_fn = [lambda: None]

        def flush_epi_outer():
            flush_fn[0]()
        npair_run = min(_NPAIR_RUN, NPAIR) if run_b else 0
        for p in range(npair_run):
            wt("g", tok_w_free["k"])
            tok_wk = dma("g", b_wkK, bass.AP(d_wkK, p * (4 * 256),
                                             [[NPAIR * 4 * 256, 128], [256, 4], [1, 256]]), "wkK")
            wt("g", tok_w_free["v"])
            tok_wv = dma("g", b_wkV, bass.AP(d_wkV, p * (4 * 256),
                                             [[NPAIR * 4 * 256, 128], [256, 4], [1, 256]]), "wkV")
            wt("g", tok_w_free["q"])
            tok_wq = dma("g", b_wqup, bass.AP(d_wqup, p * (12 * 384),
                                              [[NPAIR * 12 * 384, 128], [384, 12], [1, 384]]), "wqup")

            prod_rot = [0]

            def prod_bank():
                b = prod_rot[0] % 4
                prod_rot[0] += 1
                return b

            wt("t", tok_wq)
            wt("v", tok_Q_free)
            tok_Q_ready = None
            for half in range(2):
                wt("y", tok_qch_free[1] if half == 0 else tok_qch_free[0])
                tok_qc = dma("y", b_qch, bass.AP(d_qcF, half * (128 * 12 * 512),
                                                 [[12 * 512, 128], [512, 12], [1, 512]]),
                             "qch0" if half == 0 else "qch1")
                wt("t", tok_qc)
                for qch in range(3):
                    b = prod_bank()
                    wt("t", bankfree[b])
                    tok_mm = None
                    for ql in range(12):
                        tok_mm = em("t", lambda t, b=b, qch=qch, ql=ql: t.matmul(
                            bank(b),
                            b_wqup[:, ql, qch * 128:qch * 128 + 128],
                            b_qch[:, ql, :],
                            start=(ql == 0), stop=(ql == 11)),
                            sm_t if ql == 11 else None)
                    tok_qch_free[half] = tok_mm
                    wt("v", tok_mm)
                    tok_Q_ready = em("v", lambda v, b=b, qch=qch, half=half, p=p: v.tensor_scalar(
                        b_Q[qch][:, half * 512:half * 512 + 512], bank(b),
                        scalar1=sb_bqup[:, 3 * p + qch:3 * p + qch + 1], scalar2=ISQ,
                        op0=ALU.add, op1=ALU.mult), sm_v)
                    bankfree[b] = tok_Q_ready

            wt("v", tok_Q_ready)
            wt("v", tok_gq_prev)
            tql = None
            for qch in range(3):
                tql = em("v", lambda v, qch=qch: v.tensor_reduce(
                    sb_stab[:, qch:qch + 1], b_Q[qch], AX.X, ALU.max,
                    apply_absolute_value=True), sm_v)
            wt("v", tql)
            tqm = em("v", lambda v: v.tensor_reduce(
                sb_qmax[:, :], sb_stab[:, 0:3], AX.X, ALU.max), sm_v)
            wt("t", tok_wk)
            if p == 0:
                wt("t", tok_kvn_all)
            tok_K_ready = [None, None]
            for h in range(2):
                wt("v", tok_K_free[h])
                for tk in range(16):
                    b = prod_bank()
                    wt("t", bankfree[b])
                    tok_mm = None
                    for lat in range(4):
                        tok_mm = em("t", lambda t, b=b, h=h, tk=tk, lat=lat: t.matmul(
                            bank(b),
                            b_wkK[:, lat, h * 128:h * 128 + 128],
                            b_kvn[lat][:, tk * 512:tk * 512 + 512],
                            start=(lat == 0), stop=(lat == 3)),
                            sm_t if lat == 3 else None)
                    wt("v", tok_mm)
                    tok_ev = em("v", lambda v, b=b, h=h, tk=tk, p=p: v.tensor_scalar(
                        b_K[h][:, tk * 512:tk * 512 + 512], bank(b),
                        scalar1=sb_bkK[:, 2 * p + h:2 * p + h + 1], scalar2=None,
                        op0=ALU.add), sm_v)
                    bankfree[b] = tok_ev
                tok_K_ready[h] = tok_ev

            wt("v", tok_K_ready[0])
            wt("v", tok_K_ready[1])
            wt("v", tok_gk_prev)
            em("v", lambda v: v.tensor_reduce(
                sb_stab[:, 3:4], b_K[0], AX.X, ALU.max, apply_absolute_value=True), sm_v)
            tk1 = em("v", lambda v: v.tensor_reduce(
                sb_stab[:, 4:5], b_K[1], AX.X, ALU.max, apply_absolute_value=True), sm_v)
            wt("v", tk1)
            wt("v", tok_ropemax)
            tkm = em("v", lambda v: v.tensor_reduce(
                sb_kmax[:, :], sb_stab[:, 3:6], AX.X, ALU.max), sm_v)
            wt("t", tok_wv)
            wt("v", tok_V_free)
            tok_V_ready = None
            for tk in range(64):
                b = prod_bank()
                wt("t", bankfree[b])
                tok_mm = None
                for lat in range(4):
                    tok_mm = em("t", lambda t, b=b, tk=tk, lat=lat: t.matmul(
                        bank(b)[:, 0:256],
                        b_kvn[lat][:, tk * 128:tk * 128 + 128],
                        b_wkV[:, lat, :],
                        start=(lat == 0), stop=(lat == 3)),
                        sm_t if lat == 3 else None)
                wt("v", tok_mm)
                tok_V_ready = em("v", lambda v, b=b, tk=tk: v.tensor_copy(
                    b_V[:, tk * 256:tk * 256 + 256], bank(b)[:, 0:256]), sm_v)
                bankfree[b] = tok_V_ready

            wt("g", tqm)
            wt("g", tkm)
            wt("g", tok_tc1_prev)
            tok_gq_prev = em("g", lambda g: g.tensor_reduce(
                sb_qm[:, :], sb_qmax[:, :], AX.C, ALU.max), sm_gc)
            tok_gk_prev = em("g", lambda g: g.tensor_reduce(
                sb_km[:, :], sb_kmax[:, :], AX.C, ALU.max), sm_gc)
            wt("v", tok_gq_prev)
            wt("v", tok_gk_prev)
            tok_tc1_prev = em("v", lambda v: v.tensor_tensor(
                sb_cqk[:, :], sb_qm[:, :], sb_km[:, :], ALU.mult), sm_v)
            wt("v", tok_tc1_prev)
            tc2 = em("v", lambda v: v.tensor_scalar(
                sb_cqk[:, :], sb_cqk[:, :], scalar1=-float(QH), scalar2=-45.0,
                op0=ALU.mult, op1=ALU.max), sm_v)
            flush_epi_outer()
            wt("t", tc2)
            wt("t", bankfree[4])
            tbc = em("t", lambda t: t.matmul(
                bank(4)[:, 0:1], sb_ones[:, :], sb_cqk[:, :], start=True, stop=True), sm_t)
            wt("v", tbc)
            wt("v", tok_exp_prev_pair)
            tnc = em("v", lambda v: v.tensor_copy(sb_negc_col[:, :], bank(4)[:, 0:1]), sm_v)
            bankfree[4] = tnc
            wt("t", tok_V_ready)
            wt("s", tnc)
            tok_score_last = None

            def emit_scores(qb, kc):
                par = kc % 2
                b0, b1 = par, 2 + par
                wt("t", bankfree[b0])
                wt("t", bankfree[b1])
                em("t", lambda t, b0=b0, kc=kc, qb=qb: t.matmul(
                    bank(b0), b_K[0][:, kc * 128:kc * 128 + 128],
                    b_Q[0][:, qb * 512:qb * 512 + 512], start=True, stop=False))
                em("t", lambda t, b1=b1, kc=kc, qb=qb: t.matmul(
                    bank(b1), b_K[1][:, kc * 128:kc * 128 + 128],
                    b_Q[1][:, qb * 512:qb * 512 + 512], start=True, stop=False))
                em("t", lambda t, b0=b0, kc=kc, qb=qb: t.matmul(
                    bank(b0), b_rope[0:64, kc * 128:kc * 128 + 128],
                    b_Q[2][0:64, qb * 512:qb * 512 + 512],
                    start=False, stop=True, tile_position=(0, 0)))
                tok_sc = em("t", lambda t, b1=b1, kc=kc, qb=qb: t.matmul(
                    bank(b1), b_rope[64:128, kc * 128:kc * 128 + 128],
                    b_Q[2][64:128, qb * 512:qb * 512 + 512],
                    start=False, stop=True, tile_position=(64, 0)), sm_t)
                wt("s", tok_sc)
                wt("s", p_read[0][kc % 3])
                wt("s", p_read[1][kc % 3])
                te0 = em("s", lambda s, b0=b0, kc=kc: s.activation(
                    b_P[0][kc % 3], bank(b0), AF.Exp, bias=sb_negc_col[:, :]), sm_s)
                te1 = em("s", lambda s, b1=b1, kc=kc: s.activation(
                    b_P[1][kc % 3], bank(b1), AF.Exp, bias=sb_negc_col[:, :]), sm_s)
                bankfree[b0] = te0
                bankfree[b1] = te1
                texp[kc % 3] = (te0, te1)
                if kc % 2 == 1:
                    for h, te in ((0, te0), (1, te1)):
                        wt("v", te)
                        wt("v", tps_free[h][(kc // 2) % 2])
                        tok_add = em("v", lambda v, h=h, kc=kc: v.tensor_tensor(
                            b_Ps[h][(kc // 2) % 2], b_P[h][(kc - 1) % 3],
                            b_P[h][kc % 3], ALU.add), sm_v)
                        tps_tok[h][(kc // 2) % 2] = tok_add
                        p_read[h][(kc - 1) % 3] = tok_add
                        p_read[h][kc % 3] = tok_add
                return tok_sc, te1

            def emit_pv(qb, kc, epi_banks_wait):
                if epi_banks_wait:
                    for ob in (4, 5):
                        wt("t", bankfree[ob])
                wt("t", texp[kc % 3][0])
                wt("t", texp[kc % 3][1])
                em("t", lambda t, kc=kc: t.matmul(
                    bank(4), b_V[:, kc * 256:kc * 256 + 128], b_P[0][kc % 3],
                    start=(kc == 0), stop=(kc == 63)))
                tok = em("t", lambda t, kc=kc: t.matmul(
                    bank(5), b_V[:, kc * 256 + 128:kc * 256 + 256], b_P[1][kc % 3],
                    start=(kc == 0), stop=(kc == 63)),
                    sm_t if kc == 63 else None)
                if kc % 2 == 1:
                    j = (kc // 2) % 2
                    if kc == 1:
                        for ob in (6, 7):
                            wt("t", bankfree[ob])
                    for h in range(2):
                        wt("t", tps_tok[h][j])
                        tok = em("t", lambda t, kc=kc, h=h, j=j: t.matmul(
                            bank(6 + h)[0:1, :], sb_onec[:, :], b_Ps[h][j],
                            start=(kc == 1), stop=(kc == 63)),
                            sm_t if kc == 63 else None)
                        tps_free[h][j] = tok if kc == 63 else None
                return tok

            def emit_epi_v1(qb, tok_pv):
                trs = []
                for h in range(2):
                    wt("v", tok_pv)
                    wt("v", tok_bc_prev[h])
                    trs.append(em("v", lambda v, h=h: v.reciprocal(
                        (sb_recip0 if h == 0 else sb_recip1)[0:1, :],
                        bank(6 + h)[0:1, 0:512]), sm_v))
                    bankfree[6 + h] = trs[h]
                return trs

            def flush_epi():
                if not pend[0]:
                    return
                qb, trs = pend[0]
                pend[0] = None
                for h in range(2):
                    bcb = 1 + 2 * h
                    wt("t", trs[h])
                    wt("t", bankfree[bcb])
                    tok_bc = em("t", lambda t, bcb=bcb, h=h: t.matmul(
                        bank(bcb), sb_ones[:, :],
                        (sb_recip0 if h == 0 else sb_recip1)[0:1, :],
                        start=True, stop=True), sm_t)
                    tok_bc_prev[h] = tok_bc
                    wt("v", tok_bc)
                    tcp = em("v", lambda v, bcb=bcb: v.tensor_copy(
                        b_bcast, bank(bcb)), sm_v)
                    bankfree[bcb] = tcp
                    wt("v", tcp)
                    tmu = em("v", lambda v, h=h: v.tensor_tensor(
                        b_otmp, bank(4 + h), b_bcast, ALU.mult), sm_v)
                    bankfree[4 + h] = tmu
                    wt("v", tmu)
                    wt("v", tok_osum_chain[0])
                    tok_osum_chain[0] = em("v", lambda v, qb=qb: v.tensor_tensor(
                        b_osum[:, qb * 512:qb * 512 + 512],
                        b_osum[:, qb * 512:qb * 512 + 512], b_otmp, ALU.add), sm_v)

            flush_fn[0] = flush_epi
            for qb in range(2):
                tok_pv = None
                for kc in range(64):
                    tok_sc, te1 = emit_scores(qb, kc)
                    tok_score_last = tok_sc
                    tok_exp_prev_pair = te1
                    if kc == 2:
                        flush_epi()
                    if kc >= 2:
                        tok_pv = emit_pv(qb, kc - 2, epi_banks_wait=(kc == 2))
                tok_pv = emit_pv(qb, 62, False)
                tok_pv = emit_pv(qb, 63, False)
                tok_V_free = tok_pv
                trs = emit_epi_v1(qb, tok_pv)
                pend[0] = (qb, trs)
            tok_K_free = [tok_score_last, tok_score_last]
            tok_Q_free = tok_score_last
            tok_w_free = {"q": tok_Q_ready, "k": tok_K_ready[1], "v": tok_V_ready}

        flush_fn[0]()
        wt("v", tok_bvsum)
        wt("v", tok_osum_chain[0])
        tb1 = em("v", lambda v: v.tensor_scalar(
            b_osum, b_osum, scalar1=sb_bvsum[:, :], scalar2=None, op0=ALU.add), sm_v)
        wt("v", tb1)
        tok_osbf = em("v", lambda v: v.tensor_copy(b_osbf, b_osum), sm_v)

        if debug >= 2:
            wt("y", tok_osbf)
            dma("y", dbg["K0"].ap(), b_K[0], "cwy")
            dma("y", dbg["K1"].ap(), b_K[1], "cwy")
            dma("y", dbg["V0"].ap(), b_V, "cwy")
            for qch in range(3):
                dma("y", bass.AP(dbg["Q0"], qch * TOK, [[3 * TOK, 128], [1, TOK]]), b_Q[qch], "cwy")
            dma("y", dbg["osum"].ap(), b_osum, "cwy")

        def _phase_c():
            sched.barrier()
            C = Carver(0, OS_BASE)
            c_xn2F = [C.take(TOK, BF16) for _ in range(32)]
            c_rows = C.take(D, F32)
            c_la = C.take(32 * 128, BF16).rearrange("p (c f) -> p c f", c=32)
            c_lb = C.take(D, BF16)
            c_hT = C.take(TOK, BF16)
            c_h = C.take(128, F32)
            c_outs = [C.take(512, F32) for _ in range(2)]
            c_woT = C.take(D, BF16)
            c_biasbc = C.take(D, F32)
            C_MAIN_BASE = C.off
            c_x = C.take(D, F32)
            c_x2 = C.take(D, F32)
            c_xn2 = C.take(D, F32)

            dma("g", c_la, d_laT.ap().rearrange("p (c f) -> p c f", c=32), "cw")
            dma("g", c_lb, d_lbT.ap(), "cw")
            tok_woT = dma("g", c_woT, d_woT.ap(), "cw")
            tok_la = tok_lb = tok_woT
            tok_borow = dma("y", c_rows[0:1, :], d_bo.ap(), "cwy")

            tok_bo_bc = None
            wt("t", tok_borow)
            for ob in range(8):
                bb = 6 + (ob % 2)
                wt("t", bankfree[bb])
                tm = em("t", lambda t, bb=bb, ob=ob: t.matmul(
                    bank(bb), sb_ones[:, :], c_rows[0:1, ob * 512:(ob + 1) * 512],
                    start=True, stop=True), sm_t)
                tok_bcast_mm_last = tm
                wt("v", tm)
                tok_bo_bc = em("v", lambda v, bb=bb, ob=ob: v.tensor_copy(
                    c_biasbc[:, ob * 512:(ob + 1) * 512], bank(bb)), sm_v)
                bankfree[bb] = tok_bo_bc

            tok_x_used = None
            tok_x2_dma_last = None
            tok_xn2_cons = None
            tok_x2w = []
            tok_evS_last = None
            tok_evV_last = None
            tok_op_last = None
            for tt in range(8):
                wt("y", tok_x_used)
                tok_xl = dma("y", c_x, d_x[tt * 128:(tt + 1) * 128, :], "x0")
                wt("t", tok_osbf)
                wt("t", tok_woT)
                for ob in range(8):
                    wt("t", bankfree[ob])
                    tm = em("t", lambda t, tt=tt, ob=ob: t.matmul(
                        bank(ob), b_osbf[:, tt * 128:tt * 128 + 128],
                        c_woT[:, ob * 512:(ob + 1) * 512], start=True, stop=True),
                        sm_t if ob == 7 else None)
                tok_op_last = tm
                wt("v", tm)
                wt("v", tok_bo_bc)
                wt("v", tok_x2_dma_last)
                t1 = em("v", lambda v: v.tensor_tensor(
                    c_x2, ps[:, 0:4096], c_biasbc, ALU.add), sm_v)
                for ob in range(8):
                    bankfree[ob] = t1
                wt("v", t1)
                wt("v", tok_xl)
                tok_ev_all = em("v", lambda v: v.tensor_tensor(
                    c_x2, c_x2, c_x, ALU.add), sm_v)
                tok_x_used = tok_ev_all
                wt("y", tok_ev_all)
                tw = dma("y", bass.AP(d_x2, tt * 128 * 512,
                                      [[512, 128], [TOK * 512, 8], [1, 512]]),
                         c_x2.rearrange("p (a b) -> p a b", a=8), "x2w")
                if debug >= 3:
                    tw = dma("y", bass.AP(dbg["x2"], tt * 128 * 512,
                                          [[512, 128], [TOK * 512, 8], [1, 512]]),
                             c_x2.rearrange("p (a b) -> p a b", a=8), "x2w")
                tok_x2w.append(tw)
                tok_x2_dma_last = tok_x2w[-1]
                t9 = emit_ln(c_x2, c_xn2, c_xn2, tok_ev_all, tok_ev_all,
                             extra_s_wait=tok_xn2_cons)
                wt("t", t9)
                last_tp = None
                for dc in range(32):
                    b = 6 + (dc % 2)
                    wt("t", bankfree[b])
                    tok_tp = em("t", lambda t, b=b, dc=dc: t.transpose(
                        bank(b)[:, 0:128], c_xn2[:, dc * 128:dc * 128 + 128], sb_id[:, :]), sm_t)
                    last_tp = tok_tp
                    eng = "s" if dc % 2 == 0 else "v"
                    wt(eng, tok_tp)
                    if eng == "s":
                        tok_ev = em(eng, lambda e, b=b, dc=dc, tt=tt: e.copy(
                            c_xn2F[dc][:, tt * 128:tt * 128 + 128], bank(b)[:, 0:128]), sm_s)
                    else:
                        tok_ev = em(eng, lambda e, b=b, dc=dc, tt=tt: e.tensor_copy(
                            c_xn2F[dc][:, tt * 128:tt * 128 + 128], bank(b)[:, 0:128]), sm_v)
                    bankfree[b] = tok_ev
                    if eng == "s":
                        tok_evS_last = tok_ev
                    else:
                        tok_evV_last = tok_ev
                tok_xn2_cons = last_tp

            wt("y", tok_bcast_mm_last)
            tok_fbrow = dma("y", c_rows[0:1, :], d_fb.ap(), "cwy")
            tok_fb_bc = None
            wt("t", tok_fbrow)
            wt("t", tok_x_used)
            for ob in range(8):
                bb = 6 + (ob % 2)
                wt("t", bankfree[bb])
                tm = em("t", lambda t, bb=bb, ob=ob: t.matmul(
                    bank(bb), sb_ones[:, :], c_rows[0:1, ob * 512:(ob + 1) * 512],
                    start=True, stop=True), sm_t)
                wt("v", tm)
                tok_fb_bc = em("v", lambda v, bb=bb, ob=ob: v.tensor_copy(
                    c_biasbc[:, ob * 512:(ob + 1) * 512], bank(bb)), sm_v)
                bankfree[bb] = tok_fb_bc

            wt("t", tok_la)
            wt("t", tok_evS_last)
            wt("t", tok_evV_last)
            tok_hT_done = None
            tok_h_cons = None
            for tt in range(8):
                b = tt % 2
                wt("t", bankfree[b])
                tok_mm = None
                for dc in range(32):
                    tok_mm = em("t", lambda t, b=b, tt=tt, dc=dc: t.matmul(
                        bank(b)[:, 0:128], c_xn2F[dc][:, tt * 128:tt * 128 + 128],
                        c_la[:, dc, :], start=(dc == 0), stop=(dc == 31)),
                        sm_t if dc == 31 else None)
                wt("v", tok_mm)
                wt("v", tok_h_cons)
                tcp = em("v", lambda v, b=b: v.tensor_copy(c_h, bank(b)[:, 0:128]), sm_v)
                bankfree[b] = tcp
                tb_ = 2 + (tt % 2)
                wt("t", tcp)
                wt("t", bankfree[tb_])
                ttp = em("t", lambda t, tb_=tb_: t.transpose(
                    bank(tb_)[:, 0:128], c_h, sb_id[:, :]), sm_t)
                tok_h_cons = ttp
                wt("v", ttp)
                tok_hT_done = em("v", lambda v, tb_=tb_, tt=tt: v.tensor_scalar(
                    c_hT[:, tt * 128:tt * 128 + 128], bank(tb_)[:, 0:128],
                    scalar1=sb_coef[:, :], scalar2=None, op0=ALU.mult), sm_v)
                bankfree[tb_] = tok_hT_done

            CM = Carver(C_MAIN_BASE, OS_BASE)
            c_wf = [CM.take(16 * 512, BF16).rearrange("p (c f) -> p c f", c=16) for _ in range(2)]
            c_x2s = [CM.take(8 * 512, F32).rearrange("p (c f) -> p c f", c=8) for _ in range(2)]

            tok_wfh_free = [None, None]
            tok_x2s_free = [None, None]
            tok_outs_free = [None, None]
            x2w_all = tok_x2w[-1]
            for ob in range(8):
                wb = ob % 2
                if ob == 0:
                    wt("g", tok_x_used)
                    wt("g", tok_xn2_cons)
                    wt("g", x2w_all)
                    wt("y", tok_x_used)
                    wt("y", tok_xn2_cons)
                tok_wfh = [None, None]
                for hf in range(2):
                    wt("g", tok_wfh_free[hf])
                    wt("g", tok_wfh_free[1])
                    tok_wfh[hf] = dma("g", c_wf[hf],
                                      bass.AP(d_wfT, ob * (32 * 512) + hf * (16 * 512),
                                              [[8 * 32 * 512, 128], [512, 16], [1, 512]]),
                                      "wf0" if hf == 0 else "wf1")
                wt("y", x2w_all)
                wt("y", tok_x2s_free[wb])
                tok_x2s = dma("y", c_x2s[wb], bass.AP(d_x2, ob * TOK * 512,
                                                      [[512, 128], [128 * 512, 8], [1, 512]]),
                              "x2r0" if wb == 0 else "x2r1")
                if ob == 0:
                    wt("t", tok_lb)
                    wt("t", tok_hT_done)
                tok_lora_mm = [None] * 8
                for hf in range(2):
                    wt("t", tok_wfh[hf])
                    for tt in range(8):
                        if hf == 0:
                            wt("t", bankfree[tt])
                        for dci in range(16):
                            dc = hf * 16 + dci
                            em("t", lambda t, tt=tt, hf=hf, dc=dc, dci=dci: t.matmul(
                                bank(tt), c_xn2F[dc][:, tt * 128:tt * 128 + 128],
                                c_wf[hf][:, dci, :], start=(dc == 0), stop=False))
                        if hf == 1:
                            tok_lora_mm[tt] = em("t", lambda t, tt=tt, ob=ob: t.matmul(
                                bank(tt), c_hT[:, tt * 128:tt * 128 + 128],
                                c_lb[:, ob * 512:(ob + 1) * 512],
                                start=False, stop=True), sm_t)
                        else:
                            if tt == 7:
                                tok_wfh_free[0] = em("t", lambda t: t.nop(), sm_t)
                tok_wfh_free[1] = tok_lora_mm[7]
                for tt in range(8):
                    oi = tt % 2
                    wt("v", tok_lora_mm[tt])
                    if ob == 0 and tt == 0:
                        wt("v", tok_fb_bc)
                    wt("v", tok_outs_free[oi])
                    t1 = em("v", lambda v, tt=tt, ob=ob, oi=oi: v.tensor_tensor(
                        c_outs[oi], bank(tt), c_biasbc[:, ob * 512:(ob + 1) * 512], ALU.add), sm_v)
                    bankfree[tt] = t1
                    wt("v", t1)
                    wt("v", tok_x2s)
                    t2 = em("v", lambda v, oi=oi, wb=wb, tt=tt: v.tensor_tensor(
                        c_outs[oi], c_outs[oi], c_x2s[wb][:, tt, :], ALU.add), sm_v)
                    wt("y", t2)
                    tw = dma("y", bass.AP(d_out, tt * 128 * D + ob * 512,
                                          [[D, 128], [1, 512]]), c_outs[oi],
                             "o0" if oi == 0 else "o1")
                    tok_outs_free[oi] = tw
                    if tt == 7:
                        tok_x2s_free[wb] = t2


        if run_c:
            _phase_c()

        for _sm in dma_sems.values():
            if _sm.n > 0:
                sched.wait("y", (_sm, _sm.n))

        with nc.Block() as block:
            block.tensor(lambda eng: [f(eng) for f in sched.ops["t"]])
            block.vector(lambda eng: [f(eng) for f in sched.ops["v"]])
            block.scalar(lambda eng: [f(eng) for f in sched.ops["s"]])
            block.gpsimd(lambda eng: [f(eng) for f in sched.ops["g"]])
            block.sync(lambda eng: [f(eng) for f in sched.ops["y"]])

    return nc


def _stage(inputs):
    g = lambda k: np.asarray(inputs[k], np.float32)
    X = g("L0_attn_hidden_size").reshape(S, D)
    Wq_c = g("L0_attn_Wq_c")
    bq_c = g("L0_attn_bq_c")
    Wq_up = g("L0_attn_Wq_up")
    bq_up = g("L0_attn_bq_up")
    Wkv_c = g("L0_attn_Wkv_c")
    bkv_c = g("L0_attn_bkv_c")
    Wkv_up = g("L0_attn_Wkv_up")
    bkv_up = g("L0_attn_bkv_up")
    Wo = g("L0_attn_Wo")
    bo = g("L0_attn_bo")
    lora_A = g("L0_ffn_lora_A")
    lora_B = g("L0_ffn_lora_B")
    Wf = g("L0_ffn_W")
    fb = g("L0_ffn_bias")
    lw = g("L0_ffn_lora_weights")

    C = np.ascontiguousarray

    PERM, KPERM, VPERM = [], [], []
    for p in range(NPAIR):
        h0, h1 = 2 * p, 2 * p + 1
        PERM += list(range(h0 * QH, h0 * QH + NOPE))
        PERM += list(range(h1 * QH, h1 * QH + NOPE))
        PERM += list(range(h0 * QH + NOPE, h0 * QH + QH))
        PERM += list(range(h1 * QH + NOPE, h1 * QH + QH))
        KPERM += list(range(h0 * 256, h0 * 256 + 128))
        KPERM += list(range(h1 * 256, h1 * 256 + 128))
        VPERM += list(range(h0 * 256 + 128, h0 * 256 + 256))
        VPERM += list(range(h1 * 256 + 128, h1 * 256 + 256))

    base = {}
    base["wqcT"] = C(Wq_c.T)
    base["bqc_t"] = C(bq_c.reshape(12, 128).T)
    wqupT = Wq_up[PERM, :].T
    base["wqup_til"] = C(wqupT.reshape(12, 128, NPAIR, 384).transpose(1, 2, 0, 3)
                         .reshape(128, NPAIR, 12 * 384))
    base["bqup_t"] = C(bq_up[PERM].reshape(48, 128).T)
    wkvcT = Wkv_c.T
    base["wkvc_til"] = C(wkvcT.reshape(32, 128, 576).transpose(1, 0, 2).reshape(128, 32 * 576))
    bkvc_p = np.zeros(640, np.float32)
    bkvc_p[:576] = bkv_c
    base["bkvc_t"] = C(bkvc_p.reshape(5, 128).T)
    wkKT = Wkv_up[KPERM, :].T
    base["wkK_til"] = C(wkKT.reshape(4, 128, NPAIR, 256).transpose(1, 2, 0, 3)
                        .reshape(128, NPAIR, 4 * 256))
    base["bkupK_t"] = C(bkv_up[KPERM].reshape(32, 128).T)
    wkVT = Wkv_up[VPERM, :].T
    base["wkV_til"] = C(wkVT.reshape(4, 128, NPAIR, 256).transpose(1, 2, 0, 3)
                        .reshape(128, NPAIR, 4 * 256))
    base["bkvV_cols"] = C(bkv_up[VPERM].reshape(32, 128).T)
    base["woT"] = C(Wo.T)
    base["bo_row"] = C(bo.reshape(1, D))
    WfT = Wf.T
    base["wffn_til"] = C(WfT.reshape(32, 128, 8, 512).transpose(1, 2, 0, 3)
                         .reshape(128, 8, 32 * 512))
    base["ffnb_row"] = C(fb.reshape(1, D))
    laT = lora_A.reshape(128, D).T
    base["loraA_til"] = C(laT.reshape(32, 128, 128).transpose(1, 0, 2).reshape(128, 32 * 128))
    base["loraBT"] = C(lora_B.transpose(0, 2, 1).reshape(128, D))
    base["lwrep"] = C(np.repeat(lw, 16)[:, None])
    base["ident"] = np.eye(128, dtype=np.float32)
    base["ones_row"] = np.ones((1, 128), np.float32)
    base["onecol_bf"] = np.ones((128, 1), ml_dtypes.bfloat16)

    in_maps = []
    for c in range(NCORE):
        m = dict(base)
        m["x_rot"] = C(np.concatenate([X[TOK * c:], X[:TOK * c]], axis=0))
        in_maps.append(m)
    return in_maps


def kernel(**inputs):
    in_maps = _stage(inputs)
    nc = build_nc(debug=_DEBUG)
    res = run_bass_kernel_spmd(nc, in_maps, core_ids=list(range(NCORE)))
    if _DEBUG:
        kernel._last = res
    out = np.concatenate([np.asarray(res.results[c]["out_c"]) for c in range(NCORE)], axis=0)
    return out.reshape(1, S, D).astype(np.float32)


# version 07
# speedup vs baseline: 1.0015x; 1.0015x over previous
import os
import math
import numpy as np
import ml_dtypes
import concourse.bass as bass
import concourse.mybir as mybir
from concourse.bass_utils import run_bass_kernel_spmd
from contextlib import ExitStack

F32 = mybir.dt.float32
BF16 = mybir.dt.bfloat16
AF = mybir.ActivationFunctionType
ALU = mybir.AluOpType
AX = mybir.AxisListType

NCORE = 8
S = 8192
D = 4096
TOK = 1024
NH = 32
NPAIR = 16
QL = 1536
KVL = 512
QH = 192
NOPE = 128
VH = 128
ISQ = 1.0 / math.sqrt(QH)
EPS = 1e-5

_DEBUG = int(os.environ.get("MK_DEBUG", "0"))
_NPAIR_RUN = int(os.environ.get("MK_NPAIR", str(NPAIR)))
_PHASE = int(os.environ.get("MK_PHASE", "3"))
_TBS = int(os.environ.get("MK_TBS", "16"))
_SUB = int(os.environ.get("MK_SUB", "15"))
_TPV = int(os.environ.get("MK_TPV", "0"))
_TTS = int(os.environ.get("MK_TTS", "4"))


class Sem:
    def __init__(self, handle, name):
        self.h = handle
        self.name = name
        self.n = 0


class Sched:
    ENGINES = ("t", "v", "s", "g", "y")

    def __init__(self):
        self.ops = {e: [] for e in self.ENGINES}
        self.sems = {}

    def sem(self, name, handle):
        sm = Sem(handle, name)
        self.sems[name] = sm
        return sm

    def emit(self, e, fn, sem=None, k=1):
        if sem is None:
            self.ops[e].append(fn)
            return None
        sem.n += k
        tag = sem.n

        def run(eng, fn=fn, s=sem.h, k=k):
            fn(eng).then_inc(s, k)

        self.ops[e].append(run)
        return (sem, tag)

    def wait(self, e, tok):
        if tok is None:
            return
        sem, tag = tok
        if tag <= 0:
            return
        self.ops[e].append(lambda eng, s=sem.h, t=tag: eng.wait_ge(s, t))

    def barrier(self):
        toks = [(sm, sm.n) for sm in self.sems.values() if sm.n > 0]
        for e in self.ENGINES:
            for tok in toks:
                self.wait(e, tok)


def build_nc(debug=0):
    nc = bass.Bass("TRN2", target_bir_lowering=False)

    d_x = nc.dram_tensor("x_rot", [S, D], F32, kind="ExternalInput")
    d_wqcT = nc.dram_tensor("wqcT", [D, QL], F32, kind="ExternalInput")
    d_bqc = nc.dram_tensor("bqc_t", [128, 12], F32, kind="ExternalInput")
    d_wqup = nc.dram_tensor("wqup_til", [128, NPAIR, 12 * 384], F32, kind="ExternalInput")
    d_bqup = nc.dram_tensor("bqup_t", [128, 48], F32, kind="ExternalInput")
    d_wkvc = nc.dram_tensor("wkvc_til", [128, 32 * 576], F32, kind="ExternalInput")
    d_bkvc = nc.dram_tensor("bkvc_t", [128, 5], F32, kind="ExternalInput")
    d_wkK = nc.dram_tensor("wkK_til", [128, NPAIR, 4 * 256], F32, kind="ExternalInput")
    d_bkK = nc.dram_tensor("bkupK_t", [128, 32], F32, kind="ExternalInput")
    d_wkV = nc.dram_tensor("wkV_til", [128, NPAIR, 4 * 256], F32, kind="ExternalInput")
    d_bkV = nc.dram_tensor("bkvV_cols", [128, 32], F32, kind="ExternalInput")
    d_woT = nc.dram_tensor("woT", [VH, D], F32, kind="ExternalInput")
    d_bo = nc.dram_tensor("bo_row", [1, D], F32, kind="ExternalInput")
    d_wfT = nc.dram_tensor("wffn_til", [128, 8, 32 * 512], F32, kind="ExternalInput")
    d_fb = nc.dram_tensor("ffnb_row", [1, D], F32, kind="ExternalInput")
    d_laT = nc.dram_tensor("loraA_til", [128, 32 * 128], F32, kind="ExternalInput")
    d_lbT = nc.dram_tensor("loraBT", [128, D], F32, kind="ExternalInput")
    d_lw = nc.dram_tensor("lwrep", [128, 1], F32, kind="ExternalInput")
    d_id = nc.dram_tensor("ident", [128, 128], F32, kind="ExternalInput")
    d_ones = nc.dram_tensor("ones_row", [1, 128], F32, kind="ExternalInput")
    d_onec = nc.dram_tensor("onecol_bf", [128, 1], BF16, kind="ExternalInput")

    d_kvF = nc.dram_tensor("kvF_dram", [576, S], BF16)
    d_qcF = nc.dram_tensor("qcF_dram", [2, 128, 12 * 512], BF16)
    d_x2 = nc.dram_tensor("x2_dram", [8, TOK, 512], F32)
    d_out = nc.dram_tensor("out_c", [TOK, D], F32, kind="ExternalOutput")

    dbg = {}
    if debug >= 1:
        dbg["kvF"] = nc.dram_tensor("dbg_kvF", [576, S], BF16, kind="ExternalOutput")
        dbg["qcF"] = nc.dram_tensor("dbg_qcF", [2, 128, 12 * 512], BF16, kind="ExternalOutput")
    if debug >= 2:
        dbg["K0"] = nc.dram_tensor("dbg_K0", [NOPE, S], BF16, kind="ExternalOutput")
        dbg["K1"] = nc.dram_tensor("dbg_K1", [NOPE, S], BF16, kind="ExternalOutput")
        dbg["V0"] = nc.dram_tensor("dbg_V0", [128, 64 * 256], BF16, kind="ExternalOutput")
        dbg["Q0"] = nc.dram_tensor("dbg_Q0", [128, 3 * TOK], BF16, kind="ExternalOutput")
        dbg["osum"] = nc.dram_tensor("dbg_osum", [VH, TOK], F32, kind="ExternalOutput")
    if debug >= 3:
        dbg["x2"] = nc.dram_tensor("dbg_x2", [8, TOK, 512], F32, kind="ExternalOutput")

    sched = Sched()
    em, wt = sched.emit, sched.wait

    with ExitStack() as ctx:
        E = ctx.enter_context

        sb_id = E(nc.sbuf_tensor("sb_id", [128, 128], F32))
        sb_ones = E(nc.sbuf_tensor("sb_ones", [1, 128], F32))
        sb_onec = E(nc.sbuf_tensor("sb_onec", [128, 1], BF16))
        sb_bqc = E(nc.sbuf_tensor("sb_bqc", [128, 12], F32))
        sb_bqup = E(nc.sbuf_tensor("sb_bqup", [128, 48], F32))
        sb_bkvc = E(nc.sbuf_tensor("sb_bkvc", [128, 5], F32))
        sb_bkK = E(nc.sbuf_tensor("sb_bkK", [128, 32], F32))
        sb_bkV = E(nc.sbuf_tensor("sb_bkV", [128, 32], F32))
        sb_lw = E(nc.sbuf_tensor("sb_lw", [128, 1], F32))
        sb_coef = E(nc.sbuf_tensor("sb_coef", [128, 1], F32))
        sb_bvsum = E(nc.sbuf_tensor("sb_bvsum", [128, 1], F32))
        st_sum = E(nc.sbuf_tensor("st_sum", [128, 2], F32))
        st_sqs = E(nc.sbuf_tensor("st_sqs", [128, 2], F32))
        st_nm = E(nc.sbuf_tensor("st_nm", [128, 2], F32))
        st_m2 = E(nc.sbuf_tensor("st_m2", [128, 2], F32))
        st_var = E(nc.sbuf_tensor("st_var", [128, 2], F32))
        st_sq = E(nc.sbuf_tensor("st_sq", [128, 2], F32))
        st_rs = E(nc.sbuf_tensor("st_rs", [128, 2], F32))
        sb_stab = E(nc.sbuf_tensor("sb_stab", [128, 8], F32))
        sb_qmax = E(nc.sbuf_tensor("sb_qmax", [128, 1], F32))
        sb_kmax = E(nc.sbuf_tensor("sb_kmax", [128, 1], F32))
        sb_qm = E(nc.sbuf_tensor("sb_qm", [1, 1], F32))
        sb_km = E(nc.sbuf_tensor("sb_km", [1, 1], F32))
        sb_cqk = E(nc.sbuf_tensor("sb_cqk", [1, 1], F32))
        sb_negc_col = E(nc.sbuf_tensor("sb_negc_col", [128, 1], F32))
        sb_recip0 = E(nc.sbuf_tensor("sb_recip0", [1, 512], F32))
        sb_recip1 = E(nc.sbuf_tensor("sb_recip1", [1, 512], F32))

        AR_N = 51456
        OS_BASE = AR_N - 1536
        arena = E(nc.sbuf_tensor("arena", [128, AR_N], F32))

        class Carver:
            def __init__(self, base=0, limit=AR_N):
                self.off = base
                self.limit = limit

            def take(self, nfree, dtype):
                ef32 = nfree if dtype == F32 else (nfree + 1) // 2
                ef32 = (ef32 + 15) & ~15
                assert self.off + ef32 <= self.limit, \
                    f"arena overflow {self.off}+{ef32} > {self.limit}"
                ap = arena[:, self.off:self.off + ef32]
                self.off += ef32
                if dtype == BF16:
                    ap = ap.bitcast(BF16)[:, 0:nfree]
                return ap

        ps = E(nc.psum_tensor("ps", [128, 4096], F32))

        def bank(b):
            return ps[:, b * 512:(b + 1) * 512]

        bankfree = [None] * 8

        sm_gc = sched.sem("gc", E(nc.semaphore("sm_gc")))
        sm_t = sched.sem("t", E(nc.semaphore("sm_t")))
        sm_v = sched.sem("v", E(nc.semaphore("sm_v")))
        sm_s = sched.sem("s", E(nc.semaphore("sm_s")))
        dma_sems = {}
        for nm in ("c", "x0", "x1", "x2b", "wkvc", "wq0", "wq1", "kv0", "kv1",
                   "qc0", "qc1", "kvn", "qch0", "qch1", "wkK", "wkV", "wqup",
                   "cw", "cwy", "wf0", "wf1", "x2w", "x2r0", "x2r1", "o0", "o1", "st"):
            dma_sems[nm] = sched.sem(nm, E(nc.semaphore("sm_" + nm)))
        sm_st = dma_sems["st"]

        def dma(e, out, in_, sem):
            if isinstance(sem, str):
                sem = dma_sems[sem]
            return em(e, lambda eng, o=out, i=in_: eng.dma_start(out=o, in_=i), sem, 16)

        tok_consts = None
        for (dst, src) in [
            (sb_id[:, :], d_id.ap()), (sb_ones[:, :], d_ones.ap()),
            (sb_onec[:, :], d_onec.ap()), (sb_bqc[:, :], d_bqc.ap()),
            (sb_bqup[:, :], d_bqup.ap()), (sb_bkvc[:, :], d_bkvc.ap()),
            (sb_bkK[:, :], d_bkK.ap()), (sb_bkV[:, :], d_bkV.ap()),
            (sb_lw[:, :], d_lw.ap()),
        ]:
            tok_consts = dma("y", dst, src, "c")
        wt("v", tok_consts)
        em("v", lambda v: v.tensor_scalar(
            sb_coef[:, :], sb_lw[:, :], scalar1=2.0, scalar2=None, op0=ALU.mult), sm_v)
        tok_bvsum = em("v", lambda v: v.reduce_sum(sb_bvsum[:, :], sb_bkV[:, :], AX.X), sm_v)

        ln_rot = [0]
        last_sq_tok = [None]
        slot_sum_free = [None, None]
        slot_sqs_free = [None, None]
        slot_sq_free = [None, None]

        def emit_ln_stats(xbuf, xsqbuf, tok_loaded_s, extra_s_wait=None):
            r = ln_rot[0] % 2
            ln_rot[0] += 1
            sl = slice(r, r + 1)
            wt("s", tok_loaded_s)
            if extra_s_wait is not None:
                wt("s", extra_s_wait)
            wt("s", slot_sum_free[r])
            wt("s", last_sq_tok[0])
            t1 = em("s", lambda s, sl=sl, xb=xbuf, xo=xsqbuf: s.activation(
                xo, xb, AF.Copy, accum_out=st_sum[:, sl]), sm_s)
            wt("s", slot_sqs_free[r])
            wt("s", t1)
            t2 = em("s", lambda s, sl=sl, xb=xbuf, xq=xsqbuf: s.activation(
                xq, xb, AF.Square, accum_out=st_sqs[:, sl]), sm_s)
            last_sq_tok[0] = t2
            return r, t1, t2

        def emit_ln_finish(xbuf, xnbuf, r, t1, t2, tok_loaded_v, extra_v_wait=None):
            sl = slice(r, r + 1)
            if extra_v_wait is not None:
                wt("v", extra_v_wait)
            wt("v", t1)
            t3 = em("v", lambda v, sl=sl: v.tensor_scalar(
                st_nm[:, sl], st_sum[:, sl], scalar1=-1.0 / D, scalar2=None, op0=ALU.mult), sm_v)
            slot_sum_free[r] = t3
            wt("v", t3)
            t4 = em("v", lambda v, sl=sl: v.tensor_tensor(
                st_m2[:, sl], st_nm[:, sl], st_nm[:, sl], ALU.mult), sm_v)
            wt("v", t2)
            wt("v", t4)
            t5 = em("v", lambda v, sl=sl: v.tensor_scalar(
                st_var[:, sl], st_sqs[:, sl], scalar1=1.0 / D, scalar2=st_m2[:, sl],
                op0=ALU.mult, op1=ALU.subtract), sm_v)
            slot_sqs_free[r] = t5
            wt("v", t5)
            t6 = em("v", lambda v, sl=sl: v.tensor_scalar(
                st_var[:, sl], st_var[:, sl], scalar1=EPS, scalar2=None, op0=ALU.add), sm_v)
            wt("s", t6)
            wt("s", slot_sq_free[r])
            t7 = em("s", lambda s, sl=sl: s.activation(st_sq[:, sl], st_var[:, sl], AF.Sqrt), sm_s)
            wt("v", t7)
            t8 = em("v", lambda v, sl=sl: v.reciprocal(st_rs[:, sl], st_sq[:, sl]), sm_v)
            slot_sq_free[r] = t8
            wt("v", t8)
            wt("v", tok_loaded_v)
            t9 = em("v", lambda v, sl=sl, xb=xbuf, xo=xnbuf: v.tensor_scalar(
                xo, xb, scalar1=st_nm[:, sl], scalar2=st_rs[:, sl],
                op0=ALU.add, op1=ALU.mult), sm_v)
            return t9

        def emit_ln(xbuf, xnbuf, xsqbuf, tok_loaded_v, tok_loaded_s,
                    extra_v_wait=None, extra_s_wait=None):
            r, t1, t2 = emit_ln_stats(xbuf, xsqbuf, tok_loaded_s, extra_s_wait)
            return emit_ln_finish(xbuf, xnbuf, r, t1, t2, tok_loaded_v, extra_v_wait)

        A = Carver()
        a_x = [A.take(D, F32) for _ in range(3)]
        a_xsq = A.take(D, F32)
        a_xn = [A.take(D, F32) for _ in range(2)]
        a_xnF = [A.take(512, BF16) for _ in range(32)]
        a_wkvc = A.take(32 * 576, BF16).rearrange("p (c f) -> p c f", c=32)
        a_wqct = [A.take(4 * QL, BF16).rearrange("p (c f) -> p c f", c=4) for _ in range(2)]
        a_kvst = [A.take(512, BF16) for _ in range(2)]
        a_qcst = [A.take(512, BF16) for _ in range(2)]

        tok_wkvc = dma("g", a_wkvc, d_wkvc.ap().rearrange("p (c f) -> p c f", c=32), "wkvc")

        tok_x_free = [None, None, None]
        tok_xld = [None, None, None]
        stats_tok = [None, None]
        tok_xn_cons = [None, None]
        tok_xnF_ready = [None] * 32
        tok_xnF_free = None
        tok_kvst_free = [None, None]
        tok_qcst_free = [None, None]
        qc_buf_free = [None, None]
        tok_qc_last = [None] * 6
        pend_kv = [None]

        for tb in range(min(16, _TBS)):
            for tt in range(min(4, _TTS)):
                tile = tb * 4 + tt
                bi = tile % 3
                bj = tile % 2
                if tile == 0:
                    for t0 in (0, 1):
                        wt("y", tok_x_free[t0 % 3])
                        tok_xld[t0 % 3] = dma("y", a_x[t0 % 3],
                                              d_x[t0 * 128:(t0 + 1) * 128, :],
                                              ("x0", "x1", "x2b")[t0 % 3])
                    stats_tok[0 % 2] = emit_ln_stats(a_x[0], a_xsq, tok_xld[0])
                if tile + 2 < 64:
                    t2_ = tile + 2
                    wt("y", tok_x_free[t2_ % 3])
                    tok_xld[t2_ % 3] = dma("y", a_x[t2_ % 3],
                                           d_x[t2_ * 128:(t2_ + 1) * 128, :],
                                           ("x0", "x1", "x2b")[t2_ % 3])
                if tile + 1 < 64:
                    t1_ = tile + 1
                    stats_tok[t1_ % 2] = emit_ln_stats(a_x[t1_ % 3], a_xsq,
                                                       tok_xld[t1_ % 3])
                r_, ts1, ts2 = stats_tok[tile % 2]
                tok_xn = emit_ln_finish(a_x[bi], a_xn[bj], r_, ts1, ts2,
                                        tok_xld[bi], extra_v_wait=tok_xn_cons[bj])
                tok_x_free[bi] = tok_xn
                if not (_SUB & 8):
                    continue
                wt("t", tok_xn)
                last_tp = None
                for dc in range(32):
                    b = 5 + (dc % 2)
                    wt("t", bankfree[b])
                    tok_tp = em("t", lambda t, b=b, xb=a_xn[bj], dc=dc: t.transpose(
                        bank(b)[:, 0:128], xb[:, dc * 128:dc * 128 + 128], sb_id[:, :]), sm_t)
                    last_tp = tok_tp
                    if _TPV == 1:
                        continue
                    eng = "s" if dc % 2 == 0 else "v"
                    if _TPV == 2:
                        eng = "v"
                    if _TPV == 3:
                        eng = "s"
                    wt(eng, tok_tp)
                    if tt == 0:
                        wt(eng, tok_xnF_free)
                    if eng == "s":
                        tok_ev = em(eng, lambda e, b=b, dc=dc, tt=tt: e.copy(
                            a_xnF[dc][:, tt * 128:tt * 128 + 128], bank(b)[:, 0:128]), sm_s)
                    else:
                        tok_ev = em(eng, lambda e, b=b, dc=dc, tt=tt: e.tensor_copy(
                            a_xnF[dc][:, tt * 128:tt * 128 + 128], bank(b)[:, 0:128]), sm_v)
                    bankfree[b] = tok_ev
                    if tt == 3:
                        tok_xnF_ready[dc] = tok_ev
                tok_xn_cons[bj] = last_tp
            if pend_kv[0] is not None:
                pend_kv[0]()
                pend_kv[0] = None
            wt("t", tok_wkvc)
            for dc in range(32):
                wt("t", tok_xnF_ready[dc])
            tok_kv_last = [None] * 5
            for f in range(5):
                rows = 128 if f < 4 else 64
                wt("t", bankfree[f])
                tok_mm = None
                for dc in range(32):
                    tok_mm = em("t", lambda t, f=f, dc=dc, rows=rows: t.matmul(
                        bank(f)[0:rows, 0:512],
                        a_wkvc[:, dc, f * 128:f * 128 + rows],
                        a_xnF[dc],
                        start=(dc == 0), stop=(dc == 31)),
                        sm_t if dc == 31 else None)
                tok_kv_last[f] = tok_mm
            def flush_kv(tb=tb, toks=list(tok_kv_last)):
                for f in range(5):
                    rows = 128 if f < 4 else 64
                    kb = f % 2
                    wt("v", toks[f])
                    wt("v", tok_kvst_free[kb])
                    tok_ev = em("v", lambda v, f=f, kb=kb, rows=rows: v.tensor_scalar(
                        a_kvst[kb][0:rows, :], bank(f)[0:rows, 0:512],
                        scalar1=sb_bkvc[0:rows, f:f + 1], scalar2=None, op0=ALU.add), sm_v)
                    bankfree[f] = tok_ev
                    wt("y", tok_ev)
                    tok_st = dma("y", bass.AP(d_kvF, f * 128 * S + tb * 512,
                                              [[S, rows], [1, 512]]),
                                 a_kvst[kb][0:rows, :], "kv0" if kb == 0 else "kv1")
                    if debug >= 1:
                        tok_st = dma("y", bass.AP(dbg["kvF"], f * 128 * S + tb * 512,
                                                  [[S, rows], [1, 512]]),
                                     a_kvst[kb][0:rows, :], "kv0" if kb == 0 else "kv1")
                    tok_kvst_free[kb] = tok_st
            pend_kv[0] = flush_kv
            if tb < 2:
                if pend_kv[0] is not None:
                    pend_kv[0]()
                    pend_kv[0] = None
            if tb < 2 and (_SUB & 4):
                for pas in range(2):
                    tok_mm = None
                    for dcg in range(8):
                        gb = dcg % 2
                        wt("g", qc_buf_free[gb])
                        tok_w = dma("g", a_wqct[gb],
                                    bass.AP(d_wqcT, dcg * 4 * 128 * QL,
                                            [[QL, 128], [128 * QL, 4], [1, QL]]),
                                    "wq0" if gb == 0 else "wq1")
                        wt("t", tok_w)
                        for di in range(4):
                            dc = dcg * 4 + di
                            for f6 in range(6):
                                fg = pas * 6 + f6
                                if dc == 0:
                                    wt("t", bankfree[f6])
                                tok_mm = em("t", lambda t, f6=f6, fg=fg, gb=gb, dc=dc, di=di: t.matmul(
                                    bank(f6)[:, 0:512],
                                    a_wqct[gb][:, di, fg * 128:fg * 128 + 128],
                                    a_xnF[dc],
                                    start=(dc == 0), stop=(dc == 31)),
                                    sm_t if (f6 == 5 or dc == 31) else None)
                                if dc == 31:
                                    tok_qc_last[f6] = tok_mm
                        qc_buf_free[gb] = tok_mm
                    for f6 in range(6):
                        fg = pas * 6 + f6
                        qb_ = f6 % 2
                        wt("v", tok_qc_last[f6])
                        wt("v", tok_qcst_free[qb_])
                        tok_ev = em("v", lambda v, f6=f6, fg=fg, qb_=qb_: v.tensor_scalar(
                            a_qcst[qb_], bank(f6)[:, 0:512],
                            scalar1=sb_bqc[:, fg:fg + 1], scalar2=None, op0=ALU.add), sm_v)
                        bankfree[f6] = tok_ev
                        wt("y", tok_ev)
                        tok_st = dma("y", bass.AP(d_qcF, tb * (128 * 12 * 512) + fg * 512,
                                                  [[12 * 512, 128], [1, 512]]),
                                     a_qcst[qb_], "qc0" if qb_ == 0 else "qc1")
                        if debug >= 1:
                            tok_st = dma("y", bass.AP(dbg["qcF"], tb * (128 * 12 * 512) + fg * 512,
                                                      [[12 * 512, 128], [1, 512]]),
                                         a_qcst[qb_], "qc0" if qb_ == 0 else "qc1")
                        tok_qcst_free[qb_] = tok_st
            if tb < 2 and (_SUB & 4):
                tok_xnF_free = tok_qc_last[5]
            else:
                tok_xnF_free = tok_kv_last[4]

        if pend_kv[0] is not None:
            pend_kv[0]()
            pend_kv[0] = None

        run_b = _PHASE >= 2
        run_c = _PHASE >= 3
        sched.barrier()
        B = Carver(0, OS_BASE)
        b_kvn = [B.take(S, BF16) for _ in range(4)]
        b_rope = B.take(S, BF16)
        b_K = [B.take(S, BF16) for _ in range(2)]
        b_V = B.take(64 * 256, BF16)
        b_Q = [B.take(TOK, BF16) for _ in range(3)]
        b_P = [[B.take(512, BF16) for _ in range(3)] for _ in range(2)]
        b_qch = B.take(12 * 512, BF16).rearrange("p (c f) -> p c f", c=12)
        b_wqup = B.take(12 * 384, BF16).rearrange("p (c f) -> p c f", c=12)
        b_wkK = B.take(4 * 256, BF16).rearrange("p (c f) -> p c f", c=4)
        b_wkV = B.take(4 * 256, BF16).rearrange("p (c f) -> p c f", c=4)
        b_bcast = B.take(512, F32)
        b_otmp = B.take(512, F32)
        b_Ps = [[B.take(512, BF16) for _ in range(2)] for _ in range(2)]
        b_osum = arena[:, OS_BASE:OS_BASE + 1024]
        b_osbf = arena[:, OS_BASE + 1024:OS_BASE + 1536].bitcast(BF16)

        for lat in range(4):
            dma("y", b_kvn[lat],
                bass.AP(d_kvF, lat * 128 * S, [[S, 128], [1, S]]), "kvn")
        dma("y", b_rope[0:64, :], bass.AP(d_kvF, 512 * S, [[S, 64], [1, S]]), "kvn")
        tok_kvn_all = dma("y", b_rope[64:128, :],
                          bass.AP(d_kvF, 512 * S, [[S, 64], [1, S]]), "kvn")
        tok_rope_l = tok_rope_h = tok_kvn_all
        tok_osum0 = em("v", lambda v: v.memset(b_osum, 0.0), sm_v)

        wt("v", tok_rope_h)
        wt("v", tok_rope_l)
        tok_ropemax = em("v", lambda v: v.tensor_reduce(
            sb_stab[:, 5:6], b_rope, AX.X, ALU.max, apply_absolute_value=True), sm_v)

        tok_w_free = {"q": None, "k": None, "v": None}
        tok_qch_free = [None, None]
        tok_K_free = [None, None]
        tok_V_free = None
        tok_Q_free = None
        tok_osum_last = tok_osum0
        tok_exp_prev_pair = None
        tok_gq_prev = None
        tok_gk_prev = None
        tok_tc1_prev = None
        tok_bcast_prev = None

        pend = [None]
        texp = [(None, None)] * 3
        tps_tok = [[None, None], [None, None]]
        tps_free = [[None, None], [None, None]]
        p_read = [[None] * 3, [None] * 3]
        tok_bc_prev = [None, None]
        tok_osum_chain = [tok_osum0]
        flush_fn = [lambda: None]

        def flush_epi_outer():
            flush_fn[0]()
        npair_run = min(_NPAIR_RUN, NPAIR) if run_b else 0
        for p in range(npair_run):
            wt("g", tok_w_free["k"])
            tok_wk = dma("g", b_wkK, bass.AP(d_wkK, p * (4 * 256),
                                             [[NPAIR * 4 * 256, 128], [256, 4], [1, 256]]), "wkK")
            wt("g", tok_w_free["v"])
            tok_wv = dma("g", b_wkV, bass.AP(d_wkV, p * (4 * 256),
                                             [[NPAIR * 4 * 256, 128], [256, 4], [1, 256]]), "wkV")
            wt("g", tok_w_free["q"])
            tok_wq = dma("g", b_wqup, bass.AP(d_wqup, p * (12 * 384),
                                              [[NPAIR * 12 * 384, 128], [384, 12], [1, 384]]), "wqup")

            prod_rot = [0]

            def prod_bank():
                b = prod_rot[0] % 4
                prod_rot[0] += 1
                return b

            wt("t", tok_wq)
            wt("v", tok_Q_free)
            tok_Q_ready = None
            for half in range(2):
                wt("y", tok_qch_free[1] if half == 0 else tok_qch_free[0])
                tok_qc = dma("y", b_qch, bass.AP(d_qcF, half * (128 * 12 * 512),
                                                 [[12 * 512, 128], [512, 12], [1, 512]]),
                             "qch0" if half == 0 else "qch1")
                wt("t", tok_qc)
                for qch in range(3):
                    b = prod_bank()
                    wt("t", bankfree[b])
                    tok_mm = None
                    for ql in range(12):
                        tok_mm = em("t", lambda t, b=b, qch=qch, ql=ql: t.matmul(
                            bank(b),
                            b_wqup[:, ql, qch * 128:qch * 128 + 128],
                            b_qch[:, ql, :],
                            start=(ql == 0), stop=(ql == 11)),
                            sm_t if ql == 11 else None)
                    tok_qch_free[half] = tok_mm
                    wt("v", tok_mm)
                    tok_Q_ready = em("v", lambda v, b=b, qch=qch, half=half, p=p: v.tensor_scalar(
                        b_Q[qch][:, half * 512:half * 512 + 512], bank(b),
                        scalar1=sb_bqup[:, 3 * p + qch:3 * p + qch + 1], scalar2=ISQ,
                        op0=ALU.add, op1=ALU.mult), sm_v)
                    bankfree[b] = tok_Q_ready

            wt("v", tok_Q_ready)
            wt("v", tok_gq_prev)
            tql = None
            for qch in range(3):
                tql = em("v", lambda v, qch=qch: v.tensor_reduce(
                    sb_stab[:, qch:qch + 1], b_Q[qch], AX.X, ALU.max,
                    apply_absolute_value=True), sm_v)
            wt("v", tql)
            tqm = em("v", lambda v: v.tensor_reduce(
                sb_qmax[:, :], sb_stab[:, 0:3], AX.X, ALU.max), sm_v)
            wt("t", tok_wk)
            if p == 0:
                wt("t", tok_kvn_all)
            tok_K_ready = [None, None]
            for h in range(2):
                wt("v", tok_K_free[h])
                for tk in range(16):
                    b = prod_bank()
                    wt("t", bankfree[b])
                    tok_mm = None
                    for lat in range(4):
                        tok_mm = em("t", lambda t, b=b, h=h, tk=tk, lat=lat: t.matmul(
                            bank(b),
                            b_wkK[:, lat, h * 128:h * 128 + 128],
                            b_kvn[lat][:, tk * 512:tk * 512 + 512],
                            start=(lat == 0), stop=(lat == 3)),
                            sm_t if lat == 3 else None)
                    wt("v", tok_mm)
                    tok_ev = em("v", lambda v, b=b, h=h, tk=tk, p=p: v.tensor_scalar(
                        b_K[h][:, tk * 512:tk * 512 + 512], bank(b),
                        scalar1=sb_bkK[:, 2 * p + h:2 * p + h + 1], scalar2=None,
                        op0=ALU.add), sm_v)
                    bankfree[b] = tok_ev
                tok_K_ready[h] = tok_ev

            wt("v", tok_K_ready[0])
            wt("v", tok_K_ready[1])
            wt("v", tok_gk_prev)
            em("v", lambda v: v.tensor_reduce(
                sb_stab[:, 3:4], b_K[0], AX.X, ALU.max, apply_absolute_value=True), sm_v)
            tk1 = em("v", lambda v: v.tensor_reduce(
                sb_stab[:, 4:5], b_K[1], AX.X, ALU.max, apply_absolute_value=True), sm_v)
            wt("v", tk1)
            wt("v", tok_ropemax)
            tkm = em("v", lambda v: v.tensor_reduce(
                sb_kmax[:, :], sb_stab[:, 3:6], AX.X, ALU.max), sm_v)
            wt("t", tok_wv)
            wt("v", tok_V_free)
            tok_V_ready = None
            for tk in range(64):
                b = prod_bank()
                wt("t", bankfree[b])
                tok_mm = None
                for lat in range(4):
                    tok_mm = em("t", lambda t, b=b, tk=tk, lat=lat: t.matmul(
                        bank(b)[:, 0:256],
                        b_kvn[lat][:, tk * 128:tk * 128 + 128],
                        b_wkV[:, lat, :],
                        start=(lat == 0), stop=(lat == 3)),
                        sm_t if lat == 3 else None)
                wt("v", tok_mm)
                tok_V_ready = em("v", lambda v, b=b, tk=tk: v.tensor_copy(
                    b_V[:, tk * 256:tk * 256 + 256], bank(b)[:, 0:256]), sm_v)
                bankfree[b] = tok_V_ready

            wt("g", tqm)
            wt("g", tkm)
            wt("g", tok_tc1_prev)
            tok_gq_prev = em("g", lambda g: g.tensor_reduce(
                sb_qm[:, :], sb_qmax[:, :], AX.C, ALU.max), sm_gc)
            tok_gk_prev = em("g", lambda g: g.tensor_reduce(
                sb_km[:, :], sb_kmax[:, :], AX.C, ALU.max), sm_gc)
            wt("v", tok_gq_prev)
            wt("v", tok_gk_prev)
            tok_tc1_prev = em("v", lambda v: v.tensor_tensor(
                sb_cqk[:, :], sb_qm[:, :], sb_km[:, :], ALU.mult), sm_v)
            wt("v", tok_tc1_prev)
            tc2 = em("v", lambda v: v.tensor_scalar(
                sb_cqk[:, :], sb_cqk[:, :], scalar1=-float(QH), scalar2=-45.0,
                op0=ALU.mult, op1=ALU.max), sm_v)
            flush_epi_outer()
            wt("t", tc2)
            wt("t", bankfree[4])
            tbc = em("t", lambda t: t.matmul(
                bank(4)[:, 0:1], sb_ones[:, :], sb_cqk[:, :], start=True, stop=True), sm_t)
            wt("v", tbc)
            wt("v", tok_exp_prev_pair)
            tnc = em("v", lambda v: v.tensor_copy(sb_negc_col[:, :], bank(4)[:, 0:1]), sm_v)
            bankfree[4] = tnc
            wt("t", tok_V_ready)
            wt("s", tnc)
            tok_score_last = None

            def emit_scores(qb, kc):
                par = kc % 2
                b0, b1 = par, 2 + par
                wt("t", bankfree[b0])
                wt("t", bankfree[b1])
                em("t", lambda t, b0=b0, kc=kc, qb=qb: t.matmul(
                    bank(b0), b_K[0][:, kc * 128:kc * 128 + 128],
                    b_Q[0][:, qb * 512:qb * 512 + 512], start=True, stop=False))
                em("t", lambda t, b1=b1, kc=kc, qb=qb: t.matmul(
                    bank(b1), b_K[1][:, kc * 128:kc * 128 + 128],
                    b_Q[1][:, qb * 512:qb * 512 + 512], start=True, stop=False))
                em("t", lambda t, b0=b0, kc=kc, qb=qb: t.matmul(
                    bank(b0), b_rope[0:64, kc * 128:kc * 128 + 128],
                    b_Q[2][0:64, qb * 512:qb * 512 + 512],
                    start=False, stop=True, tile_position=(0, 0)))
                tok_sc = em("t", lambda t, b1=b1, kc=kc, qb=qb: t.matmul(
                    bank(b1), b_rope[64:128, kc * 128:kc * 128 + 128],
                    b_Q[2][64:128, qb * 512:qb * 512 + 512],
                    start=False, stop=True, tile_position=(64, 0)), sm_t)
                wt("s", tok_sc)
                wt("s", p_read[0][kc % 3])
                wt("s", p_read[1][kc % 3])
                te0 = em("s", lambda s, b0=b0, kc=kc: s.activation(
                    b_P[0][kc % 3], bank(b0), AF.Exp, bias=sb_negc_col[:, :]), sm_s)
                te1 = em("s", lambda s, b1=b1, kc=kc: s.activation(
                    b_P[1][kc % 3], bank(b1), AF.Exp, bias=sb_negc_col[:, :]), sm_s)
                bankfree[b0] = te0
                bankfree[b1] = te1
                texp[kc % 3] = (te0, te1)
                if kc % 2 == 1:
                    for h, te in ((0, te0), (1, te1)):
                        wt("v", te)
                        wt("v", tps_free[h][(kc // 2) % 2])
                        tok_add = em("v", lambda v, h=h, kc=kc: v.tensor_tensor(
                            b_Ps[h][(kc // 2) % 2], b_P[h][(kc - 1) % 3],
                            b_P[h][kc % 3], ALU.add), sm_v)
                        tps_tok[h][(kc // 2) % 2] = tok_add
                        p_read[h][(kc - 1) % 3] = tok_add
                        p_read[h][kc % 3] = tok_add
                return tok_sc, te1

            def emit_pv(qb, kc, epi_banks_wait):
                if epi_banks_wait:
                    for ob in (4, 5):
                        wt("t", bankfree[ob])
                wt("t", texp[kc % 3][0])
                wt("t", texp[kc % 3][1])
                em("t", lambda t, kc=kc: t.matmul(
                    bank(4), b_V[:, kc * 256:kc * 256 + 128], b_P[0][kc % 3],
                    start=(kc == 0), stop=(kc == 63)))
                tok = em("t", lambda t, kc=kc: t.matmul(
                    bank(5), b_V[:, kc * 256 + 128:kc * 256 + 256], b_P[1][kc % 3],
                    start=(kc == 0), stop=(kc == 63)),
                    sm_t if kc == 63 else None)
                if kc % 2 == 1:
                    j = (kc // 2) % 2
                    if kc == 1:
                        for ob in (6, 7):
                            wt("t", bankfree[ob])
                    for h in range(2):
                        wt("t", tps_tok[h][j])
                        tok = em("t", lambda t, kc=kc, h=h, j=j: t.matmul(
                            bank(6 + h)[0:1, :], sb_onec[:, :], b_Ps[h][j],
                            start=(kc == 1), stop=(kc == 63)),
                            sm_t if kc == 63 else None)
                        tps_free[h][j] = tok if kc == 63 else None
                return tok

            def emit_epi_v1(qb, tok_pv):
                trs = []
                for h in range(2):
                    wt("v", tok_pv)
                    wt("v", tok_bc_prev[h])
                    trs.append(em("v", lambda v, h=h: v.reciprocal(
                        (sb_recip0 if h == 0 else sb_recip1)[0:1, :],
                        bank(6 + h)[0:1, 0:512]), sm_v))
                    bankfree[6 + h] = trs[h]
                return trs

            def flush_epi():
                if not pend[0]:
                    return
                qb, trs = pend[0]
                pend[0] = None
                for h in range(2):
                    bcb = 1 + 2 * h
                    wt("t", trs[h])
                    wt("t", bankfree[bcb])
                    tok_bc = em("t", lambda t, bcb=bcb, h=h: t.matmul(
                        bank(bcb), sb_ones[:, :],
                        (sb_recip0 if h == 0 else sb_recip1)[0:1, :],
                        start=True, stop=True), sm_t)
                    tok_bc_prev[h] = tok_bc
                    wt("v", tok_bc)
                    tcp = em("v", lambda v, bcb=bcb: v.tensor_copy(
                        b_bcast, bank(bcb)), sm_v)
                    bankfree[bcb] = tcp
                    wt("v", tcp)
                    tmu = em("v", lambda v, h=h: v.tensor_tensor(
                        b_otmp, bank(4 + h), b_bcast, ALU.mult), sm_v)
                    bankfree[4 + h] = tmu
                    wt("v", tmu)
                    wt("v", tok_osum_chain[0])
                    tok_osum_chain[0] = em("v", lambda v, qb=qb: v.tensor_tensor(
                        b_osum[:, qb * 512:qb * 512 + 512],
                        b_osum[:, qb * 512:qb * 512 + 512], b_otmp, ALU.add), sm_v)

            flush_fn[0] = flush_epi
            for qb in range(2):
                tok_pv = None
                for kc in range(64):
                    tok_sc, te1 = emit_scores(qb, kc)
                    tok_score_last = tok_sc
                    tok_exp_prev_pair = te1
                    if kc == 2:
                        flush_epi()
                    if kc >= 2:
                        tok_pv = emit_pv(qb, kc - 2, epi_banks_wait=(kc == 2))
                tok_pv = emit_pv(qb, 62, False)
                tok_pv = emit_pv(qb, 63, False)
                tok_V_free = tok_pv
                trs = emit_epi_v1(qb, tok_pv)
                pend[0] = (qb, trs)
            tok_K_free = [tok_score_last, tok_score_last]
            tok_Q_free = tok_score_last
            tok_w_free = {"q": tok_Q_ready, "k": tok_K_ready[1], "v": tok_V_ready}

        flush_fn[0]()
        wt("v", tok_bvsum)
        wt("v", tok_osum_chain[0])
        tb1 = em("v", lambda v: v.tensor_scalar(
            b_osum, b_osum, scalar1=sb_bvsum[:, :], scalar2=None, op0=ALU.add), sm_v)
        wt("v", tb1)
        tok_osbf = em("v", lambda v: v.tensor_copy(b_osbf, b_osum), sm_v)

        if debug >= 2:
            wt("y", tok_osbf)
            dma("y", dbg["K0"].ap(), b_K[0], "cwy")
            dma("y", dbg["K1"].ap(), b_K[1], "cwy")
            dma("y", dbg["V0"].ap(), b_V, "cwy")
            for qch in range(3):
                dma("y", bass.AP(dbg["Q0"], qch * TOK, [[3 * TOK, 128], [1, TOK]]), b_Q[qch], "cwy")
            dma("y", dbg["osum"].ap(), b_osum, "cwy")

        def _phase_c():
            sched.barrier()
            C = Carver(0, OS_BASE)
            c_xn2F = [C.take(TOK, BF16) for _ in range(32)]
            c_rows = C.take(D, F32)
            c_la = C.take(32 * 128, BF16).rearrange("p (c f) -> p c f", c=32)
            c_lb = C.take(D, BF16)
            c_hT = C.take(TOK, BF16)
            c_h = C.take(128, F32)
            c_outs = [C.take(512, F32) for _ in range(2)]
            c_woT = C.take(D, BF16)
            c_biasbc = C.take(D, F32)
            C_MAIN_BASE = C.off
            c_x = C.take(D, F32)
            c_x2 = C.take(D, F32)
            c_xn2 = C.take(D, F32)

            dma("g", c_la, d_laT.ap().rearrange("p (c f) -> p c f", c=32), "cw")
            dma("g", c_lb, d_lbT.ap(), "cw")
            tok_woT = dma("g", c_woT, d_woT.ap(), "cw")
            tok_la = tok_lb = tok_woT
            tok_borow = dma("y", c_rows[0:1, :], d_bo.ap(), "cwy")

            tok_bo_bc = None
            wt("t", tok_borow)
            for ob in range(8):
                bb = 6 + (ob % 2)
                wt("t", bankfree[bb])
                tm = em("t", lambda t, bb=bb, ob=ob: t.matmul(
                    bank(bb), sb_ones[:, :], c_rows[0:1, ob * 512:(ob + 1) * 512],
                    start=True, stop=True), sm_t)
                tok_bcast_mm_last = tm
                wt("v", tm)
                tok_bo_bc = em("v", lambda v, bb=bb, ob=ob: v.tensor_copy(
                    c_biasbc[:, ob * 512:(ob + 1) * 512], bank(bb)), sm_v)
                bankfree[bb] = tok_bo_bc

            tok_x_used = None
            tok_x2_dma_last = None
            tok_xn2_cons = None
            tok_x2w = []
            tok_evS_last = None
            tok_evV_last = None
            tok_op_last = None
            for tt in range(8):
                wt("y", tok_x_used)
                tok_xl = dma("y", c_x, d_x[tt * 128:(tt + 1) * 128, :], "x0")
                wt("t", tok_osbf)
                wt("t", tok_woT)
                for ob in range(8):
                    wt("t", bankfree[ob])
                    tm = em("t", lambda t, tt=tt, ob=ob: t.matmul(
                        bank(ob), b_osbf[:, tt * 128:tt * 128 + 128],
                        c_woT[:, ob * 512:(ob + 1) * 512], start=True, stop=True),
                        sm_t if ob == 7 else None)
                tok_op_last = tm
                wt("v", tm)
                wt("v", tok_bo_bc)
                wt("v", tok_x2_dma_last)
                t1 = em("v", lambda v: v.tensor_tensor(
                    c_x2, ps[:, 0:4096], c_biasbc, ALU.add), sm_v)
                for ob in range(8):
                    bankfree[ob] = t1
                wt("v", t1)
                wt("v", tok_xl)
                tok_ev_all = em("v", lambda v: v.tensor_tensor(
                    c_x2, c_x2, c_x, ALU.add), sm_v)
                tok_x_used = tok_ev_all
                wt("y", tok_ev_all)
                tw = dma("y", bass.AP(d_x2, tt * 128 * 512,
                                      [[512, 128], [TOK * 512, 8], [1, 512]]),
                         c_x2.rearrange("p (a b) -> p a b", a=8), "x2w")
                if debug >= 3:
                    tw = dma("y", bass.AP(dbg["x2"], tt * 128 * 512,
                                          [[512, 128], [TOK * 512, 8], [1, 512]]),
                             c_x2.rearrange("p (a b) -> p a b", a=8), "x2w")
                tok_x2w.append(tw)
                tok_x2_dma_last = tok_x2w[-1]
                t9 = emit_ln(c_x2, c_xn2, c_xn2, tok_ev_all, tok_ev_all,
                             extra_s_wait=tok_xn2_cons)
                wt("t", t9)
                last_tp = None
                for dc in range(32):
                    b = 6 + (dc % 2)
                    wt("t", bankfree[b])
                    tok_tp = em("t", lambda t, b=b, dc=dc: t.transpose(
                        bank(b)[:, 0:128], c_xn2[:, dc * 128:dc * 128 + 128], sb_id[:, :]), sm_t)
                    last_tp = tok_tp
                    eng = "s" if dc % 2 == 0 else "v"
                    wt(eng, tok_tp)
                    if eng == "s":
                        tok_ev = em(eng, lambda e, b=b, dc=dc, tt=tt: e.copy(
                            c_xn2F[dc][:, tt * 128:tt * 128 + 128], bank(b)[:, 0:128]), sm_s)
                    else:
                        tok_ev = em(eng, lambda e, b=b, dc=dc, tt=tt: e.tensor_copy(
                            c_xn2F[dc][:, tt * 128:tt * 128 + 128], bank(b)[:, 0:128]), sm_v)
                    bankfree[b] = tok_ev
                    if eng == "s":
                        tok_evS_last = tok_ev
                    else:
                        tok_evV_last = tok_ev
                tok_xn2_cons = last_tp

            wt("y", tok_bcast_mm_last)
            tok_fbrow = dma("y", c_rows[0:1, :], d_fb.ap(), "cwy")
            tok_fb_bc = None
            wt("t", tok_fbrow)
            wt("t", tok_x_used)
            for ob in range(8):
                bb = 6 + (ob % 2)
                wt("t", bankfree[bb])
                tm = em("t", lambda t, bb=bb, ob=ob: t.matmul(
                    bank(bb), sb_ones[:, :], c_rows[0:1, ob * 512:(ob + 1) * 512],
                    start=True, stop=True), sm_t)
                wt("v", tm)
                tok_fb_bc = em("v", lambda v, bb=bb, ob=ob: v.tensor_copy(
                    c_biasbc[:, ob * 512:(ob + 1) * 512], bank(bb)), sm_v)
                bankfree[bb] = tok_fb_bc

            wt("t", tok_la)
            wt("t", tok_evS_last)
            wt("t", tok_evV_last)
            tok_hT_done = None
            tok_h_cons = None
            for tt in range(8):
                b = tt % 2
                wt("t", bankfree[b])
                tok_mm = None
                for dc in range(32):
                    tok_mm = em("t", lambda t, b=b, tt=tt, dc=dc: t.matmul(
                        bank(b)[:, 0:128], c_xn2F[dc][:, tt * 128:tt * 128 + 128],
                        c_la[:, dc, :], start=(dc == 0), stop=(dc == 31)),
                        sm_t if dc == 31 else None)
                wt("v", tok_mm)
                wt("v", tok_h_cons)
                tcp = em("v", lambda v, b=b: v.tensor_copy(c_h, bank(b)[:, 0:128]), sm_v)
                bankfree[b] = tcp
                tb_ = 2 + (tt % 2)
                wt("t", tcp)
                wt("t", bankfree[tb_])
                ttp = em("t", lambda t, tb_=tb_: t.transpose(
                    bank(tb_)[:, 0:128], c_h, sb_id[:, :]), sm_t)
                tok_h_cons = ttp
                wt("v", ttp)
                tok_hT_done = em("v", lambda v, tb_=tb_, tt=tt: v.tensor_scalar(
                    c_hT[:, tt * 128:tt * 128 + 128], bank(tb_)[:, 0:128],
                    scalar1=sb_coef[:, :], scalar2=None, op0=ALU.mult), sm_v)
                bankfree[tb_] = tok_hT_done

            CM = Carver(C_MAIN_BASE, OS_BASE)
            c_wf = [CM.take(16 * 512, BF16).rearrange("p (c f) -> p c f", c=16) for _ in range(2)]
            c_x2s = [CM.take(8 * 512, F32).rearrange("p (c f) -> p c f", c=8) for _ in range(2)]

            tok_wfh_free = [None, None]
            tok_x2s_free = [None, None]
            tok_outs_free = [None, None]
            x2w_all = tok_x2w[-1]
            for ob in range(8):
                wb = ob % 2
                if ob == 0:
                    wt("g", tok_x_used)
                    wt("g", tok_xn2_cons)
                    wt("g", x2w_all)
                    wt("y", tok_x_used)
                    wt("y", tok_xn2_cons)
                tok_wfh = [None, None]
                for hf in range(2):
                    wt("g", tok_wfh_free[hf])
                    wt("g", tok_wfh_free[1])
                    tok_wfh[hf] = dma("g", c_wf[hf],
                                      bass.AP(d_wfT, ob * (32 * 512) + hf * (16 * 512),
                                              [[8 * 32 * 512, 128], [512, 16], [1, 512]]),
                                      "wf0" if hf == 0 else "wf1")
                wt("y", x2w_all)
                wt("y", tok_x2s_free[wb])
                tok_x2s = dma("y", c_x2s[wb], bass.AP(d_x2, ob * TOK * 512,
                                                      [[512, 128], [128 * 512, 8], [1, 512]]),
                              "x2r0" if wb == 0 else "x2r1")
                if ob == 0:
                    wt("t", tok_lb)
                    wt("t", tok_hT_done)
                tok_lora_mm = [None] * 8
                for hf in range(2):
                    wt("t", tok_wfh[hf])
                    for tt in range(8):
                        if hf == 0:
                            wt("t", bankfree[tt])
                        for dci in range(16):
                            dc = hf * 16 + dci
                            em("t", lambda t, tt=tt, hf=hf, dc=dc, dci=dci: t.matmul(
                                bank(tt), c_xn2F[dc][:, tt * 128:tt * 128 + 128],
                                c_wf[hf][:, dci, :], start=(dc == 0), stop=False))
                        if hf == 1:
                            tok_lora_mm[tt] = em("t", lambda t, tt=tt, ob=ob: t.matmul(
                                bank(tt), c_hT[:, tt * 128:tt * 128 + 128],
                                c_lb[:, ob * 512:(ob + 1) * 512],
                                start=False, stop=True), sm_t)
                        else:
                            if tt == 7:
                                tok_wfh_free[0] = em("t", lambda t: t.nop(), sm_t)
                tok_wfh_free[1] = tok_lora_mm[7]
                for tt in range(8):
                    oi = tt % 2
                    wt("v", tok_lora_mm[tt])
                    if ob == 0 and tt == 0:
                        wt("v", tok_fb_bc)
                    wt("v", tok_outs_free[oi])
                    t1 = em("v", lambda v, tt=tt, ob=ob, oi=oi: v.tensor_tensor(
                        c_outs[oi], bank(tt), c_biasbc[:, ob * 512:(ob + 1) * 512], ALU.add), sm_v)
                    bankfree[tt] = t1
                    wt("v", t1)
                    wt("v", tok_x2s)
                    t2 = em("v", lambda v, oi=oi, wb=wb, tt=tt: v.tensor_tensor(
                        c_outs[oi], c_outs[oi], c_x2s[wb][:, tt, :], ALU.add), sm_v)
                    wt("y", t2)
                    tw = dma("y", bass.AP(d_out, tt * 128 * D + ob * 512,
                                          [[D, 128], [1, 512]]), c_outs[oi],
                             "o0" if oi == 0 else "o1")
                    tok_outs_free[oi] = tw
                    if tt == 7:
                        tok_x2s_free[wb] = t2


        if run_c:
            _phase_c()

        for _sm in dma_sems.values():
            if _sm.n > 0:
                sched.wait("y", (_sm, _sm.n))

        with nc.Block() as block:
            block.tensor(lambda eng: [f(eng) for f in sched.ops["t"]])
            block.vector(lambda eng: [f(eng) for f in sched.ops["v"]])
            block.scalar(lambda eng: [f(eng) for f in sched.ops["s"]])
            block.gpsimd(lambda eng: [f(eng) for f in sched.ops["g"]])
            block.sync(lambda eng: [f(eng) for f in sched.ops["y"]])

    return nc


def _stage(inputs):
    g = lambda k: np.asarray(inputs[k], np.float32)
    X = g("L0_attn_hidden_size").reshape(S, D)
    Wq_c = g("L0_attn_Wq_c")
    bq_c = g("L0_attn_bq_c")
    Wq_up = g("L0_attn_Wq_up")
    bq_up = g("L0_attn_bq_up")
    Wkv_c = g("L0_attn_Wkv_c")
    bkv_c = g("L0_attn_bkv_c")
    Wkv_up = g("L0_attn_Wkv_up")
    bkv_up = g("L0_attn_bkv_up")
    Wo = g("L0_attn_Wo")
    bo = g("L0_attn_bo")
    lora_A = g("L0_ffn_lora_A")
    lora_B = g("L0_ffn_lora_B")
    Wf = g("L0_ffn_W")
    fb = g("L0_ffn_bias")
    lw = g("L0_ffn_lora_weights")

    C = np.ascontiguousarray

    PERM, KPERM, VPERM = [], [], []
    for p in range(NPAIR):
        h0, h1 = 2 * p, 2 * p + 1
        PERM += list(range(h0 * QH, h0 * QH + NOPE))
        PERM += list(range(h1 * QH, h1 * QH + NOPE))
        PERM += list(range(h0 * QH + NOPE, h0 * QH + QH))
        PERM += list(range(h1 * QH + NOPE, h1 * QH + QH))
        KPERM += list(range(h0 * 256, h0 * 256 + 128))
        KPERM += list(range(h1 * 256, h1 * 256 + 128))
        VPERM += list(range(h0 * 256 + 128, h0 * 256 + 256))
        VPERM += list(range(h1 * 256 + 128, h1 * 256 + 256))

    base = {}
    base["wqcT"] = C(Wq_c.T)
    base["bqc_t"] = C(bq_c.reshape(12, 128).T)
    wqupT = Wq_up[PERM, :].T
    base["wqup_til"] = C(wqupT.reshape(12, 128, NPAIR, 384).transpose(1, 2, 0, 3)
                         .reshape(128, NPAIR, 12 * 384))
    base["bqup_t"] = C(bq_up[PERM].reshape(48, 128).T)
    wkvcT = Wkv_c.T
    base["wkvc_til"] = C(wkvcT.reshape(32, 128, 576).transpose(1, 0, 2).reshape(128, 32 * 576))
    bkvc_p = np.zeros(640, np.float32)
    bkvc_p[:576] = bkv_c
    base["bkvc_t"] = C(bkvc_p.reshape(5, 128).T)
    wkKT = Wkv_up[KPERM, :].T
    base["wkK_til"] = C(wkKT.reshape(4, 128, NPAIR, 256).transpose(1, 2, 0, 3)
                        .reshape(128, NPAIR, 4 * 256))
    base["bkupK_t"] = C(bkv_up[KPERM].reshape(32, 128).T)
    wkVT = Wkv_up[VPERM, :].T
    base["wkV_til"] = C(wkVT.reshape(4, 128, NPAIR, 256).transpose(1, 2, 0, 3)
                        .reshape(128, NPAIR, 4 * 256))
    base["bkvV_cols"] = C(bkv_up[VPERM].reshape(32, 128).T)
    base["woT"] = C(Wo.T)
    base["bo_row"] = C(bo.reshape(1, D))
    WfT = Wf.T
    base["wffn_til"] = C(WfT.reshape(32, 128, 8, 512).transpose(1, 2, 0, 3)
                         .reshape(128, 8, 32 * 512))
    base["ffnb_row"] = C(fb.reshape(1, D))
    laT = lora_A.reshape(128, D).T
    base["loraA_til"] = C(laT.reshape(32, 128, 128).transpose(1, 0, 2).reshape(128, 32 * 128))
    base["loraBT"] = C(lora_B.transpose(0, 2, 1).reshape(128, D))
    base["lwrep"] = C(np.repeat(lw, 16)[:, None])
    base["ident"] = np.eye(128, dtype=np.float32)
    base["ones_row"] = np.ones((1, 128), np.float32)
    base["onecol_bf"] = np.ones((128, 1), ml_dtypes.bfloat16)

    in_maps = []
    for c in range(NCORE):
        m = dict(base)
        m["x_rot"] = C(np.concatenate([X[TOK * c:], X[:TOK * c]], axis=0))
        in_maps.append(m)
    return in_maps


def kernel(**inputs):
    in_maps = _stage(inputs)
    nc = build_nc(debug=_DEBUG)
    res = run_bass_kernel_spmd(nc, in_maps, core_ids=list(range(NCORE)))
    if _DEBUG:
        kernel._last = res
    out = np.concatenate([np.asarray(res.results[c]["out_c"]) for c in range(NCORE)], axis=0)
    return out.reshape(1, S, D).astype(np.float32)


# version 08
# speedup vs baseline: 1.0033x; 1.0019x over previous
import os
import math
import numpy as np
import ml_dtypes
import concourse.bass as bass
import concourse.mybir as mybir
from concourse.bass_utils import run_bass_kernel_spmd
from contextlib import ExitStack

F32 = mybir.dt.float32
BF16 = mybir.dt.bfloat16
AF = mybir.ActivationFunctionType
ALU = mybir.AluOpType
AX = mybir.AxisListType

NCORE = 8
S = 8192
D = 4096
TOK = 1024
NH = 32
NPAIR = 16
QL = 1536
KVL = 512
QH = 192
NOPE = 128
VH = 128
ISQ = 1.0 / math.sqrt(QH)
EPS = 1e-5

_DEBUG = int(os.environ.get("MK_DEBUG", "0"))
_NPAIR_RUN = int(os.environ.get("MK_NPAIR", str(NPAIR)))
_PHASE = int(os.environ.get("MK_PHASE", "3"))
_TBS = int(os.environ.get("MK_TBS", "16"))
_SUB = int(os.environ.get("MK_SUB", "15"))
_TPV = int(os.environ.get("MK_TPV", "0"))
_TTS = int(os.environ.get("MK_TTS", "4"))


class Sem:
    def __init__(self, handle, name):
        self.h = handle
        self.name = name
        self.n = 0


class Sched:
    ENGINES = ("t", "v", "s", "g", "y")

    def __init__(self):
        self.ops = {e: [] for e in self.ENGINES}
        self.sems = {}

    def sem(self, name, handle):
        sm = Sem(handle, name)
        self.sems[name] = sm
        return sm

    def emit(self, e, fn, sem=None, k=1):
        if sem is None:
            self.ops[e].append(fn)
            return None
        sem.n += k
        tag = sem.n

        def run(eng, fn=fn, s=sem.h, k=k):
            fn(eng).then_inc(s, k)

        self.ops[e].append(run)
        return (sem, tag)

    def wait(self, e, tok):
        if tok is None:
            return
        sem, tag = tok
        if tag <= 0:
            return
        self.ops[e].append(lambda eng, s=sem.h, t=tag: eng.wait_ge(s, t))

    def barrier(self):
        toks = [(sm, sm.n) for sm in self.sems.values() if sm.n > 0]
        for e in self.ENGINES:
            for tok in toks:
                self.wait(e, tok)


def build_nc(debug=0):
    nc = bass.Bass("TRN2", target_bir_lowering=False)

    d_x = nc.dram_tensor("x_rot", [S, D], F32, kind="ExternalInput")
    d_wqcT = nc.dram_tensor("wqcT", [D, QL], F32, kind="ExternalInput")
    d_bqc = nc.dram_tensor("bqc_t", [128, 12], F32, kind="ExternalInput")
    d_wqup = nc.dram_tensor("wqup_til", [128, NPAIR, 12 * 384], F32, kind="ExternalInput")
    d_bqup = nc.dram_tensor("bqup_t", [128, 48], F32, kind="ExternalInput")
    d_wkvc = nc.dram_tensor("wkvc_til", [128, 32 * 576], F32, kind="ExternalInput")
    d_bkvc = nc.dram_tensor("bkvc_t", [128, 5], F32, kind="ExternalInput")
    d_wkK = nc.dram_tensor("wkK_til", [128, NPAIR, 4 * 256], F32, kind="ExternalInput")
    d_bkK = nc.dram_tensor("bkupK_t", [128, 32], F32, kind="ExternalInput")
    d_wkV = nc.dram_tensor("wkV_til", [128, NPAIR, 4 * 256], F32, kind="ExternalInput")
    d_bkV = nc.dram_tensor("bkvV_cols", [128, 32], F32, kind="ExternalInput")
    d_woT = nc.dram_tensor("woT", [VH, D], F32, kind="ExternalInput")
    d_bo = nc.dram_tensor("bo_row", [1, D], F32, kind="ExternalInput")
    d_wfT = nc.dram_tensor("wffn_til", [128, 8, 32 * 512], F32, kind="ExternalInput")
    d_fb = nc.dram_tensor("ffnb_row", [1, D], F32, kind="ExternalInput")
    d_laT = nc.dram_tensor("loraA_til", [128, 32 * 128], F32, kind="ExternalInput")
    d_lbT = nc.dram_tensor("loraBT", [128, D], F32, kind="ExternalInput")
    d_lw = nc.dram_tensor("lwrep", [128, 1], F32, kind="ExternalInput")
    d_id = nc.dram_tensor("ident", [128, 128], F32, kind="ExternalInput")
    d_ones = nc.dram_tensor("ones_row", [1, 128], F32, kind="ExternalInput")
    d_onec = nc.dram_tensor("onecol_bf", [128, 1], BF16, kind="ExternalInput")

    d_kvF = nc.dram_tensor("kvF_dram", [576, S], BF16)
    d_qcF = nc.dram_tensor("qcF_dram", [2, 128, 12 * 512], BF16)
    d_x2 = nc.dram_tensor("x2_dram", [8, TOK, 512], F32)
    d_out = nc.dram_tensor("out_c", [TOK, D], F32, kind="ExternalOutput")

    dbg = {}
    if debug >= 1:
        dbg["kvF"] = nc.dram_tensor("dbg_kvF", [576, S], BF16, kind="ExternalOutput")
        dbg["qcF"] = nc.dram_tensor("dbg_qcF", [2, 128, 12 * 512], BF16, kind="ExternalOutput")
    if debug >= 2:
        dbg["K0"] = nc.dram_tensor("dbg_K0", [NOPE, S], BF16, kind="ExternalOutput")
        dbg["K1"] = nc.dram_tensor("dbg_K1", [NOPE, S], BF16, kind="ExternalOutput")
        dbg["V0"] = nc.dram_tensor("dbg_V0", [128, 64 * 256], BF16, kind="ExternalOutput")
        dbg["Q0"] = nc.dram_tensor("dbg_Q0", [128, 3 * TOK], BF16, kind="ExternalOutput")
        dbg["osum"] = nc.dram_tensor("dbg_osum", [VH, TOK], F32, kind="ExternalOutput")
    if debug >= 3:
        dbg["x2"] = nc.dram_tensor("dbg_x2", [8, TOK, 512], F32, kind="ExternalOutput")

    sched = Sched()
    em, wt = sched.emit, sched.wait

    with ExitStack() as ctx:
        E = ctx.enter_context

        sb_id = E(nc.sbuf_tensor("sb_id", [128, 128], F32))
        sb_ones = E(nc.sbuf_tensor("sb_ones", [1, 128], F32))
        sb_onec = E(nc.sbuf_tensor("sb_onec", [128, 1], BF16))
        sb_bqc = E(nc.sbuf_tensor("sb_bqc", [128, 12], F32))
        sb_bqup = E(nc.sbuf_tensor("sb_bqup", [128, 48], F32))
        sb_bkvc = E(nc.sbuf_tensor("sb_bkvc", [128, 5], F32))
        sb_bkK = E(nc.sbuf_tensor("sb_bkK", [128, 32], F32))
        sb_bkV = E(nc.sbuf_tensor("sb_bkV", [128, 32], F32))
        sb_lw = E(nc.sbuf_tensor("sb_lw", [128, 1], F32))
        sb_coef = E(nc.sbuf_tensor("sb_coef", [128, 1], F32))
        sb_bvsum = E(nc.sbuf_tensor("sb_bvsum", [128, 1], F32))
        st_sum = E(nc.sbuf_tensor("st_sum", [128, 2], F32))
        st_sqs = E(nc.sbuf_tensor("st_sqs", [128, 2], F32))
        st_nm = E(nc.sbuf_tensor("st_nm", [128, 2], F32))
        st_m2 = E(nc.sbuf_tensor("st_m2", [128, 2], F32))
        st_var = E(nc.sbuf_tensor("st_var", [128, 2], F32))
        st_sq = E(nc.sbuf_tensor("st_sq", [128, 2], F32))
        st_rs = E(nc.sbuf_tensor("st_rs", [128, 2], F32))
        sb_stab = E(nc.sbuf_tensor("sb_stab", [128, 8], F32))
        sb_qmax = E(nc.sbuf_tensor("sb_qmax", [128, 1], F32))
        sb_kmax = E(nc.sbuf_tensor("sb_kmax", [128, 1], F32))
        sb_qm = E(nc.sbuf_tensor("sb_qm", [1, 1], F32))
        sb_km = E(nc.sbuf_tensor("sb_km", [1, 1], F32))
        sb_cqk = E(nc.sbuf_tensor("sb_cqk", [1, 1], F32))
        sb_negc_col = E(nc.sbuf_tensor("sb_negc_col", [128, 1], F32))
        sb_recip0 = E(nc.sbuf_tensor("sb_recip0", [1, 512], F32))
        sb_recip1 = E(nc.sbuf_tensor("sb_recip1", [1, 512], F32))

        AR_N = 51456
        OS_BASE = AR_N - 1536
        arena = E(nc.sbuf_tensor("arena", [128, AR_N], F32))

        class Carver:
            def __init__(self, base=0, limit=AR_N):
                self.off = base
                self.limit = limit

            def take(self, nfree, dtype):
                ef32 = nfree if dtype == F32 else (nfree + 1) // 2
                ef32 = (ef32 + 15) & ~15
                assert self.off + ef32 <= self.limit, \
                    f"arena overflow {self.off}+{ef32} > {self.limit}"
                ap = arena[:, self.off:self.off + ef32]
                self.off += ef32
                if dtype == BF16:
                    ap = ap.bitcast(BF16)[:, 0:nfree]
                return ap

        ps = E(nc.psum_tensor("ps", [128, 4096], F32))

        def bank(b):
            return ps[:, b * 512:(b + 1) * 512]

        bankfree = [None] * 8

        sm_gc = sched.sem("gc", E(nc.semaphore("sm_gc")))
        sm_t = sched.sem("t", E(nc.semaphore("sm_t")))
        sm_v = sched.sem("v", E(nc.semaphore("sm_v")))
        sm_s = sched.sem("s", E(nc.semaphore("sm_s")))
        dma_sems = {}
        for nm in ("c", "x0", "x1", "x2b", "wkvc", "wq0", "wq1", "kv0", "kv1",
                   "qc0", "qc1", "kvn", "qch0", "qch1", "wkK", "wkV", "wqup",
                   "cw", "cwy", "wf0", "wf1", "x2w", "x2r0", "x2r1", "o0", "o1", "st"):
            dma_sems[nm] = sched.sem(nm, E(nc.semaphore("sm_" + nm)))
        sm_st = dma_sems["st"]

        def dma(e, out, in_, sem):
            if isinstance(sem, str):
                sem = dma_sems[sem]
            return em(e, lambda eng, o=out, i=in_: eng.dma_start(out=o, in_=i), sem, 16)

        tok_consts = None
        for (dst, src) in [
            (sb_id[:, :], d_id.ap()), (sb_ones[:, :], d_ones.ap()),
            (sb_onec[:, :], d_onec.ap()), (sb_bqc[:, :], d_bqc.ap()),
            (sb_bqup[:, :], d_bqup.ap()), (sb_bkvc[:, :], d_bkvc.ap()),
            (sb_bkK[:, :], d_bkK.ap()), (sb_bkV[:, :], d_bkV.ap()),
            (sb_lw[:, :], d_lw.ap()),
        ]:
            tok_consts = dma("y", dst, src, "c")
        wt("v", tok_consts)
        em("v", lambda v: v.tensor_scalar(
            sb_coef[:, :], sb_lw[:, :], scalar1=2.0, scalar2=None, op0=ALU.mult), sm_v)
        tok_bvsum = em("v", lambda v: v.reduce_sum(sb_bvsum[:, :], sb_bkV[:, :], AX.X), sm_v)

        ln_rot = [0]
        last_sq_tok = [None]
        slot_sum_free = [None, None]
        slot_sqs_free = [None, None]
        slot_sq_free = [None, None]

        def emit_ln_stats(xbuf, xsqbuf, tok_loaded_s, extra_s_wait=None):
            r = ln_rot[0] % 2
            ln_rot[0] += 1
            sl = slice(r, r + 1)
            wt("s", tok_loaded_s)
            if extra_s_wait is not None:
                wt("s", extra_s_wait)
            wt("s", slot_sum_free[r])
            wt("s", last_sq_tok[0])
            t1 = em("s", lambda s, sl=sl, xb=xbuf, xo=xsqbuf: s.activation(
                xo, xb, AF.Copy, accum_out=st_sum[:, sl]), sm_s)
            wt("s", slot_sqs_free[r])
            wt("s", t1)
            t2 = em("s", lambda s, sl=sl, xb=xbuf, xq=xsqbuf: s.activation(
                xq, xb, AF.Square, accum_out=st_sqs[:, sl]), sm_s)
            last_sq_tok[0] = t2
            return r, t1, t2

        def emit_ln_finish(xbuf, xnbuf, r, t1, t2, tok_loaded_v, extra_v_wait=None):
            sl = slice(r, r + 1)
            if extra_v_wait is not None:
                wt("v", extra_v_wait)
            wt("v", t1)
            t3 = em("v", lambda v, sl=sl: v.tensor_scalar(
                st_nm[:, sl], st_sum[:, sl], scalar1=-1.0 / D, scalar2=None, op0=ALU.mult), sm_v)
            slot_sum_free[r] = t3
            wt("v", t3)
            t4 = em("v", lambda v, sl=sl: v.tensor_tensor(
                st_m2[:, sl], st_nm[:, sl], st_nm[:, sl], ALU.mult), sm_v)
            wt("v", t2)
            wt("v", t4)
            t5 = em("v", lambda v, sl=sl: v.tensor_scalar(
                st_var[:, sl], st_sqs[:, sl], scalar1=1.0 / D, scalar2=st_m2[:, sl],
                op0=ALU.mult, op1=ALU.subtract), sm_v)
            slot_sqs_free[r] = t5
            wt("v", t5)
            t6 = em("v", lambda v, sl=sl: v.tensor_scalar(
                st_var[:, sl], st_var[:, sl], scalar1=EPS, scalar2=None, op0=ALU.add), sm_v)
            wt("s", t6)
            wt("s", slot_sq_free[r])
            t7 = em("s", lambda s, sl=sl: s.activation(st_sq[:, sl], st_var[:, sl], AF.Sqrt), sm_s)
            wt("v", t7)
            t8 = em("v", lambda v, sl=sl: v.reciprocal(st_rs[:, sl], st_sq[:, sl]), sm_v)
            slot_sq_free[r] = t8
            wt("v", t8)
            wt("v", tok_loaded_v)
            t9 = em("v", lambda v, sl=sl, xb=xbuf, xo=xnbuf: v.tensor_scalar(
                xo, xb, scalar1=st_nm[:, sl], scalar2=st_rs[:, sl],
                op0=ALU.add, op1=ALU.mult), sm_v)
            return t9

        def emit_ln(xbuf, xnbuf, xsqbuf, tok_loaded_v, tok_loaded_s,
                    extra_v_wait=None, extra_s_wait=None):
            r, t1, t2 = emit_ln_stats(xbuf, xsqbuf, tok_loaded_s, extra_s_wait)
            return emit_ln_finish(xbuf, xnbuf, r, t1, t2, tok_loaded_v, extra_v_wait)

        A = Carver()
        a_x = [A.take(D, F32) for _ in range(2)]
        a_xsq = A.take(D, F32)
        a_xn = [A.take(D, F32) for _ in range(2)]
        a_xnF = [[A.take(512, BF16) for _ in range(32)] for _ in range(2)]
        a_wkvc = A.take(32 * 576, BF16).rearrange("p (c f) -> p c f", c=32)
        a_wqct = [A.take(2 * QL, BF16).rearrange("p (c f) -> p c f", c=2) for _ in range(2)]
        a_kvst = [A.take(512, BF16) for _ in range(2)]
        a_qcst = [A.take(512, BF16) for _ in range(2)]

        tok_wkvc = dma("g", a_wkvc, d_wkvc.ap().rearrange("p (c f) -> p c f", c=32), "wkvc")

        tok_x_free = [None, None]
        tok_xld = [None, None]
        stats_tok = [None, None]
        tok_xn_cons = [None, None]
        tok_xnF_ready = [[None] * 32, [None] * 32]
        tok_set_free = [None, None]
        tok_kv_last = [None] * 5
        tok_kvst_free = [None, None]
        tok_qcst_free = [None, None]
        qc_buf_free = [None, None]
        tok_qc_last = [None] * 6

        def gen_kv_mms(tb):
            st = tb % 2
            if tb == 0:
                wt("t", tok_wkvc)
            for dc in range(32):
                wt("t", tok_xnF_ready[st][dc])
            for f in range(5):
                rows = 128 if f < 4 else 64
                wt("t", bankfree[f])
                for dc in range(32):
                    tok_kv_last[f] = em("t", lambda t, f=f, dc=dc, rows=rows, st=st: t.matmul(
                        bank(f)[0:rows, 0:512],
                        a_wkvc[:, dc, f * 128:f * 128 + rows],
                        a_xnF[st][dc],
                        start=(dc == 0), stop=(dc == 31)),
                        sm_t if dc == 31 else None)
                    yield

        def flush_kv(tb):
            for f in range(5):
                rows = 128 if f < 4 else 64
                kb = f % 2
                wt("v", tok_kv_last[f])
                wt("v", tok_kvst_free[kb])
                tok_ev = em("v", lambda v, f=f, kb=kb, rows=rows: v.tensor_scalar(
                    a_kvst[kb][0:rows, :], bank(f)[0:rows, 0:512],
                    scalar1=sb_bkvc[0:rows, f:f + 1], scalar2=None, op0=ALU.add), sm_v)
                bankfree[f] = tok_ev
                wt("y", tok_ev)
                tok_st = dma("y", bass.AP(d_kvF, f * 128 * S + tb * 512,
                                          [[S, rows], [1, 512]]),
                             a_kvst[kb][0:rows, :], "kv0" if kb == 0 else "kv1")
                if debug >= 1:
                    tok_st = dma("y", bass.AP(dbg["kvF"], f * 128 * S + tb * 512,
                                              [[S, rows], [1, 512]]),
                                 a_kvst[kb][0:rows, :], "kv0" if kb == 0 else "kv1")
                tok_kvst_free[kb] = tok_st

        kv_gen = None
        for tb in range(16):
            st = tb % 2
            for tt in range(4):
                tile = tb * 4 + tt
                bj = tile % 2
                if tile == 0:
                    tok_xld[0] = dma("y", a_x[0], d_x[0:128, :], "x0")
                    stats_tok[0] = emit_ln_stats(a_x[0], a_xsq, tok_xld[0])
                if tile + 1 < 64:
                    t1_ = tile + 1
                    wt("y", tok_x_free[t1_ % 2])
                    tok_xld[t1_ % 2] = dma("y", a_x[t1_ % 2],
                                           d_x[t1_ * 128:(t1_ + 1) * 128, :],
                                           "x0" if t1_ % 2 == 0 else "x1")
                    stats_tok[t1_ % 2] = emit_ln_stats(a_x[t1_ % 2], a_xsq,
                                                       tok_xld[t1_ % 2])
                r_, ts1, ts2 = stats_tok[tile % 2]
                tok_xn = emit_ln_finish(a_x[bj], a_xn[bj], r_, ts1, ts2,
                                        tok_xld[bj], extra_v_wait=tok_xn_cons[bj])
                tok_x_free[bj] = tok_xn
                wt("t", tok_xn)
                last_tp = None
                for dc in range(32):
                    b = 5 + (dc % 2)
                    wt("t", bankfree[b])
                    tok_tp = em("t", lambda t, b=b, xb=a_xn[bj], dc=dc: t.transpose(
                        bank(b)[:, 0:128], xb[:, dc * 128:dc * 128 + 128], sb_id[:, :]), sm_t)
                    last_tp = tok_tp
                    if kv_gen is not None:
                        next(kv_gen, None)
                        next(kv_gen, None)
                    eng = "s" if dc % 2 == 0 else "v"
                    wt(eng, tok_tp)
                    if tt == 0 and dc == 0:
                        wt("s", tok_set_free[st])
                        wt("v", tok_set_free[st])
                    if eng == "s":
                        tok_ev = em(eng, lambda e, b=b, dc=dc, tt=tt, st=st: e.copy(
                            a_xnF[st][dc][:, tt * 128:tt * 128 + 128], bank(b)[:, 0:128]), sm_s)
                    else:
                        tok_ev = em(eng, lambda e, b=b, dc=dc, tt=tt, st=st: e.tensor_copy(
                            a_xnF[st][dc][:, tt * 128:tt * 128 + 128], bank(b)[:, 0:128]), sm_v)
                    bankfree[b] = tok_ev
                    if tt == 3:
                        tok_xnF_ready[st][dc] = tok_ev
                tok_xn_cons[bj] = last_tp
            if kv_gen is not None:
                for _ in kv_gen:
                    pass
                flush_kv(tb - 1)
                tok_set_free[(tb - 1) % 2] = tok_kv_last[4]
            if tb < 2:
                for dc in range(32):
                    wt("t", tok_xnF_ready[st][dc])
                for pas in range(2):
                    tok_mm = None
                    for dcg in range(16):
                        gb = dcg % 2
                        wt("g", qc_buf_free[gb])
                        tok_w = dma("g", a_wqct[gb],
                                    bass.AP(d_wqcT, dcg * 2 * 128 * QL,
                                            [[QL, 128], [128 * QL, 2], [1, QL]]),
                                    "wq0" if gb == 0 else "wq1")
                        wt("t", tok_w)
                        for di in range(2):
                            dc = dcg * 2 + di
                            for f6 in range(6):
                                fg = pas * 6 + f6
                                if dc == 0:
                                    wt("t", bankfree[f6])
                                tok_mm = em("t", lambda t, f6=f6, fg=fg, gb=gb, dc=dc, di=di, st=st: t.matmul(
                                    bank(f6)[:, 0:512],
                                    a_wqct[gb][:, di, fg * 128:fg * 128 + 128],
                                    a_xnF[st][dc],
                                    start=(dc == 0), stop=(dc == 31)),
                                    sm_t if (f6 == 5 or dc == 31) else None)
                                if dc == 31:
                                    tok_qc_last[f6] = tok_mm
                        qc_buf_free[gb] = tok_mm
                    for f6 in range(6):
                        fg = pas * 6 + f6
                        qb_ = f6 % 2
                        wt("v", tok_qc_last[f6])
                        wt("v", tok_qcst_free[qb_])
                        tok_ev = em("v", lambda v, f6=f6, fg=fg, qb_=qb_: v.tensor_scalar(
                            a_qcst[qb_], bank(f6)[:, 0:512],
                            scalar1=sb_bqc[:, fg:fg + 1], scalar2=None, op0=ALU.add), sm_v)
                        bankfree[f6] = tok_ev
                        wt("y", tok_ev)
                        tok_st = dma("y", bass.AP(d_qcF, tb * (128 * 12 * 512) + fg * 512,
                                                  [[12 * 512, 128], [1, 512]]),
                                     a_qcst[qb_], "qc0" if qb_ == 0 else "qc1")
                        if debug >= 1:
                            tok_st = dma("y", bass.AP(dbg["qcF"], tb * (128 * 12 * 512) + fg * 512,
                                                      [[12 * 512, 128], [1, 512]]),
                                         a_qcst[qb_], "qc0" if qb_ == 0 else "qc1")
                        tok_qcst_free[qb_] = tok_st
            kv_gen = gen_kv_mms(tb)
        for _ in kv_gen:
            pass
        flush_kv(15)
        tok_set_free[15 % 2] = tok_kv_last[4]

        run_b = _PHASE >= 2
        run_c = _PHASE >= 3
        sched.barrier()
        B = Carver(0, OS_BASE)
        b_kvn = [B.take(S, BF16) for _ in range(4)]
        b_rope = B.take(S, BF16)
        b_K = [B.take(S, BF16) for _ in range(2)]
        b_V = B.take(64 * 256, BF16)
        b_Q = [B.take(TOK, BF16) for _ in range(3)]
        b_P = [[B.take(512, BF16) for _ in range(3)] for _ in range(2)]
        b_qch = B.take(12 * 512, BF16).rearrange("p (c f) -> p c f", c=12)
        b_wqup = B.take(12 * 384, BF16).rearrange("p (c f) -> p c f", c=12)
        b_wkK = B.take(4 * 256, BF16).rearrange("p (c f) -> p c f", c=4)
        b_wkV = B.take(4 * 256, BF16).rearrange("p (c f) -> p c f", c=4)
        b_bcast = B.take(512, F32)
        b_otmp = B.take(512, F32)
        b_Ps = [[B.take(512, BF16) for _ in range(2)] for _ in range(2)]
        b_osum = arena[:, OS_BASE:OS_BASE + 1024]
        b_osbf = arena[:, OS_BASE + 1024:OS_BASE + 1536].bitcast(BF16)

        for lat in range(4):
            dma("y", b_kvn[lat],
                bass.AP(d_kvF, lat * 128 * S, [[S, 128], [1, S]]), "kvn")
        dma("y", b_rope[0:64, :], bass.AP(d_kvF, 512 * S, [[S, 64], [1, S]]), "kvn")
        tok_kvn_all = dma("y", b_rope[64:128, :],
                          bass.AP(d_kvF, 512 * S, [[S, 64], [1, S]]), "kvn")
        tok_rope_l = tok_rope_h = tok_kvn_all
        tok_osum0 = em("v", lambda v: v.memset(b_osum, 0.0), sm_v)

        wt("v", tok_rope_h)
        wt("v", tok_rope_l)
        tok_ropemax = em("v", lambda v: v.tensor_reduce(
            sb_stab[:, 5:6], b_rope, AX.X, ALU.max, apply_absolute_value=True), sm_v)

        tok_w_free = {"q": None, "k": None, "v": None}
        tok_qch_free = [None, None]
        tok_K_free = [None, None]
        tok_V_free = None
        tok_Q_free = None
        tok_osum_last = tok_osum0
        tok_exp_prev_pair = None
        tok_gq_prev = None
        tok_gk_prev = None
        tok_tc1_prev = None
        tok_bcast_prev = None

        pend = [None]
        texp = [(None, None)] * 3
        tps_tok = [[None, None], [None, None]]
        tps_free = [[None, None], [None, None]]
        p_read = [[None] * 3, [None] * 3]
        tok_bc_prev = [None, None]
        tok_osum_chain = [tok_osum0]
        flush_fn = [lambda: None]

        def flush_epi_outer():
            flush_fn[0]()
        npair_run = min(_NPAIR_RUN, NPAIR) if run_b else 0
        for p in range(npair_run):
            wt("g", tok_w_free["k"])
            tok_wk = dma("g", b_wkK, bass.AP(d_wkK, p * (4 * 256),
                                             [[NPAIR * 4 * 256, 128], [256, 4], [1, 256]]), "wkK")
            wt("g", tok_w_free["v"])
            tok_wv = dma("g", b_wkV, bass.AP(d_wkV, p * (4 * 256),
                                             [[NPAIR * 4 * 256, 128], [256, 4], [1, 256]]), "wkV")
            wt("g", tok_w_free["q"])
            tok_wq = dma("g", b_wqup, bass.AP(d_wqup, p * (12 * 384),
                                              [[NPAIR * 12 * 384, 128], [384, 12], [1, 384]]), "wqup")

            prod_rot = [0]

            def prod_bank():
                b = prod_rot[0] % 4
                prod_rot[0] += 1
                return b

            wt("t", tok_wq)
            wt("v", tok_Q_free)
            tok_Q_ready = None
            for half in range(2):
                wt("y", tok_qch_free[1] if half == 0 else tok_qch_free[0])
                tok_qc = dma("y", b_qch, bass.AP(d_qcF, half * (128 * 12 * 512),
                                                 [[12 * 512, 128], [512, 12], [1, 512]]),
                             "qch0" if half == 0 else "qch1")
                wt("t", tok_qc)
                for qch in range(3):
                    b = prod_bank()
                    wt("t", bankfree[b])
                    tok_mm = None
                    for ql in range(12):
                        tok_mm = em("t", lambda t, b=b, qch=qch, ql=ql: t.matmul(
                            bank(b),
                            b_wqup[:, ql, qch * 128:qch * 128 + 128],
                            b_qch[:, ql, :],
                            start=(ql == 0), stop=(ql == 11)),
                            sm_t if ql == 11 else None)
                    tok_qch_free[half] = tok_mm
                    wt("v", tok_mm)
                    tok_Q_ready = em("v", lambda v, b=b, qch=qch, half=half, p=p: v.tensor_scalar(
                        b_Q[qch][:, half * 512:half * 512 + 512], bank(b),
                        scalar1=sb_bqup[:, 3 * p + qch:3 * p + qch + 1], scalar2=ISQ,
                        op0=ALU.add, op1=ALU.mult), sm_v)
                    bankfree[b] = tok_Q_ready

            wt("v", tok_Q_ready)
            wt("v", tok_gq_prev)
            tql = None
            for qch in range(3):
                tql = em("v", lambda v, qch=qch: v.tensor_reduce(
                    sb_stab[:, qch:qch + 1], b_Q[qch], AX.X, ALU.max,
                    apply_absolute_value=True), sm_v)
            wt("v", tql)
            tqm = em("v", lambda v: v.tensor_reduce(
                sb_qmax[:, :], sb_stab[:, 0:3], AX.X, ALU.max), sm_v)
            wt("t", tok_wk)
            if p == 0:
                wt("t", tok_kvn_all)
            tok_K_ready = [None, None]
            for h in range(2):
                wt("v", tok_K_free[h])
                for tk in range(16):
                    b = prod_bank()
                    wt("t", bankfree[b])
                    tok_mm = None
                    for lat in range(4):
                        tok_mm = em("t", lambda t, b=b, h=h, tk=tk, lat=lat: t.matmul(
                            bank(b),
                            b_wkK[:, lat, h * 128:h * 128 + 128],
                            b_kvn[lat][:, tk * 512:tk * 512 + 512],
                            start=(lat == 0), stop=(lat == 3)),
                            sm_t if lat == 3 else None)
                    wt("v", tok_mm)
                    tok_ev = em("v", lambda v, b=b, h=h, tk=tk, p=p: v.tensor_scalar(
                        b_K[h][:, tk * 512:tk * 512 + 512], bank(b),
                        scalar1=sb_bkK[:, 2 * p + h:2 * p + h + 1], scalar2=None,
                        op0=ALU.add), sm_v)
                    bankfree[b] = tok_ev
                tok_K_ready[h] = tok_ev

            wt("v", tok_K_ready[0])
            wt("v", tok_K_ready[1])
            wt("v", tok_gk_prev)
            em("v", lambda v: v.tensor_reduce(
                sb_stab[:, 3:4], b_K[0], AX.X, ALU.max, apply_absolute_value=True), sm_v)
            tk1 = em("v", lambda v: v.tensor_reduce(
                sb_stab[:, 4:5], b_K[1], AX.X, ALU.max, apply_absolute_value=True), sm_v)
            wt("v", tk1)
            wt("v", tok_ropemax)
            tkm = em("v", lambda v: v.tensor_reduce(
                sb_kmax[:, :], sb_stab[:, 3:6], AX.X, ALU.max), sm_v)
            wt("t", tok_wv)
            wt("v", tok_V_free)
            tok_V_ready = None
            for tk in range(64):
                b = prod_bank()
                wt("t", bankfree[b])
                tok_mm = None
                for lat in range(4):
                    tok_mm = em("t", lambda t, b=b, tk=tk, lat=lat: t.matmul(
                        bank(b)[:, 0:256],
                        b_kvn[lat][:, tk * 128:tk * 128 + 128],
                        b_wkV[:, lat, :],
                        start=(lat == 0), stop=(lat == 3)),
                        sm_t if lat == 3 else None)
                wt("v", tok_mm)
                tok_V_ready = em("v", lambda v, b=b, tk=tk: v.tensor_copy(
                    b_V[:, tk * 256:tk * 256 + 256], bank(b)[:, 0:256]), sm_v)
                bankfree[b] = tok_V_ready

            wt("g", tqm)
            wt("g", tkm)
            wt("g", tok_tc1_prev)
            tok_gq_prev = em("g", lambda g: g.tensor_reduce(
                sb_qm[:, :], sb_qmax[:, :], AX.C, ALU.max), sm_gc)
            tok_gk_prev = em("g", lambda g: g.tensor_reduce(
                sb_km[:, :], sb_kmax[:, :], AX.C, ALU.max), sm_gc)
            wt("v", tok_gq_prev)
            wt("v", tok_gk_prev)
            tok_tc1_prev = em("v", lambda v: v.tensor_tensor(
                sb_cqk[:, :], sb_qm[:, :], sb_km[:, :], ALU.mult), sm_v)
            wt("v", tok_tc1_prev)
            tc2 = em("v", lambda v: v.tensor_scalar(
                sb_cqk[:, :], sb_cqk[:, :], scalar1=-float(QH), scalar2=-45.0,
                op0=ALU.mult, op1=ALU.max), sm_v)
            flush_epi_outer()
            wt("t", tc2)
            wt("t", bankfree[4])
            tbc = em("t", lambda t: t.matmul(
                bank(4)[:, 0:1], sb_ones[:, :], sb_cqk[:, :], start=True, stop=True), sm_t)
            wt("v", tbc)
            wt("v", tok_exp_prev_pair)
            tnc = em("v", lambda v: v.tensor_copy(sb_negc_col[:, :], bank(4)[:, 0:1]), sm_v)
            bankfree[4] = tnc
            wt("t", tok_V_ready)
            wt("s", tnc)
            tok_score_last = None

            def emit_scores(qb, kc):
                par = kc % 2
                b0, b1 = par, 2 + par
                wt("t", bankfree[b0])
                wt("t", bankfree[b1])
                em("t", lambda t, b0=b0, kc=kc, qb=qb: t.matmul(
                    bank(b0), b_K[0][:, kc * 128:kc * 128 + 128],
                    b_Q[0][:, qb * 512:qb * 512 + 512], start=True, stop=False))
                em("t", lambda t, b1=b1, kc=kc, qb=qb: t.matmul(
                    bank(b1), b_K[1][:, kc * 128:kc * 128 + 128],
                    b_Q[1][:, qb * 512:qb * 512 + 512], start=True, stop=False))
                em("t", lambda t, b0=b0, kc=kc, qb=qb: t.matmul(
                    bank(b0), b_rope[0:64, kc * 128:kc * 128 + 128],
                    b_Q[2][0:64, qb * 512:qb * 512 + 512],
                    start=False, stop=True, tile_position=(0, 0)))
                tok_sc = em("t", lambda t, b1=b1, kc=kc, qb=qb: t.matmul(
                    bank(b1), b_rope[64:128, kc * 128:kc * 128 + 128],
                    b_Q[2][64:128, qb * 512:qb * 512 + 512],
                    start=False, stop=True, tile_position=(64, 0)), sm_t)
                wt("s", tok_sc)
                wt("s", p_read[0][kc % 3])
                wt("s", p_read[1][kc % 3])
                te0 = em("s", lambda s, b0=b0, kc=kc: s.activation(
                    b_P[0][kc % 3], bank(b0), AF.Exp, bias=sb_negc_col[:, :]), sm_s)
                te1 = em("s", lambda s, b1=b1, kc=kc: s.activation(
                    b_P[1][kc % 3], bank(b1), AF.Exp, bias=sb_negc_col[:, :]), sm_s)
                bankfree[b0] = te0
                bankfree[b1] = te1
                texp[kc % 3] = (te0, te1)
                if kc % 2 == 1:
                    for h, te in ((0, te0), (1, te1)):
                        wt("v", te)
                        wt("v", tps_free[h][(kc // 2) % 2])
                        tok_add = em("v", lambda v, h=h, kc=kc: v.tensor_tensor(
                            b_Ps[h][(kc // 2) % 2], b_P[h][(kc - 1) % 3],
                            b_P[h][kc % 3], ALU.add), sm_v)
                        tps_tok[h][(kc // 2) % 2] = tok_add
                        p_read[h][(kc - 1) % 3] = tok_add
                        p_read[h][kc % 3] = tok_add
                return tok_sc, te1

            def emit_pv(qb, kc, epi_banks_wait):
                if epi_banks_wait:
                    for ob in (4, 5):
                        wt("t", bankfree[ob])
                wt("t", texp[kc % 3][0])
                wt("t", texp[kc % 3][1])
                em("t", lambda t, kc=kc: t.matmul(
                    bank(4), b_V[:, kc * 256:kc * 256 + 128], b_P[0][kc % 3],
                    start=(kc == 0), stop=(kc == 63)))
                tok = em("t", lambda t, kc=kc: t.matmul(
                    bank(5), b_V[:, kc * 256 + 128:kc * 256 + 256], b_P[1][kc % 3],
                    start=(kc == 0), stop=(kc == 63)),
                    sm_t if kc == 63 else None)
                if kc % 2 == 1:
                    j = (kc // 2) % 2
                    if kc == 1:
                        for ob in (6, 7):
                            wt("t", bankfree[ob])
                    for h in range(2):
                        wt("t", tps_tok[h][j])
                        tok = em("t", lambda t, kc=kc, h=h, j=j: t.matmul(
                            bank(6 + h)[0:1, :], sb_onec[:, :], b_Ps[h][j],
                            start=(kc == 1), stop=(kc == 63)),
                            sm_t if kc == 63 else None)
                        tps_free[h][j] = tok if kc == 63 else None
                return tok

            def emit_epi_v1(qb, tok_pv):
                trs = []
                for h in range(2):
                    wt("v", tok_pv)
                    wt("v", tok_bc_prev[h])
                    trs.append(em("v", lambda v, h=h: v.reciprocal(
                        (sb_recip0 if h == 0 else sb_recip1)[0:1, :],
                        bank(6 + h)[0:1, 0:512]), sm_v))
                    bankfree[6 + h] = trs[h]
                return trs

            def flush_epi():
                if not pend[0]:
                    return
                qb, trs = pend[0]
                pend[0] = None
                for h in range(2):
                    bcb = 1 + 2 * h
                    wt("t", trs[h])
                    wt("t", bankfree[bcb])
                    tok_bc = em("t", lambda t, bcb=bcb, h=h: t.matmul(
                        bank(bcb), sb_ones[:, :],
                        (sb_recip0 if h == 0 else sb_recip1)[0:1, :],
                        start=True, stop=True), sm_t)
                    tok_bc_prev[h] = tok_bc
                    wt("v", tok_bc)
                    tcp = em("v", lambda v, bcb=bcb: v.tensor_copy(
                        b_bcast, bank(bcb)), sm_v)
                    bankfree[bcb] = tcp
                    wt("v", tcp)
                    tmu = em("v", lambda v, h=h: v.tensor_tensor(
                        b_otmp, bank(4 + h), b_bcast, ALU.mult), sm_v)
                    bankfree[4 + h] = tmu
                    wt("v", tmu)
                    wt("v", tok_osum_chain[0])
                    tok_osum_chain[0] = em("v", lambda v, qb=qb: v.tensor_tensor(
                        b_osum[:, qb * 512:qb * 512 + 512],
                        b_osum[:, qb * 512:qb * 512 + 512], b_otmp, ALU.add), sm_v)

            flush_fn[0] = flush_epi
            for qb in range(2):
                tok_pv = None
                for kc in range(64):
                    tok_sc, te1 = emit_scores(qb, kc)
                    tok_score_last = tok_sc
                    tok_exp_prev_pair = te1
                    if kc == 2:
                        flush_epi()
                    if kc >= 2:
                        tok_pv = emit_pv(qb, kc - 2, epi_banks_wait=(kc == 2))
                tok_pv = emit_pv(qb, 62, False)
                tok_pv = emit_pv(qb, 63, False)
                tok_V_free = tok_pv
                trs = emit_epi_v1(qb, tok_pv)
                pend[0] = (qb, trs)
            tok_K_free = [tok_score_last, tok_score_last]
            tok_Q_free = tok_score_last
            tok_w_free = {"q": tok_Q_ready, "k": tok_K_ready[1], "v": tok_V_ready}

        flush_fn[0]()
        wt("v", tok_bvsum)
        wt("v", tok_osum_chain[0])
        tb1 = em("v", lambda v: v.tensor_scalar(
            b_osum, b_osum, scalar1=sb_bvsum[:, :], scalar2=None, op0=ALU.add), sm_v)
        wt("v", tb1)
        tok_osbf = em("v", lambda v: v.tensor_copy(b_osbf, b_osum), sm_v)

        if debug >= 2:
            wt("y", tok_osbf)
            dma("y", dbg["K0"].ap(), b_K[0], "cwy")
            dma("y", dbg["K1"].ap(), b_K[1], "cwy")
            dma("y", dbg["V0"].ap(), b_V, "cwy")
            for qch in range(3):
                dma("y", bass.AP(dbg["Q0"], qch * TOK, [[3 * TOK, 128], [1, TOK]]), b_Q[qch], "cwy")
            dma("y", dbg["osum"].ap(), b_osum, "cwy")

        def _phase_c():
            sched.barrier()
            C = Carver(0, OS_BASE)
            c_xn2F = [C.take(TOK, BF16) for _ in range(32)]
            c_rows = C.take(D, F32)
            c_la = C.take(32 * 128, BF16).rearrange("p (c f) -> p c f", c=32)
            c_lb = C.take(D, BF16)
            c_hT = C.take(TOK, BF16)
            c_h = C.take(128, F32)
            c_outs = [C.take(512, F32) for _ in range(2)]
            c_woT = C.take(D, BF16)
            c_biasbc = C.take(D, F32)
            C_MAIN_BASE = C.off
            c_x = C.take(D, F32)
            c_x2 = C.take(D, F32)
            c_xn2 = C.take(D, F32)

            dma("g", c_la, d_laT.ap().rearrange("p (c f) -> p c f", c=32), "cw")
            dma("g", c_lb, d_lbT.ap(), "cw")
            tok_woT = dma("g", c_woT, d_woT.ap(), "cw")
            tok_la = tok_lb = tok_woT
            tok_borow = dma("y", c_rows[0:1, :], d_bo.ap(), "cwy")

            tok_bo_bc = None
            wt("t", tok_borow)
            for ob in range(8):
                bb = 6 + (ob % 2)
                wt("t", bankfree[bb])
                tm = em("t", lambda t, bb=bb, ob=ob: t.matmul(
                    bank(bb), sb_ones[:, :], c_rows[0:1, ob * 512:(ob + 1) * 512],
                    start=True, stop=True), sm_t)
                tok_bcast_mm_last = tm
                wt("v", tm)
                tok_bo_bc = em("v", lambda v, bb=bb, ob=ob: v.tensor_copy(
                    c_biasbc[:, ob * 512:(ob + 1) * 512], bank(bb)), sm_v)
                bankfree[bb] = tok_bo_bc

            tok_x_used = None
            tok_x2_dma_last = None
            tok_xn2_cons = None
            tok_x2w = []
            tok_evS_last = None
            tok_evV_last = None
            tok_op_last = None
            for tt in range(8):
                wt("y", tok_x_used)
                tok_xl = dma("y", c_x, d_x[tt * 128:(tt + 1) * 128, :], "x0")
                wt("t", tok_osbf)
                wt("t", tok_woT)
                for ob in range(8):
                    wt("t", bankfree[ob])
                    tm = em("t", lambda t, tt=tt, ob=ob: t.matmul(
                        bank(ob), b_osbf[:, tt * 128:tt * 128 + 128],
                        c_woT[:, ob * 512:(ob + 1) * 512], start=True, stop=True),
                        sm_t if ob == 7 else None)
                tok_op_last = tm
                wt("v", tm)
                wt("v", tok_bo_bc)
                wt("v", tok_x2_dma_last)
                t1 = em("v", lambda v: v.tensor_tensor(
                    c_x2, ps[:, 0:4096], c_biasbc, ALU.add), sm_v)
                for ob in range(8):
                    bankfree[ob] = t1
                wt("v", t1)
                wt("v", tok_xl)
                tok_ev_all = em("v", lambda v: v.tensor_tensor(
                    c_x2, c_x2, c_x, ALU.add), sm_v)
                tok_x_used = tok_ev_all
                wt("y", tok_ev_all)
                tw = dma("y", bass.AP(d_x2, tt * 128 * 512,
                                      [[512, 128], [TOK * 512, 8], [1, 512]]),
                         c_x2.rearrange("p (a b) -> p a b", a=8), "x2w")
                if debug >= 3:
                    tw = dma("y", bass.AP(dbg["x2"], tt * 128 * 512,
                                          [[512, 128], [TOK * 512, 8], [1, 512]]),
                             c_x2.rearrange("p (a b) -> p a b", a=8), "x2w")
                tok_x2w.append(tw)
                tok_x2_dma_last = tok_x2w[-1]
                t9 = emit_ln(c_x2, c_xn2, c_xn2, tok_ev_all, tok_ev_all,
                             extra_s_wait=tok_xn2_cons)
                wt("t", t9)
                last_tp = None
                for dc in range(32):
                    b = 6 + (dc % 2)
                    wt("t", bankfree[b])
                    tok_tp = em("t", lambda t, b=b, dc=dc: t.transpose(
                        bank(b)[:, 0:128], c_xn2[:, dc * 128:dc * 128 + 128], sb_id[:, :]), sm_t)
                    last_tp = tok_tp
                    eng = "s" if dc % 2 == 0 else "v"
                    wt(eng, tok_tp)
                    if eng == "s":
                        tok_ev = em(eng, lambda e, b=b, dc=dc, tt=tt: e.copy(
                            c_xn2F[dc][:, tt * 128:tt * 128 + 128], bank(b)[:, 0:128]), sm_s)
                    else:
                        tok_ev = em(eng, lambda e, b=b, dc=dc, tt=tt: e.tensor_copy(
                            c_xn2F[dc][:, tt * 128:tt * 128 + 128], bank(b)[:, 0:128]), sm_v)
                    bankfree[b] = tok_ev
                    if eng == "s":
                        tok_evS_last = tok_ev
                    else:
                        tok_evV_last = tok_ev
                tok_xn2_cons = last_tp

            wt("y", tok_bcast_mm_last)
            tok_fbrow = dma("y", c_rows[0:1, :], d_fb.ap(), "cwy")
            tok_fb_bc = None
            wt("t", tok_fbrow)
            wt("t", tok_x_used)
            for ob in range(8):
                bb = 6 + (ob % 2)
                wt("t", bankfree[bb])
                tm = em("t", lambda t, bb=bb, ob=ob: t.matmul(
                    bank(bb), sb_ones[:, :], c_rows[0:1, ob * 512:(ob + 1) * 512],
                    start=True, stop=True), sm_t)
                wt("v", tm)
                tok_fb_bc = em("v", lambda v, bb=bb, ob=ob: v.tensor_copy(
                    c_biasbc[:, ob * 512:(ob + 1) * 512], bank(bb)), sm_v)
                bankfree[bb] = tok_fb_bc

            wt("t", tok_la)
            wt("t", tok_evS_last)
            wt("t", tok_evV_last)
            tok_hT_done = None
            tok_h_cons = None
            for tt in range(8):
                b = tt % 2
                wt("t", bankfree[b])
                tok_mm = None
                for dc in range(32):
                    tok_mm = em("t", lambda t, b=b, tt=tt, dc=dc: t.matmul(
                        bank(b)[:, 0:128], c_xn2F[dc][:, tt * 128:tt * 128 + 128],
                        c_la[:, dc, :], start=(dc == 0), stop=(dc == 31)),
                        sm_t if dc == 31 else None)
                wt("v", tok_mm)
                wt("v", tok_h_cons)
                tcp = em("v", lambda v, b=b: v.tensor_copy(c_h, bank(b)[:, 0:128]), sm_v)
                bankfree[b] = tcp
                tb_ = 2 + (tt % 2)
                wt("t", tcp)
                wt("t", bankfree[tb_])
                ttp = em("t", lambda t, tb_=tb_: t.transpose(
                    bank(tb_)[:, 0:128], c_h, sb_id[:, :]), sm_t)
                tok_h_cons = ttp
                wt("v", ttp)
                tok_hT_done = em("v", lambda v, tb_=tb_, tt=tt: v.tensor_scalar(
                    c_hT[:, tt * 128:tt * 128 + 128], bank(tb_)[:, 0:128],
                    scalar1=sb_coef[:, :], scalar2=None, op0=ALU.mult), sm_v)
                bankfree[tb_] = tok_hT_done

            CM = Carver(C_MAIN_BASE, OS_BASE)
            c_wf = [CM.take(16 * 512, BF16).rearrange("p (c f) -> p c f", c=16) for _ in range(2)]
            c_x2s = [CM.take(8 * 512, F32).rearrange("p (c f) -> p c f", c=8) for _ in range(2)]

            tok_wfh_free = [None, None]
            tok_x2s_free = [None, None]
            tok_outs_free = [None, None]
            x2w_all = tok_x2w[-1]
            for ob in range(8):
                wb = ob % 2
                if ob == 0:
                    wt("g", tok_x_used)
                    wt("g", tok_xn2_cons)
                    wt("g", x2w_all)
                    wt("y", tok_x_used)
                    wt("y", tok_xn2_cons)
                tok_wfh = [None, None]
                for hf in range(2):
                    wt("g", tok_wfh_free[hf])
                    wt("g", tok_wfh_free[1])
                    tok_wfh[hf] = dma("g", c_wf[hf],
                                      bass.AP(d_wfT, ob * (32 * 512) + hf * (16 * 512),
                                              [[8 * 32 * 512, 128], [512, 16], [1, 512]]),
                                      "wf0" if hf == 0 else "wf1")
                wt("y", x2w_all)
                wt("y", tok_x2s_free[wb])
                tok_x2s = dma("y", c_x2s[wb], bass.AP(d_x2, ob * TOK * 512,
                                                      [[512, 128], [128 * 512, 8], [1, 512]]),
                              "x2r0" if wb == 0 else "x2r1")
                if ob == 0:
                    wt("t", tok_lb)
                    wt("t", tok_hT_done)
                tok_lora_mm = [None] * 8
                for hf in range(2):
                    wt("t", tok_wfh[hf])
                    for tt in range(8):
                        if hf == 0:
                            wt("t", bankfree[tt])
                        for dci in range(16):
                            dc = hf * 16 + dci
                            em("t", lambda t, tt=tt, hf=hf, dc=dc, dci=dci: t.matmul(
                                bank(tt), c_xn2F[dc][:, tt * 128:tt * 128 + 128],
                                c_wf[hf][:, dci, :], start=(dc == 0), stop=False))
                        if hf == 1:
                            tok_lora_mm[tt] = em("t", lambda t, tt=tt, ob=ob: t.matmul(
                                bank(tt), c_hT[:, tt * 128:tt * 128 + 128],
                                c_lb[:, ob * 512:(ob + 1) * 512],
                                start=False, stop=True), sm_t)
                        else:
                            if tt == 7:
                                tok_wfh_free[0] = em("t", lambda t: t.nop(), sm_t)
                tok_wfh_free[1] = tok_lora_mm[7]
                for tt in range(8):
                    oi = tt % 2
                    wt("v", tok_lora_mm[tt])
                    if ob == 0 and tt == 0:
                        wt("v", tok_fb_bc)
                    wt("v", tok_outs_free[oi])
                    t1 = em("v", lambda v, tt=tt, ob=ob, oi=oi: v.tensor_tensor(
                        c_outs[oi], bank(tt), c_biasbc[:, ob * 512:(ob + 1) * 512], ALU.add), sm_v)
                    bankfree[tt] = t1
                    wt("v", t1)
                    wt("v", tok_x2s)
                    t2 = em("v", lambda v, oi=oi, wb=wb, tt=tt: v.tensor_tensor(
                        c_outs[oi], c_outs[oi], c_x2s[wb][:, tt, :], ALU.add), sm_v)
                    wt("y", t2)
                    tw = dma("y", bass.AP(d_out, tt * 128 * D + ob * 512,
                                          [[D, 128], [1, 512]]), c_outs[oi],
                             "o0" if oi == 0 else "o1")
                    tok_outs_free[oi] = tw
                    if tt == 7:
                        tok_x2s_free[wb] = t2


        if run_c:
            _phase_c()

        for _sm in dma_sems.values():
            if _sm.n > 0:
                sched.wait("y", (_sm, _sm.n))

        with nc.Block() as block:
            block.tensor(lambda eng: [f(eng) for f in sched.ops["t"]])
            block.vector(lambda eng: [f(eng) for f in sched.ops["v"]])
            block.scalar(lambda eng: [f(eng) for f in sched.ops["s"]])
            block.gpsimd(lambda eng: [f(eng) for f in sched.ops["g"]])
            block.sync(lambda eng: [f(eng) for f in sched.ops["y"]])

    return nc


def _stage(inputs):
    g = lambda k: np.asarray(inputs[k], np.float32)
    X = g("L0_attn_hidden_size").reshape(S, D)
    Wq_c = g("L0_attn_Wq_c")
    bq_c = g("L0_attn_bq_c")
    Wq_up = g("L0_attn_Wq_up")
    bq_up = g("L0_attn_bq_up")
    Wkv_c = g("L0_attn_Wkv_c")
    bkv_c = g("L0_attn_bkv_c")
    Wkv_up = g("L0_attn_Wkv_up")
    bkv_up = g("L0_attn_bkv_up")
    Wo = g("L0_attn_Wo")
    bo = g("L0_attn_bo")
    lora_A = g("L0_ffn_lora_A")
    lora_B = g("L0_ffn_lora_B")
    Wf = g("L0_ffn_W")
    fb = g("L0_ffn_bias")
    lw = g("L0_ffn_lora_weights")

    C = np.ascontiguousarray

    PERM, KPERM, VPERM = [], [], []
    for p in range(NPAIR):
        h0, h1 = 2 * p, 2 * p + 1
        PERM += list(range(h0 * QH, h0 * QH + NOPE))
        PERM += list(range(h1 * QH, h1 * QH + NOPE))
        PERM += list(range(h0 * QH + NOPE, h0 * QH + QH))
        PERM += list(range(h1 * QH + NOPE, h1 * QH + QH))
        KPERM += list(range(h0 * 256, h0 * 256 + 128))
        KPERM += list(range(h1 * 256, h1 * 256 + 128))
        VPERM += list(range(h0 * 256 + 128, h0 * 256 + 256))
        VPERM += list(range(h1 * 256 + 128, h1 * 256 + 256))

    base = {}
    base["wqcT"] = C(Wq_c.T)
    base["bqc_t"] = C(bq_c.reshape(12, 128).T)
    wqupT = Wq_up[PERM, :].T
    base["wqup_til"] = C(wqupT.reshape(12, 128, NPAIR, 384).transpose(1, 2, 0, 3)
                         .reshape(128, NPAIR, 12 * 384))
    base["bqup_t"] = C(bq_up[PERM].reshape(48, 128).T)
    wkvcT = Wkv_c.T
    base["wkvc_til"] = C(wkvcT.reshape(32, 128, 576).transpose(1, 0, 2).reshape(128, 32 * 576))
    bkvc_p = np.zeros(640, np.float32)
    bkvc_p[:576] = bkv_c
    base["bkvc_t"] = C(bkvc_p.reshape(5, 128).T)
    wkKT = Wkv_up[KPERM, :].T
    base["wkK_til"] = C(wkKT.reshape(4, 128, NPAIR, 256).transpose(1, 2, 0, 3)
                        .reshape(128, NPAIR, 4 * 256))
    base["bkupK_t"] = C(bkv_up[KPERM].reshape(32, 128).T)
    wkVT = Wkv_up[VPERM, :].T
    base["wkV_til"] = C(wkVT.reshape(4, 128, NPAIR, 256).transpose(1, 2, 0, 3)
                        .reshape(128, NPAIR, 4 * 256))
    base["bkvV_cols"] = C(bkv_up[VPERM].reshape(32, 128).T)
    base["woT"] = C(Wo.T)
    base["bo_row"] = C(bo.reshape(1, D))
    WfT = Wf.T
    base["wffn_til"] = C(WfT.reshape(32, 128, 8, 512).transpose(1, 2, 0, 3)
                         .reshape(128, 8, 32 * 512))
    base["ffnb_row"] = C(fb.reshape(1, D))
    laT = lora_A.reshape(128, D).T
    base["loraA_til"] = C(laT.reshape(32, 128, 128).transpose(1, 0, 2).reshape(128, 32 * 128))
    base["loraBT"] = C(lora_B.transpose(0, 2, 1).reshape(128, D))
    base["lwrep"] = C(np.repeat(lw, 16)[:, None])
    base["ident"] = np.eye(128, dtype=np.float32)
    base["ones_row"] = np.ones((1, 128), np.float32)
    base["onecol_bf"] = np.ones((128, 1), ml_dtypes.bfloat16)

    in_maps = []
    for c in range(NCORE):
        m = dict(base)
        m["x_rot"] = C(np.concatenate([X[TOK * c:], X[:TOK * c]], axis=0))
        in_maps.append(m)
    return in_maps


def kernel(**inputs):
    in_maps = _stage(inputs)
    nc = build_nc(debug=_DEBUG)
    res = run_bass_kernel_spmd(nc, in_maps, core_ids=list(range(NCORE)))
    if _DEBUG:
        kernel._last = res
    out = np.concatenate([np.asarray(res.results[c]["out_c"]) for c in range(NCORE)], axis=0)
    return out.reshape(1, S, D).astype(np.float32)


# version 09
# speedup vs baseline: 1.0240x; 1.0206x over previous
import os
import math
import numpy as np
import ml_dtypes
import concourse.bass as bass
import concourse.mybir as mybir
from concourse.bass_utils import run_bass_kernel_spmd
from contextlib import ExitStack

F32 = mybir.dt.float32
BF16 = mybir.dt.bfloat16
AF = mybir.ActivationFunctionType
ALU = mybir.AluOpType
AX = mybir.AxisListType

NCORE = 8
S = 8192
D = 4096
TOK = 1024
NH = 32
NPAIR = 16
QL = 1536
KVL = 512
QH = 192
NOPE = 128
VH = 128
ISQ = 1.0 / math.sqrt(QH)
EPS = 1e-5

_DEBUG = int(os.environ.get("MK_DEBUG", "0"))
_NPAIR_RUN = int(os.environ.get("MK_NPAIR", str(NPAIR)))
_PHASE = int(os.environ.get("MK_PHASE", "3"))
_TBS = int(os.environ.get("MK_TBS", "16"))
_SUB = int(os.environ.get("MK_SUB", "15"))
_TPV = int(os.environ.get("MK_TPV", "0"))
_TTS = int(os.environ.get("MK_TTS", "4"))


class Sem:
    def __init__(self, handle, name):
        self.h = handle
        self.name = name
        self.n = 0


class Sched:
    ENGINES = ("t", "v", "s", "g", "y")

    def __init__(self):
        self.ops = {e: [] for e in self.ENGINES}
        self.sems = {}

    def sem(self, name, handle):
        sm = Sem(handle, name)
        self.sems[name] = sm
        return sm

    def emit(self, e, fn, sem=None, k=1):
        if sem is None:
            self.ops[e].append(fn)
            return None
        sem.n += k
        tag = sem.n

        def run(eng, fn=fn, s=sem.h, k=k):
            fn(eng).then_inc(s, k)

        self.ops[e].append(run)
        return (sem, tag)

    def wait(self, e, tok):
        if tok is None:
            return
        sem, tag = tok
        if tag <= 0:
            return
        self.ops[e].append(lambda eng, s=sem.h, t=tag: eng.wait_ge(s, t))

    def barrier(self):
        toks = [(sm, sm.n) for sm in self.sems.values() if sm.n > 0]
        for e in self.ENGINES:
            for tok in toks:
                self.wait(e, tok)


def build_nc(debug=0):
    nc = bass.Bass("TRN2", target_bir_lowering=False)

    d_x = nc.dram_tensor("x_rot", [S, D], F32, kind="ExternalInput")
    d_wqcT = nc.dram_tensor("wqcT", [D, QL], F32, kind="ExternalInput")
    d_bqc = nc.dram_tensor("bqc_t", [128, 12], F32, kind="ExternalInput")
    d_wqup = nc.dram_tensor("wqup_til", [128, NPAIR, 12 * 384], F32, kind="ExternalInput")
    d_bqup = nc.dram_tensor("bqup_t", [128, 48], F32, kind="ExternalInput")
    d_wkvc = nc.dram_tensor("wkvc_til", [128, 32 * 576], F32, kind="ExternalInput")
    d_bkvc = nc.dram_tensor("bkvc_t", [128, 5], F32, kind="ExternalInput")
    d_wkK = nc.dram_tensor("wkK_til", [128, NPAIR, 4 * 256], F32, kind="ExternalInput")
    d_bkK = nc.dram_tensor("bkupK_t", [128, 32], F32, kind="ExternalInput")
    d_wkV = nc.dram_tensor("wkV_til", [128, NPAIR, 4 * 256], F32, kind="ExternalInput")
    d_bkV = nc.dram_tensor("bkvV_cols", [128, 32], F32, kind="ExternalInput")
    d_woT = nc.dram_tensor("woT", [VH, D], F32, kind="ExternalInput")
    d_bo = nc.dram_tensor("bo_row", [1, D], F32, kind="ExternalInput")
    d_wfT = nc.dram_tensor("wffn_til", [128, 8, 32 * 512], F32, kind="ExternalInput")
    d_fb = nc.dram_tensor("ffnb_row", [1, D], F32, kind="ExternalInput")
    d_laT = nc.dram_tensor("loraA_til", [128, 32 * 128], F32, kind="ExternalInput")
    d_lbT = nc.dram_tensor("loraBT", [128, D], F32, kind="ExternalInput")
    d_lw = nc.dram_tensor("lwrep", [128, 1], F32, kind="ExternalInput")
    d_id = nc.dram_tensor("ident", [128, 128], F32, kind="ExternalInput")
    d_ones = nc.dram_tensor("ones_row", [1, 128], F32, kind="ExternalInput")
    d_onec = nc.dram_tensor("onecol_bf", [128, 1], BF16, kind="ExternalInput")

    d_kvF = nc.dram_tensor("kvF_dram", [576, S], BF16)
    d_qcF = nc.dram_tensor("qcF_dram", [2, 128, 12 * 512], BF16)
    d_x2 = nc.dram_tensor("x2_dram", [8, TOK, 512], F32)
    d_out = nc.dram_tensor("out_c", [TOK, D], F32, kind="ExternalOutput")

    dbg = {}
    if debug >= 1:
        dbg["kvF"] = nc.dram_tensor("dbg_kvF", [576, S], BF16, kind="ExternalOutput")
        dbg["qcF"] = nc.dram_tensor("dbg_qcF", [2, 128, 12 * 512], BF16, kind="ExternalOutput")
    if debug >= 2:
        dbg["K0"] = nc.dram_tensor("dbg_K0", [NOPE, S], BF16, kind="ExternalOutput")
        dbg["K1"] = nc.dram_tensor("dbg_K1", [NOPE, S], BF16, kind="ExternalOutput")
        dbg["V0"] = nc.dram_tensor("dbg_V0", [128, 64 * 256], BF16, kind="ExternalOutput")
        dbg["Q0"] = nc.dram_tensor("dbg_Q0", [128, 3 * TOK], BF16, kind="ExternalOutput")
        dbg["osum"] = nc.dram_tensor("dbg_osum", [VH, TOK], F32, kind="ExternalOutput")
    if debug >= 3:
        dbg["x2"] = nc.dram_tensor("dbg_x2", [8, TOK, 512], F32, kind="ExternalOutput")

    sched = Sched()
    em, wt = sched.emit, sched.wait

    with ExitStack() as ctx:
        E = ctx.enter_context

        sb_id = E(nc.sbuf_tensor("sb_id", [128, 128], F32))
        sb_ones = E(nc.sbuf_tensor("sb_ones", [1, 128], F32))
        sb_onec = E(nc.sbuf_tensor("sb_onec", [128, 1], BF16))
        sb_bqc = E(nc.sbuf_tensor("sb_bqc", [128, 12], F32))
        sb_bqup = E(nc.sbuf_tensor("sb_bqup", [128, 48], F32))
        sb_bkvc = E(nc.sbuf_tensor("sb_bkvc", [128, 5], F32))
        sb_bkK = E(nc.sbuf_tensor("sb_bkK", [128, 32], F32))
        sb_bkV = E(nc.sbuf_tensor("sb_bkV", [128, 32], F32))
        sb_lw = E(nc.sbuf_tensor("sb_lw", [128, 1], F32))
        sb_coef = E(nc.sbuf_tensor("sb_coef", [128, 1], F32))
        sb_bvsum = E(nc.sbuf_tensor("sb_bvsum", [128, 1], F32))
        st_sum = E(nc.sbuf_tensor("st_sum", [128, 2], F32))
        st_sqs = E(nc.sbuf_tensor("st_sqs", [128, 2], F32))
        st_nm = E(nc.sbuf_tensor("st_nm", [128, 2], F32))
        st_m2 = E(nc.sbuf_tensor("st_m2", [128, 2], F32))
        st_var = E(nc.sbuf_tensor("st_var", [128, 2], F32))
        st_sq = E(nc.sbuf_tensor("st_sq", [128, 2], F32))
        st_rs = E(nc.sbuf_tensor("st_rs", [128, 2], F32))
        sb_stab = E(nc.sbuf_tensor("sb_stab", [128, 8], F32))
        sb_qmax = E(nc.sbuf_tensor("sb_qmax", [128, 1], F32))
        sb_kmax = E(nc.sbuf_tensor("sb_kmax", [128, 1], F32))
        sb_qm = E(nc.sbuf_tensor("sb_qm", [1, 1], F32))
        sb_km = E(nc.sbuf_tensor("sb_km", [1, 1], F32))
        sb_cqk = E(nc.sbuf_tensor("sb_cqk", [1, 1], F32))
        sb_negc_col = E(nc.sbuf_tensor("sb_negc_col", [128, 1], F32))
        sb_recip0 = E(nc.sbuf_tensor("sb_recip0", [1, 512], F32))
        sb_recip1 = E(nc.sbuf_tensor("sb_recip1", [1, 512], F32))

        AR_N = 51456
        OS_BASE = AR_N - 1536
        arena = E(nc.sbuf_tensor("arena", [128, AR_N], F32))

        class Carver:
            def __init__(self, base=0, limit=AR_N):
                self.off = base
                self.limit = limit

            def take(self, nfree, dtype):
                ef32 = nfree if dtype == F32 else (nfree + 1) // 2
                ef32 = (ef32 + 15) & ~15
                assert self.off + ef32 <= self.limit, \
                    f"arena overflow {self.off}+{ef32} > {self.limit}"
                ap = arena[:, self.off:self.off + ef32]
                self.off += ef32
                if dtype == BF16:
                    ap = ap.bitcast(BF16)[:, 0:nfree]
                return ap

        ps = E(nc.psum_tensor("ps", [128, 4096], F32))

        def bank(b):
            return ps[:, b * 512:(b + 1) * 512]

        bankfree = [None] * 8

        sm_gc = sched.sem("gc", E(nc.semaphore("sm_gc")))
        sm_t = sched.sem("t", E(nc.semaphore("sm_t")))
        sm_v = sched.sem("v", E(nc.semaphore("sm_v")))
        sm_s = sched.sem("s", E(nc.semaphore("sm_s")))
        dma_sems = {}
        for nm in ("c", "x0", "x1", "x2b", "wkvc", "wq0", "wq1", "kv0", "kv1",
                   "qc0", "qc1", "kvn", "qch0", "qch1", "wkK", "wkV", "wqup",
                   "cw", "cwy", "wf0", "wf1", "x2w", "x2r0", "x2r1", "o0", "o1", "st"):
            dma_sems[nm] = sched.sem(nm, E(nc.semaphore("sm_" + nm)))
        sm_st = dma_sems["st"]

        def dma(e, out, in_, sem):
            if isinstance(sem, str):
                sem = dma_sems[sem]
            return em(e, lambda eng, o=out, i=in_: eng.dma_start(out=o, in_=i), sem, 16)

        tok_consts = None
        for (dst, src) in [
            (sb_id[:, :], d_id.ap()), (sb_ones[:, :], d_ones.ap()),
            (sb_onec[:, :], d_onec.ap()), (sb_bqc[:, :], d_bqc.ap()),
            (sb_bqup[:, :], d_bqup.ap()), (sb_bkvc[:, :], d_bkvc.ap()),
            (sb_bkK[:, :], d_bkK.ap()), (sb_bkV[:, :], d_bkV.ap()),
            (sb_lw[:, :], d_lw.ap()),
        ]:
            tok_consts = dma("y", dst, src, "c")
        wt("v", tok_consts)
        em("v", lambda v: v.tensor_scalar(
            sb_coef[:, :], sb_lw[:, :], scalar1=2.0, scalar2=None, op0=ALU.mult), sm_v)
        tok_bvsum = em("v", lambda v: v.reduce_sum(sb_bvsum[:, :], sb_bkV[:, :], AX.X), sm_v)

        ln_rot = [0]
        last_sq_tok = [None]
        slot_sum_free = [None, None]
        slot_sqs_free = [None, None]
        slot_sq_free = [None, None]

        def emit_ln_stats(xbuf, xsqbuf, tok_loaded_s, extra_s_wait=None):
            r = ln_rot[0] % 2
            ln_rot[0] += 1
            sl = slice(r, r + 1)
            wt("s", tok_loaded_s)
            if extra_s_wait is not None:
                wt("s", extra_s_wait)
            wt("s", slot_sum_free[r])
            wt("s", last_sq_tok[0])
            t1 = em("s", lambda s, sl=sl, xb=xbuf, xo=xsqbuf: s.activation(
                xo, xb, AF.Copy, accum_out=st_sum[:, sl]), sm_s)
            wt("s", slot_sqs_free[r])
            wt("s", t1)
            t2 = em("s", lambda s, sl=sl, xb=xbuf, xq=xsqbuf: s.activation(
                xq, xb, AF.Square, accum_out=st_sqs[:, sl]), sm_s)
            last_sq_tok[0] = t2
            return r, t1, t2

        def emit_ln_finish(xbuf, xnbuf, r, t1, t2, tok_loaded_v, extra_v_wait=None):
            sl = slice(r, r + 1)
            if extra_v_wait is not None:
                wt("v", extra_v_wait)
            wt("v", t1)
            t3 = em("v", lambda v, sl=sl: v.tensor_scalar(
                st_nm[:, sl], st_sum[:, sl], scalar1=-1.0 / D, scalar2=None, op0=ALU.mult), sm_v)
            slot_sum_free[r] = t3
            wt("v", t3)
            t4 = em("v", lambda v, sl=sl: v.tensor_tensor(
                st_m2[:, sl], st_nm[:, sl], st_nm[:, sl], ALU.mult), sm_v)
            wt("v", t2)
            wt("v", t4)
            t5 = em("v", lambda v, sl=sl: v.tensor_scalar(
                st_var[:, sl], st_sqs[:, sl], scalar1=1.0 / D, scalar2=st_m2[:, sl],
                op0=ALU.mult, op1=ALU.subtract), sm_v)
            slot_sqs_free[r] = t5
            wt("v", t5)
            t6 = em("v", lambda v, sl=sl: v.tensor_scalar(
                st_var[:, sl], st_var[:, sl], scalar1=EPS, scalar2=None, op0=ALU.add), sm_v)
            wt("s", t6)
            wt("s", slot_sq_free[r])
            t7 = em("s", lambda s, sl=sl: s.activation(st_sq[:, sl], st_var[:, sl], AF.Sqrt), sm_s)
            wt("v", t7)
            t8 = em("v", lambda v, sl=sl: v.reciprocal(st_rs[:, sl], st_sq[:, sl]), sm_v)
            slot_sq_free[r] = t8
            wt("v", t8)
            wt("v", tok_loaded_v)
            t9 = em("v", lambda v, sl=sl, xb=xbuf, xo=xnbuf: v.tensor_scalar(
                xo, xb, scalar1=st_nm[:, sl], scalar2=st_rs[:, sl],
                op0=ALU.add, op1=ALU.mult), sm_v)
            return t9

        def emit_ln(xbuf, xnbuf, xsqbuf, tok_loaded_v, tok_loaded_s,
                    extra_v_wait=None, extra_s_wait=None):
            r, t1, t2 = emit_ln_stats(xbuf, xsqbuf, tok_loaded_s, extra_s_wait)
            return emit_ln_finish(xbuf, xnbuf, r, t1, t2, tok_loaded_v, extra_v_wait)

        A = Carver()
        a_x = [A.take(D, F32) for _ in range(2)]
        a_xsq = A.take(D, F32)
        a_xn = [A.take(D, F32) for _ in range(2)]
        a_xnF = [[A.take(512, BF16) for _ in range(32)] for _ in range(2)]
        a_wkvc = A.take(32 * 576, BF16).rearrange("p (c f) -> p c f", c=32)
        a_wqct = [A.take(2 * QL, BF16).rearrange("p (c f) -> p c f", c=2) for _ in range(2)]
        a_kvst = [A.take(512, BF16) for _ in range(2)]
        a_qcst = [A.take(512, BF16) for _ in range(2)]

        tok_wkvc = dma("g", a_wkvc, d_wkvc.ap().rearrange("p (c f) -> p c f", c=32), "wkvc")

        tok_x_free = [None, None]
        tok_xld = [None, None]
        stats_tok = [None, None]
        tok_xn_cons = [None, None]
        tok_xnF_ready = [[None] * 32, [None] * 32]
        tok_set_free = [None, None]
        tok_kv_last = [None] * 5
        tok_kvst_free = [None, None]
        tok_qcst_free = [None, None]
        qc_buf_free = [None, None]
        tok_qc_last = [None] * 6

        def gen_kv_mms(tb):
            st = tb % 2
            if tb == 0:
                wt("t", tok_wkvc)
            for dc in range(32):
                wt("t", tok_xnF_ready[st][dc])
            for f in range(5):
                rows = 128 if f < 4 else 64
                wt("t", bankfree[f])
                for dc in range(32):
                    tok_kv_last[f] = em("t", lambda t, f=f, dc=dc, rows=rows, st=st: t.matmul(
                        bank(f)[0:rows, 0:512],
                        a_wkvc[:, dc, f * 128:f * 128 + rows],
                        a_xnF[st][dc],
                        start=(dc == 0), stop=(dc == 31)),
                        sm_t if dc == 31 else None)
                    yield

        def flush_kv(tb):
            for f in range(5):
                rows = 128 if f < 4 else 64
                kb = f % 2
                wt("v", tok_kv_last[f])
                wt("v", tok_kvst_free[kb])
                tok_ev = em("v", lambda v, f=f, kb=kb, rows=rows: v.tensor_scalar(
                    a_kvst[kb][0:rows, :], bank(f)[0:rows, 0:512],
                    scalar1=sb_bkvc[0:rows, f:f + 1], scalar2=None, op0=ALU.add), sm_v)
                bankfree[f] = tok_ev
                wt("y", tok_ev)
                tok_st = dma("y", bass.AP(d_kvF, f * 128 * S + tb * 512,
                                          [[S, rows], [1, 512]]),
                             a_kvst[kb][0:rows, :], "kv0" if kb == 0 else "kv1")
                if debug >= 1:
                    tok_st = dma("y", bass.AP(dbg["kvF"], f * 128 * S + tb * 512,
                                              [[S, rows], [1, 512]]),
                                 a_kvst[kb][0:rows, :], "kv0" if kb == 0 else "kv1")
                tok_kvst_free[kb] = tok_st

        kv_gen = None
        for tb in range(16):
            st = tb % 2
            for tt in range(4):
                tile = tb * 4 + tt
                bj = tile % 2
                if tile == 0:
                    tok_xld[0] = dma("y", a_x[0], d_x[0:128, :], "x0")
                    stats_tok[0] = emit_ln_stats(a_x[0], a_xsq, tok_xld[0])
                if tile + 1 < 64:
                    t1_ = tile + 1
                    wt("y", tok_x_free[t1_ % 2])
                    tok_xld[t1_ % 2] = dma("y", a_x[t1_ % 2],
                                           d_x[t1_ * 128:(t1_ + 1) * 128, :],
                                           "x0" if t1_ % 2 == 0 else "x1")
                    stats_tok[t1_ % 2] = emit_ln_stats(a_x[t1_ % 2], a_xsq,
                                                       tok_xld[t1_ % 2])
                r_, ts1, ts2 = stats_tok[tile % 2]
                tok_xn = emit_ln_finish(a_x[bj], a_xn[bj], r_, ts1, ts2,
                                        tok_xld[bj], extra_v_wait=tok_xn_cons[bj])
                tok_x_free[bj] = tok_xn
                wt("t", tok_xn)
                last_tp = None
                for dc in range(32):
                    b = 5 + (dc % 3)
                    wt("t", bankfree[b])
                    tok_tp = em("t", lambda t, b=b, xb=a_xn[bj], dc=dc: t.transpose(
                        bank(b)[:, 0:128], xb[:, dc * 128:dc * 128 + 128], sb_id[:, :]), sm_t)
                    last_tp = tok_tp
                    if kv_gen is not None:
                        next(kv_gen, None)
                        next(kv_gen, None)
                    eng = "s" if dc % 2 == 0 else "v"
                    wt(eng, tok_tp)
                    if tt == 0 and dc == 0:
                        wt("s", tok_set_free[st])
                        wt("v", tok_set_free[st])
                    if eng == "s":
                        tok_ev = em(eng, lambda e, b=b, dc=dc, tt=tt, st=st: e.copy(
                            a_xnF[st][dc][:, tt * 128:tt * 128 + 128], bank(b)[:, 0:128]), sm_s)
                    else:
                        tok_ev = em(eng, lambda e, b=b, dc=dc, tt=tt, st=st: e.tensor_copy(
                            a_xnF[st][dc][:, tt * 128:tt * 128 + 128], bank(b)[:, 0:128]), sm_v)
                    bankfree[b] = tok_ev
                    if tt == 3:
                        tok_xnF_ready[st][dc] = tok_ev
                tok_xn_cons[bj] = last_tp
            if kv_gen is not None:
                for _ in kv_gen:
                    pass
                flush_kv(tb - 1)
                tok_set_free[(tb - 1) % 2] = tok_kv_last[4]
            if tb < 2:
                for dc in range(32):
                    wt("t", tok_xnF_ready[st][dc])
                for pas in range(2):
                    tok_mm = None
                    for dcg in range(16):
                        gb = dcg % 2
                        wt("g", qc_buf_free[gb])
                        tok_w = dma("g", a_wqct[gb],
                                    bass.AP(d_wqcT, dcg * 2 * 128 * QL,
                                            [[QL, 128], [128 * QL, 2], [1, QL]]),
                                    "wq0" if gb == 0 else "wq1")
                        wt("t", tok_w)
                        for di in range(2):
                            dc = dcg * 2 + di
                            for f6 in range(6):
                                fg = pas * 6 + f6
                                if dc == 0:
                                    wt("t", bankfree[f6])
                                tok_mm = em("t", lambda t, f6=f6, fg=fg, gb=gb, dc=dc, di=di, st=st: t.matmul(
                                    bank(f6)[:, 0:512],
                                    a_wqct[gb][:, di, fg * 128:fg * 128 + 128],
                                    a_xnF[st][dc],
                                    start=(dc == 0), stop=(dc == 31)),
                                    sm_t if (f6 == 5 or dc == 31) else None)
                                if dc == 31:
                                    tok_qc_last[f6] = tok_mm
                        qc_buf_free[gb] = tok_mm
                    for f6 in range(6):
                        fg = pas * 6 + f6
                        qb_ = f6 % 2
                        wt("v", tok_qc_last[f6])
                        wt("v", tok_qcst_free[qb_])
                        tok_ev = em("v", lambda v, f6=f6, fg=fg, qb_=qb_: v.tensor_scalar(
                            a_qcst[qb_], bank(f6)[:, 0:512],
                            scalar1=sb_bqc[:, fg:fg + 1], scalar2=None, op0=ALU.add), sm_v)
                        bankfree[f6] = tok_ev
                        wt("y", tok_ev)
                        tok_st = dma("y", bass.AP(d_qcF, tb * (128 * 12 * 512) + fg * 512,
                                                  [[12 * 512, 128], [1, 512]]),
                                     a_qcst[qb_], "qc0" if qb_ == 0 else "qc1")
                        if debug >= 1:
                            tok_st = dma("y", bass.AP(dbg["qcF"], tb * (128 * 12 * 512) + fg * 512,
                                                      [[12 * 512, 128], [1, 512]]),
                                         a_qcst[qb_], "qc0" if qb_ == 0 else "qc1")
                        tok_qcst_free[qb_] = tok_st
            kv_gen = gen_kv_mms(tb)
        for _ in kv_gen:
            pass
        flush_kv(15)
        tok_set_free[15 % 2] = tok_kv_last[4]

        run_b = _PHASE >= 2
        run_c = _PHASE >= 3
        sched.barrier()
        B = Carver(0, OS_BASE)
        b_kvn = [B.take(S, BF16) for _ in range(4)]
        b_rope = B.take(S, BF16)
        b_K = [B.take(S, BF16) for _ in range(2)]
        b_V = B.take(64 * 256, BF16)
        b_Q = [B.take(TOK, BF16) for _ in range(3)]
        b_P = [[B.take(512, BF16) for _ in range(3)] for _ in range(2)]
        b_qch = B.take(12 * 512, BF16).rearrange("p (c f) -> p c f", c=12)
        b_wqup = B.take(12 * 384, BF16).rearrange("p (c f) -> p c f", c=12)
        b_wkK = B.take(4 * 256, BF16).rearrange("p (c f) -> p c f", c=4)
        b_wkV = B.take(4 * 256, BF16).rearrange("p (c f) -> p c f", c=4)
        b_bcast = B.take(512, F32)
        b_otmp = B.take(512, F32)
        b_Ps = [[B.take(512, BF16) for _ in range(2)] for _ in range(2)]
        b_osum = arena[:, OS_BASE:OS_BASE + 1024]
        b_osbf = arena[:, OS_BASE + 1024:OS_BASE + 1536].bitcast(BF16)

        for lat in range(4):
            dma("y", b_kvn[lat],
                bass.AP(d_kvF, lat * 128 * S, [[S, 128], [1, S]]), "kvn")
        dma("y", b_rope[0:64, :], bass.AP(d_kvF, 512 * S, [[S, 64], [1, S]]), "kvn")
        tok_kvn_all = dma("y", b_rope[64:128, :],
                          bass.AP(d_kvF, 512 * S, [[S, 64], [1, S]]), "kvn")
        tok_rope_l = tok_rope_h = tok_kvn_all
        tok_osum0 = em("v", lambda v: v.memset(b_osum, 0.0), sm_v)

        wt("v", tok_rope_h)
        wt("v", tok_rope_l)
        tok_ropemax = em("v", lambda v: v.tensor_reduce(
            sb_stab[:, 5:6], b_rope, AX.X, ALU.max, apply_absolute_value=True), sm_v)

        tok_w_free = {"q": None, "k": None, "v": None}
        tok_qch_free = [None, None]
        tok_K_free = [None, None]
        tok_V_free = None
        tok_Q_free = None
        tok_osum_last = tok_osum0
        tok_exp_prev_pair = None
        tok_gq_prev = None
        tok_gk_prev = None
        tok_tc1_prev = None
        tok_bcast_prev = None

        pend = [None]
        texp = [(None, None)] * 3
        tps_tok = [[None, None], [None, None]]
        tps_free = [[None, None], [None, None]]
        p_read = [[None] * 3, [None] * 3]
        tok_bc_prev = [None, None]
        tok_osum_chain = [tok_osum0]
        flush_fn = [lambda: None]

        def flush_epi_outer():
            flush_fn[0]()
        npair_run = min(_NPAIR_RUN, NPAIR) if run_b else 0
        for p in range(npair_run):
            wt("g", tok_w_free["k"])
            tok_wk = dma("g", b_wkK, bass.AP(d_wkK, p * (4 * 256),
                                             [[NPAIR * 4 * 256, 128], [256, 4], [1, 256]]), "wkK")
            wt("g", tok_w_free["v"])
            tok_wv = dma("g", b_wkV, bass.AP(d_wkV, p * (4 * 256),
                                             [[NPAIR * 4 * 256, 128], [256, 4], [1, 256]]), "wkV")
            wt("g", tok_w_free["q"])
            tok_wq = dma("g", b_wqup, bass.AP(d_wqup, p * (12 * 384),
                                              [[NPAIR * 12 * 384, 128], [384, 12], [1, 384]]), "wqup")

            prod_rot = [0]

            def prod_bank():
                b = prod_rot[0] % 4
                prod_rot[0] += 1
                return b

            wt("t", tok_wq)
            wt("v", tok_Q_free)
            tok_Q_ready = None
            for half in range(2):
                wt("y", tok_qch_free[1] if half == 0 else tok_qch_free[0])
                tok_qc = dma("y", b_qch, bass.AP(d_qcF, half * (128 * 12 * 512),
                                                 [[12 * 512, 128], [512, 12], [1, 512]]),
                             "qch0" if half == 0 else "qch1")
                wt("t", tok_qc)
                for qch in range(3):
                    b = prod_bank()
                    wt("t", bankfree[b])
                    tok_mm = None
                    for ql in range(12):
                        tok_mm = em("t", lambda t, b=b, qch=qch, ql=ql: t.matmul(
                            bank(b),
                            b_wqup[:, ql, qch * 128:qch * 128 + 128],
                            b_qch[:, ql, :],
                            start=(ql == 0), stop=(ql == 11)),
                            sm_t if ql == 11 else None)
                    tok_qch_free[half] = tok_mm
                    wt("v", tok_mm)
                    tok_Q_ready = em("v", lambda v, b=b, qch=qch, half=half, p=p: v.tensor_scalar(
                        b_Q[qch][:, half * 512:half * 512 + 512], bank(b),
                        scalar1=sb_bqup[:, 3 * p + qch:3 * p + qch + 1], scalar2=ISQ,
                        op0=ALU.add, op1=ALU.mult), sm_v)
                    bankfree[b] = tok_Q_ready

            wt("v", tok_Q_ready)
            wt("v", tok_gq_prev)
            tql = None
            for qch in range(3):
                tql = em("v", lambda v, qch=qch: v.tensor_reduce(
                    sb_stab[:, qch:qch + 1], b_Q[qch], AX.X, ALU.max,
                    apply_absolute_value=True), sm_v)
            wt("v", tql)
            tqm = em("v", lambda v: v.tensor_reduce(
                sb_qmax[:, :], sb_stab[:, 0:3], AX.X, ALU.max), sm_v)
            wt("t", tok_wk)
            if p == 0:
                wt("t", tok_kvn_all)
            tok_K_ready = [None, None]
            for h in range(2):
                wt("v", tok_K_free[h])
                for tk in range(16):
                    b = prod_bank()
                    wt("t", bankfree[b])
                    tok_mm = None
                    for lat in range(4):
                        tok_mm = em("t", lambda t, b=b, h=h, tk=tk, lat=lat: t.matmul(
                            bank(b),
                            b_wkK[:, lat, h * 128:h * 128 + 128],
                            b_kvn[lat][:, tk * 512:tk * 512 + 512],
                            start=(lat == 0), stop=(lat == 3)),
                            sm_t if lat == 3 else None)
                    wt("v", tok_mm)
                    tok_ev = em("v", lambda v, b=b, h=h, tk=tk, p=p: v.tensor_scalar(
                        b_K[h][:, tk * 512:tk * 512 + 512], bank(b),
                        scalar1=sb_bkK[:, 2 * p + h:2 * p + h + 1], scalar2=None,
                        op0=ALU.add), sm_v)
                    bankfree[b] = tok_ev
                tok_K_ready[h] = tok_ev

            wt("v", tok_K_ready[0])
            wt("v", tok_K_ready[1])
            wt("v", tok_gk_prev)
            em("v", lambda v: v.tensor_reduce(
                sb_stab[:, 3:4], b_K[0], AX.X, ALU.max, apply_absolute_value=True), sm_v)
            tk1 = em("v", lambda v: v.tensor_reduce(
                sb_stab[:, 4:5], b_K[1], AX.X, ALU.max, apply_absolute_value=True), sm_v)
            wt("v", tk1)
            wt("v", tok_ropemax)
            tkm = em("v", lambda v: v.tensor_reduce(
                sb_kmax[:, :], sb_stab[:, 3:6], AX.X, ALU.max), sm_v)
            wt("t", tok_wv)
            wt("v", tok_V_free)
            tok_V_ready = None
            for tk in range(64):
                b = prod_bank()
                wt("t", bankfree[b])
                tok_mm = None
                for lat in range(4):
                    tok_mm = em("t", lambda t, b=b, tk=tk, lat=lat: t.matmul(
                        bank(b)[:, 0:256],
                        b_kvn[lat][:, tk * 128:tk * 128 + 128],
                        b_wkV[:, lat, :],
                        start=(lat == 0), stop=(lat == 3)),
                        sm_t if lat == 3 else None)
                wt("v", tok_mm)
                tok_V_ready = em("v", lambda v, b=b, tk=tk: v.tensor_copy(
                    b_V[:, tk * 256:tk * 256 + 256], bank(b)[:, 0:256]), sm_v)
                bankfree[b] = tok_V_ready

            wt("g", tqm)
            wt("g", tkm)
            wt("g", tok_tc1_prev)
            tok_gq_prev = em("g", lambda g: g.tensor_reduce(
                sb_qm[:, :], sb_qmax[:, :], AX.C, ALU.max), sm_gc)
            tok_gk_prev = em("g", lambda g: g.tensor_reduce(
                sb_km[:, :], sb_kmax[:, :], AX.C, ALU.max), sm_gc)
            wt("v", tok_gq_prev)
            wt("v", tok_gk_prev)
            tok_tc1_prev = em("v", lambda v: v.tensor_tensor(
                sb_cqk[:, :], sb_qm[:, :], sb_km[:, :], ALU.mult), sm_v)
            wt("v", tok_tc1_prev)
            tc2 = em("v", lambda v: v.tensor_scalar(
                sb_cqk[:, :], sb_cqk[:, :], scalar1=-float(QH), scalar2=-45.0,
                op0=ALU.mult, op1=ALU.max), sm_v)
            flush_epi_outer()
            wt("t", tc2)
            wt("t", bankfree[4])
            tbc = em("t", lambda t: t.matmul(
                bank(4)[:, 0:1], sb_ones[:, :], sb_cqk[:, :], start=True, stop=True), sm_t)
            wt("v", tbc)
            wt("v", tok_exp_prev_pair)
            tnc = em("v", lambda v: v.tensor_copy(sb_negc_col[:, :], bank(4)[:, 0:1]), sm_v)
            bankfree[4] = tnc
            wt("t", tok_V_ready)
            wt("s", tnc)
            tok_score_last = None

            def emit_scores(qb, kc):
                par = kc % 2
                b0, b1 = par, 2 + par
                wt("t", bankfree[b0])
                wt("t", bankfree[b1])
                em("t", lambda t, b0=b0, kc=kc, qb=qb: t.matmul(
                    bank(b0), b_K[0][:, kc * 128:kc * 128 + 128],
                    b_Q[0][:, qb * 512:qb * 512 + 512], start=True, stop=False))
                em("t", lambda t, b1=b1, kc=kc, qb=qb: t.matmul(
                    bank(b1), b_K[1][:, kc * 128:kc * 128 + 128],
                    b_Q[1][:, qb * 512:qb * 512 + 512], start=True, stop=False))
                em("t", lambda t, b0=b0, kc=kc, qb=qb: t.matmul(
                    bank(b0), b_rope[0:64, kc * 128:kc * 128 + 128],
                    b_Q[2][0:64, qb * 512:qb * 512 + 512],
                    start=False, stop=True, tile_position=(0, 0)))
                tok_sc = em("t", lambda t, b1=b1, kc=kc, qb=qb: t.matmul(
                    bank(b1), b_rope[64:128, kc * 128:kc * 128 + 128],
                    b_Q[2][64:128, qb * 512:qb * 512 + 512],
                    start=False, stop=True, tile_position=(64, 0)), sm_t)
                wt("s", tok_sc)
                wt("s", p_read[0][kc % 3])
                wt("s", p_read[1][kc % 3])
                te0 = em("s", lambda s, b0=b0, kc=kc: s.activation(
                    b_P[0][kc % 3], bank(b0), AF.Exp, bias=sb_negc_col[:, :]), sm_s)
                te1 = em("s", lambda s, b1=b1, kc=kc: s.activation(
                    b_P[1][kc % 3], bank(b1), AF.Exp, bias=sb_negc_col[:, :]), sm_s)
                bankfree[b0] = te0
                bankfree[b1] = te1
                texp[kc % 3] = (te0, te1)
                if kc % 2 == 1:
                    for h, te in ((0, te0), (1, te1)):
                        wt("v", te)
                        wt("v", tps_free[h][(kc // 2) % 2])
                        tok_add = em("v", lambda v, h=h, kc=kc: v.tensor_tensor(
                            b_Ps[h][(kc // 2) % 2], b_P[h][(kc - 1) % 3],
                            b_P[h][kc % 3], ALU.add), sm_v)
                        tps_tok[h][(kc // 2) % 2] = tok_add
                        p_read[h][(kc - 1) % 3] = tok_add
                        p_read[h][kc % 3] = tok_add
                return tok_sc, te1

            def emit_pv(qb, kc, epi_banks_wait):
                if epi_banks_wait:
                    for ob in (4, 5):
                        wt("t", bankfree[ob])
                wt("t", texp[kc % 3][0])
                wt("t", texp[kc % 3][1])
                em("t", lambda t, kc=kc: t.matmul(
                    bank(4), b_V[:, kc * 256:kc * 256 + 128], b_P[0][kc % 3],
                    start=(kc == 0), stop=(kc == 63)))
                tok = em("t", lambda t, kc=kc: t.matmul(
                    bank(5), b_V[:, kc * 256 + 128:kc * 256 + 256], b_P[1][kc % 3],
                    start=(kc == 0), stop=(kc == 63)),
                    sm_t if kc == 63 else None)
                if kc % 2 == 1:
                    j = (kc // 2) % 2
                    if kc == 1:
                        for ob in (6, 7):
                            wt("t", bankfree[ob])
                    for h in range(2):
                        wt("t", tps_tok[h][j])
                        tok = em("t", lambda t, kc=kc, h=h, j=j: t.matmul(
                            bank(6 + h)[0:1, :], sb_onec[:, :], b_Ps[h][j],
                            start=(kc == 1), stop=(kc == 63)),
                            sm_t if kc == 63 else None)
                        tps_free[h][j] = tok if kc == 63 else None
                return tok

            def emit_epi_v1(qb, tok_pv):
                trs = []
                for h in range(2):
                    wt("v", tok_pv)
                    wt("v", tok_bc_prev[h])
                    trs.append(em("v", lambda v, h=h: v.reciprocal(
                        (sb_recip0 if h == 0 else sb_recip1)[0:1, :],
                        bank(6 + h)[0:1, 0:512]), sm_v))
                    bankfree[6 + h] = trs[h]
                return trs

            def flush_epi():
                if not pend[0]:
                    return
                qb, trs = pend[0]
                pend[0] = None
                for h in range(2):
                    bcb = 1 + 2 * h
                    wt("t", trs[h])
                    wt("t", bankfree[bcb])
                    tok_bc = em("t", lambda t, bcb=bcb, h=h: t.matmul(
                        bank(bcb), sb_ones[:, :],
                        (sb_recip0 if h == 0 else sb_recip1)[0:1, :],
                        start=True, stop=True), sm_t)
                    tok_bc_prev[h] = tok_bc
                    wt("v", tok_bc)
                    tcp = em("v", lambda v, bcb=bcb: v.tensor_copy(
                        b_bcast, bank(bcb)), sm_v)
                    bankfree[bcb] = tcp
                    wt("v", tcp)
                    tmu = em("v", lambda v, h=h: v.tensor_tensor(
                        b_otmp, bank(4 + h), b_bcast, ALU.mult), sm_v)
                    bankfree[4 + h] = tmu
                    wt("v", tmu)
                    wt("v", tok_osum_chain[0])
                    tok_osum_chain[0] = em("v", lambda v, qb=qb: v.tensor_tensor(
                        b_osum[:, qb * 512:qb * 512 + 512],
                        b_osum[:, qb * 512:qb * 512 + 512], b_otmp, ALU.add), sm_v)

            flush_fn[0] = flush_epi
            for qb in range(2):
                tok_pv = None
                for kc in range(64):
                    tok_sc, te1 = emit_scores(qb, kc)
                    tok_score_last = tok_sc
                    tok_exp_prev_pair = te1
                    if kc == 2:
                        flush_epi()
                    if kc >= 2:
                        tok_pv = emit_pv(qb, kc - 2, epi_banks_wait=(kc == 2))
                tok_pv = emit_pv(qb, 62, False)
                tok_pv = emit_pv(qb, 63, False)
                tok_V_free = tok_pv
                trs = emit_epi_v1(qb, tok_pv)
                pend[0] = (qb, trs)
            tok_K_free = [tok_score_last, tok_score_last]
            tok_Q_free = tok_score_last
            tok_w_free = {"q": tok_Q_ready, "k": tok_K_ready[1], "v": tok_V_ready}

        flush_fn[0]()
        wt("v", tok_bvsum)
        wt("v", tok_osum_chain[0])
        tb1 = em("v", lambda v: v.tensor_scalar(
            b_osum, b_osum, scalar1=sb_bvsum[:, :], scalar2=None, op0=ALU.add), sm_v)
        wt("v", tb1)
        tok_osbf = em("v", lambda v: v.tensor_copy(b_osbf, b_osum), sm_v)

        if debug >= 2:
            wt("y", tok_osbf)
            dma("y", dbg["K0"].ap(), b_K[0], "cwy")
            dma("y", dbg["K1"].ap(), b_K[1], "cwy")
            dma("y", dbg["V0"].ap(), b_V, "cwy")
            for qch in range(3):
                dma("y", bass.AP(dbg["Q0"], qch * TOK, [[3 * TOK, 128], [1, TOK]]), b_Q[qch], "cwy")
            dma("y", dbg["osum"].ap(), b_osum, "cwy")

        def _phase_c():
            sched.barrier()
            C = Carver(0, OS_BASE)
            c_xn2F = [C.take(TOK, BF16) for _ in range(32)]
            c_rows = C.take(D, F32)
            c_la = C.take(32 * 128, BF16).rearrange("p (c f) -> p c f", c=32)
            c_lb = C.take(D, BF16)
            c_hT = C.take(TOK, BF16)
            c_h = C.take(128, F32)
            c_outs = [C.take(512, F32) for _ in range(2)]
            c_woT = C.take(D, BF16)
            c_biasbc = C.take(D, F32)
            C_MAIN_BASE = C.off
            c_x = C.take(D, F32)
            c_x2 = C.take(D, F32)
            c_xn2 = C.take(D, F32)

            dma("g", c_la, d_laT.ap().rearrange("p (c f) -> p c f", c=32), "cw")
            dma("g", c_lb, d_lbT.ap(), "cw")
            tok_woT = dma("g", c_woT, d_woT.ap(), "cw")
            tok_la = tok_lb = tok_woT
            tok_borow = dma("y", c_rows[0:1, :], d_bo.ap(), "cwy")

            tok_bo_bc = None
            wt("t", tok_borow)
            for ob in range(8):
                bb = 6 + (ob % 2)
                wt("t", bankfree[bb])
                tm = em("t", lambda t, bb=bb, ob=ob: t.matmul(
                    bank(bb), sb_ones[:, :], c_rows[0:1, ob * 512:(ob + 1) * 512],
                    start=True, stop=True), sm_t)
                tok_bcast_mm_last = tm
                wt("v", tm)
                tok_bo_bc = em("v", lambda v, bb=bb, ob=ob: v.tensor_copy(
                    c_biasbc[:, ob * 512:(ob + 1) * 512], bank(bb)), sm_v)
                bankfree[bb] = tok_bo_bc

            tok_x_used = None
            tok_x2_dma_last = None
            tok_xn2_cons = None
            tok_x2w = []
            tok_evS_last = None
            tok_evV_last = None
            tok_op_last = None
            for tt in range(8):
                wt("y", tok_x_used)
                tok_xl = dma("y", c_x, d_x[tt * 128:(tt + 1) * 128, :], "x0")
                wt("t", tok_osbf)
                wt("t", tok_woT)
                for ob in range(8):
                    wt("t", bankfree[ob])
                    tm = em("t", lambda t, tt=tt, ob=ob: t.matmul(
                        bank(ob), b_osbf[:, tt * 128:tt * 128 + 128],
                        c_woT[:, ob * 512:(ob + 1) * 512], start=True, stop=True),
                        sm_t if ob == 7 else None)
                tok_op_last = tm
                wt("v", tm)
                wt("v", tok_bo_bc)
                wt("v", tok_x2_dma_last)
                t1 = em("v", lambda v: v.tensor_tensor(
                    c_x2, ps[:, 0:4096], c_biasbc, ALU.add), sm_v)
                for ob in range(8):
                    bankfree[ob] = t1
                wt("v", t1)
                wt("v", tok_xl)
                tok_ev_all = em("v", lambda v: v.tensor_tensor(
                    c_x2, c_x2, c_x, ALU.add), sm_v)
                tok_x_used = tok_ev_all
                wt("y", tok_ev_all)
                tw = dma("y", bass.AP(d_x2, tt * 128 * 512,
                                      [[512, 128], [TOK * 512, 8], [1, 512]]),
                         c_x2.rearrange("p (a b) -> p a b", a=8), "x2w")
                if debug >= 3:
                    tw = dma("y", bass.AP(dbg["x2"], tt * 128 * 512,
                                          [[512, 128], [TOK * 512, 8], [1, 512]]),
                             c_x2.rearrange("p (a b) -> p a b", a=8), "x2w")
                tok_x2w.append(tw)
                tok_x2_dma_last = tok_x2w[-1]
                t9 = emit_ln(c_x2, c_xn2, c_xn2, tok_ev_all, tok_ev_all,
                             extra_s_wait=tok_xn2_cons)
                wt("t", t9)
                last_tp = None
                for dc in range(32):
                    b = 5 + (dc % 3)
                    wt("t", bankfree[b])
                    tok_tp = em("t", lambda t, b=b, dc=dc: t.transpose(
                        bank(b)[:, 0:128], c_xn2[:, dc * 128:dc * 128 + 128], sb_id[:, :]), sm_t)
                    last_tp = tok_tp
                    eng = "s" if dc % 2 == 0 else "v"
                    wt(eng, tok_tp)
                    if eng == "s":
                        tok_ev = em(eng, lambda e, b=b, dc=dc, tt=tt: e.copy(
                            c_xn2F[dc][:, tt * 128:tt * 128 + 128], bank(b)[:, 0:128]), sm_s)
                    else:
                        tok_ev = em(eng, lambda e, b=b, dc=dc, tt=tt: e.tensor_copy(
                            c_xn2F[dc][:, tt * 128:tt * 128 + 128], bank(b)[:, 0:128]), sm_v)
                    bankfree[b] = tok_ev
                    if eng == "s":
                        tok_evS_last = tok_ev
                    else:
                        tok_evV_last = tok_ev
                tok_xn2_cons = last_tp

            wt("y", tok_bcast_mm_last)
            tok_fbrow = dma("y", c_rows[0:1, :], d_fb.ap(), "cwy")
            tok_fb_bc = None
            wt("t", tok_fbrow)
            wt("t", tok_x_used)
            for ob in range(8):
                bb = 6 + (ob % 2)
                wt("t", bankfree[bb])
                tm = em("t", lambda t, bb=bb, ob=ob: t.matmul(
                    bank(bb), sb_ones[:, :], c_rows[0:1, ob * 512:(ob + 1) * 512],
                    start=True, stop=True), sm_t)
                wt("v", tm)
                tok_fb_bc = em("v", lambda v, bb=bb, ob=ob: v.tensor_copy(
                    c_biasbc[:, ob * 512:(ob + 1) * 512], bank(bb)), sm_v)
                bankfree[bb] = tok_fb_bc

            wt("t", tok_la)
            wt("t", tok_evS_last)
            wt("t", tok_evV_last)
            tok_hT_done = None
            tok_h_cons = None
            for tt in range(8):
                b = tt % 2
                wt("t", bankfree[b])
                tok_mm = None
                for dc in range(32):
                    tok_mm = em("t", lambda t, b=b, tt=tt, dc=dc: t.matmul(
                        bank(b)[:, 0:128], c_xn2F[dc][:, tt * 128:tt * 128 + 128],
                        c_la[:, dc, :], start=(dc == 0), stop=(dc == 31)),
                        sm_t if dc == 31 else None)
                wt("v", tok_mm)
                wt("v", tok_h_cons)
                tcp = em("v", lambda v, b=b: v.tensor_copy(c_h, bank(b)[:, 0:128]), sm_v)
                bankfree[b] = tcp
                tb_ = 2 + (tt % 2)
                wt("t", tcp)
                wt("t", bankfree[tb_])
                ttp = em("t", lambda t, tb_=tb_: t.transpose(
                    bank(tb_)[:, 0:128], c_h, sb_id[:, :]), sm_t)
                tok_h_cons = ttp
                wt("v", ttp)
                tok_hT_done = em("v", lambda v, tb_=tb_, tt=tt: v.tensor_scalar(
                    c_hT[:, tt * 128:tt * 128 + 128], bank(tb_)[:, 0:128],
                    scalar1=sb_coef[:, :], scalar2=None, op0=ALU.mult), sm_v)
                bankfree[tb_] = tok_hT_done

            CM = Carver(C_MAIN_BASE, OS_BASE)
            c_wf = [CM.take(16 * 512, BF16).rearrange("p (c f) -> p c f", c=16) for _ in range(2)]
            c_x2s = [CM.take(8 * 512, F32).rearrange("p (c f) -> p c f", c=8) for _ in range(2)]

            tok_wfh_free = [None, None]
            tok_x2s_free = [None, None]
            tok_outs_free = [None, None]
            x2w_all = tok_x2w[-1]
            for ob in range(8):
                wb = ob % 2
                if ob == 0:
                    wt("g", tok_x_used)
                    wt("g", tok_xn2_cons)
                    wt("g", x2w_all)
                    wt("y", tok_x_used)
                    wt("y", tok_xn2_cons)
                tok_wfh = [None, None]
                for hf in range(2):
                    wt("g", tok_wfh_free[hf])
                    wt("g", tok_wfh_free[1])
                    tok_wfh[hf] = dma("g", c_wf[hf],
                                      bass.AP(d_wfT, ob * (32 * 512) + hf * (16 * 512),
                                              [[8 * 32 * 512, 128], [512, 16], [1, 512]]),
                                      "wf0" if hf == 0 else "wf1")
                wt("y", x2w_all)
                wt("y", tok_x2s_free[wb])
                tok_x2s = dma("y", c_x2s[wb], bass.AP(d_x2, ob * TOK * 512,
                                                      [[512, 128], [128 * 512, 8], [1, 512]]),
                              "x2r0" if wb == 0 else "x2r1")
                if ob == 0:
                    wt("t", tok_lb)
                    wt("t", tok_hT_done)
                tok_lora_mm = [None] * 8
                for hf in range(2):
                    wt("t", tok_wfh[hf])
                    for tt in range(8):
                        if hf == 0:
                            wt("t", bankfree[tt])
                        for dci in range(16):
                            dc = hf * 16 + dci
                            em("t", lambda t, tt=tt, hf=hf, dc=dc, dci=dci: t.matmul(
                                bank(tt), c_xn2F[dc][:, tt * 128:tt * 128 + 128],
                                c_wf[hf][:, dci, :], start=(dc == 0), stop=False))
                        if hf == 1:
                            tok_lora_mm[tt] = em("t", lambda t, tt=tt, ob=ob: t.matmul(
                                bank(tt), c_hT[:, tt * 128:tt * 128 + 128],
                                c_lb[:, ob * 512:(ob + 1) * 512],
                                start=False, stop=True), sm_t)
                        else:
                            if tt == 7:
                                tok_wfh_free[0] = em("t", lambda t: t.nop(), sm_t)
                tok_wfh_free[1] = tok_lora_mm[7]
                for tt in range(8):
                    oi = tt % 2
                    wt("v", tok_lora_mm[tt])
                    if ob == 0 and tt == 0:
                        wt("v", tok_fb_bc)
                    wt("v", tok_outs_free[oi])
                    t1 = em("v", lambda v, tt=tt, ob=ob, oi=oi: v.tensor_tensor(
                        c_outs[oi], bank(tt), c_biasbc[:, ob * 512:(ob + 1) * 512], ALU.add), sm_v)
                    bankfree[tt] = t1
                    wt("v", t1)
                    wt("v", tok_x2s)
                    t2 = em("v", lambda v, oi=oi, wb=wb, tt=tt: v.tensor_tensor(
                        c_outs[oi], c_outs[oi], c_x2s[wb][:, tt, :], ALU.add), sm_v)
                    wt("y", t2)
                    tw = dma("y", bass.AP(d_out, tt * 128 * D + ob * 512,
                                          [[D, 128], [1, 512]]), c_outs[oi],
                             "o0" if oi == 0 else "o1")
                    tok_outs_free[oi] = tw
                    if tt == 7:
                        tok_x2s_free[wb] = t2


        if run_c:
            _phase_c()

        for _sm in dma_sems.values():
            if _sm.n > 0:
                sched.wait("y", (_sm, _sm.n))

        with nc.Block() as block:
            block.tensor(lambda eng: [f(eng) for f in sched.ops["t"]])
            block.vector(lambda eng: [f(eng) for f in sched.ops["v"]])
            block.scalar(lambda eng: [f(eng) for f in sched.ops["s"]])
            block.gpsimd(lambda eng: [f(eng) for f in sched.ops["g"]])
            block.sync(lambda eng: [f(eng) for f in sched.ops["y"]])

    return nc


def _stage(inputs):
    g = lambda k: np.asarray(inputs[k], np.float32)
    X = g("L0_attn_hidden_size").reshape(S, D)
    Wq_c = g("L0_attn_Wq_c")
    bq_c = g("L0_attn_bq_c")
    Wq_up = g("L0_attn_Wq_up")
    bq_up = g("L0_attn_bq_up")
    Wkv_c = g("L0_attn_Wkv_c")
    bkv_c = g("L0_attn_bkv_c")
    Wkv_up = g("L0_attn_Wkv_up")
    bkv_up = g("L0_attn_bkv_up")
    Wo = g("L0_attn_Wo")
    bo = g("L0_attn_bo")
    lora_A = g("L0_ffn_lora_A")
    lora_B = g("L0_ffn_lora_B")
    Wf = g("L0_ffn_W")
    fb = g("L0_ffn_bias")
    lw = g("L0_ffn_lora_weights")

    C = np.ascontiguousarray

    PERM, KPERM, VPERM = [], [], []
    for p in range(NPAIR):
        h0, h1 = 2 * p, 2 * p + 1
        PERM += list(range(h0 * QH, h0 * QH + NOPE))
        PERM += list(range(h1 * QH, h1 * QH + NOPE))
        PERM += list(range(h0 * QH + NOPE, h0 * QH + QH))
        PERM += list(range(h1 * QH + NOPE, h1 * QH + QH))
        KPERM += list(range(h0 * 256, h0 * 256 + 128))
        KPERM += list(range(h1 * 256, h1 * 256 + 128))
        VPERM += list(range(h0 * 256 + 128, h0 * 256 + 256))
        VPERM += list(range(h1 * 256 + 128, h1 * 256 + 256))

    base = {}
    base["wqcT"] = C(Wq_c.T)
    base["bqc_t"] = C(bq_c.reshape(12, 128).T)
    wqupT = Wq_up[PERM, :].T
    base["wqup_til"] = C(wqupT.reshape(12, 128, NPAIR, 384).transpose(1, 2, 0, 3)
                         .reshape(128, NPAIR, 12 * 384))
    base["bqup_t"] = C(bq_up[PERM].reshape(48, 128).T)
    wkvcT = Wkv_c.T
    base["wkvc_til"] = C(wkvcT.reshape(32, 128, 576).transpose(1, 0, 2).reshape(128, 32 * 576))
    bkvc_p = np.zeros(640, np.float32)
    bkvc_p[:576] = bkv_c
    base["bkvc_t"] = C(bkvc_p.reshape(5, 128).T)
    wkKT = Wkv_up[KPERM, :].T
    base["wkK_til"] = C(wkKT.reshape(4, 128, NPAIR, 256).transpose(1, 2, 0, 3)
                        .reshape(128, NPAIR, 4 * 256))
    base["bkupK_t"] = C(bkv_up[KPERM].reshape(32, 128).T)
    wkVT = Wkv_up[VPERM, :].T
    base["wkV_til"] = C(wkVT.reshape(4, 128, NPAIR, 256).transpose(1, 2, 0, 3)
                        .reshape(128, NPAIR, 4 * 256))
    base["bkvV_cols"] = C(bkv_up[VPERM].reshape(32, 128).T)
    base["woT"] = C(Wo.T)
    base["bo_row"] = C(bo.reshape(1, D))
    WfT = Wf.T
    base["wffn_til"] = C(WfT.reshape(32, 128, 8, 512).transpose(1, 2, 0, 3)
                         .reshape(128, 8, 32 * 512))
    base["ffnb_row"] = C(fb.reshape(1, D))
    laT = lora_A.reshape(128, D).T
    base["loraA_til"] = C(laT.reshape(32, 128, 128).transpose(1, 0, 2).reshape(128, 32 * 128))
    base["loraBT"] = C(lora_B.transpose(0, 2, 1).reshape(128, D))
    base["lwrep"] = C(np.repeat(lw, 16)[:, None])
    base["ident"] = np.eye(128, dtype=np.float32)
    base["ones_row"] = np.ones((1, 128), np.float32)
    base["onecol_bf"] = np.ones((128, 1), ml_dtypes.bfloat16)

    in_maps = []
    for c in range(NCORE):
        m = dict(base)
        m["x_rot"] = C(np.concatenate([X[TOK * c:], X[:TOK * c]], axis=0))
        in_maps.append(m)
    return in_maps


def kernel(**inputs):
    in_maps = _stage(inputs)
    nc = build_nc(debug=_DEBUG)
    res = run_bass_kernel_spmd(nc, in_maps, core_ids=list(range(NCORE)))
    if _DEBUG:
        kernel._last = res
    out = np.concatenate([np.asarray(res.results[c]["out_c"]) for c in range(NCORE)], axis=0)
    return out.reshape(1, S, D).astype(np.float32)


# version 10
# speedup vs baseline: 1.0555x; 1.0307x over previous
import os
import math
import numpy as np
import ml_dtypes
import concourse.bass as bass
import concourse.mybir as mybir
from concourse.bass_utils import run_bass_kernel_spmd
from contextlib import ExitStack

F32 = mybir.dt.float32
BF16 = mybir.dt.bfloat16
AF = mybir.ActivationFunctionType
ALU = mybir.AluOpType
AX = mybir.AxisListType

NCORE = 8
S = 8192
D = 4096
TOK = 1024
NH = 32
NPAIR = 16
QL = 1536
KVL = 512
QH = 192
NOPE = 128
VH = 128
ISQ = 1.0 / math.sqrt(QH)
EPS = 1e-5

_DEBUG = int(os.environ.get("MK_DEBUG", "0"))
_NPAIR_RUN = int(os.environ.get("MK_NPAIR", str(NPAIR)))
_PHASE = int(os.environ.get("MK_PHASE", "3"))
_TBS = int(os.environ.get("MK_TBS", "16"))
_SUB = int(os.environ.get("MK_SUB", "15"))
_TPV = int(os.environ.get("MK_TPV", "0"))
_TTS = int(os.environ.get("MK_TTS", "4"))


class Sem:
    def __init__(self, handle, name):
        self.h = handle
        self.name = name
        self.n = 0


class Sched:
    ENGINES = ("t", "v", "s", "g", "y")

    def __init__(self):
        self.ops = {e: [] for e in self.ENGINES}
        self.sems = {}

    def sem(self, name, handle):
        sm = Sem(handle, name)
        self.sems[name] = sm
        return sm

    def emit(self, e, fn, sem=None, k=1):
        if sem is None:
            self.ops[e].append(fn)
            return None
        sem.n += k
        tag = sem.n

        def run(eng, fn=fn, s=sem.h, k=k):
            fn(eng).then_inc(s, k)

        self.ops[e].append(run)
        return (sem, tag)

    def wait(self, e, tok):
        if tok is None:
            return
        sem, tag = tok
        if tag <= 0:
            return
        self.ops[e].append(lambda eng, s=sem.h, t=tag: eng.wait_ge(s, t))

    def barrier(self):
        toks = [(sm, sm.n) for sm in self.sems.values() if sm.n > 0]
        for e in self.ENGINES:
            for tok in toks:
                self.wait(e, tok)


def build_nc(debug=0):
    nc = bass.Bass("TRN2", target_bir_lowering=False)

    d_x = nc.dram_tensor("x_rot", [S, D], F32, kind="ExternalInput")
    d_wqcT = nc.dram_tensor("wqcT", [D, QL], F32, kind="ExternalInput")
    d_bqc = nc.dram_tensor("bqc_t", [128, 12], F32, kind="ExternalInput")
    d_wqup = nc.dram_tensor("wqup_til", [128, NPAIR, 12 * 384], F32, kind="ExternalInput")
    d_bqup = nc.dram_tensor("bqup_t", [128, 48], F32, kind="ExternalInput")
    d_wkvc = nc.dram_tensor("wkvc_til", [128, 32 * 576], F32, kind="ExternalInput")
    d_bkvc = nc.dram_tensor("bkvc_t", [128, 5], F32, kind="ExternalInput")
    d_wkK = nc.dram_tensor("wkK_til", [128, NPAIR, 4 * 256], F32, kind="ExternalInput")
    d_bkK = nc.dram_tensor("bkupK_t", [128, 32], F32, kind="ExternalInput")
    d_wkV = nc.dram_tensor("wkV_til", [128, NPAIR, 4 * 256], F32, kind="ExternalInput")
    d_bkV = nc.dram_tensor("bkvV_cols", [128, 32], F32, kind="ExternalInput")
    d_woT = nc.dram_tensor("woT", [VH, D], F32, kind="ExternalInput")
    d_bo = nc.dram_tensor("bo_row", [1, D], F32, kind="ExternalInput")
    d_wfT = nc.dram_tensor("wffn_til", [128, 8, 32 * 512], F32, kind="ExternalInput")
    d_fb = nc.dram_tensor("ffnb_row", [1, D], F32, kind="ExternalInput")
    d_laT = nc.dram_tensor("loraA_til", [128, 32 * 128], F32, kind="ExternalInput")
    d_lbT = nc.dram_tensor("loraBT", [128, D], F32, kind="ExternalInput")
    d_lw = nc.dram_tensor("lwrep", [128, 1], F32, kind="ExternalInput")
    d_id = nc.dram_tensor("ident", [128, 128], F32, kind="ExternalInput")
    d_ones = nc.dram_tensor("ones_row", [1, 128], F32, kind="ExternalInput")
    d_onec = nc.dram_tensor("onecol_bf", [128, 1], BF16, kind="ExternalInput")

    d_kvF = nc.dram_tensor("kvF_dram", [576, S], BF16)
    d_qcF = nc.dram_tensor("qcF_dram", [2, 128, 12 * 512], BF16)
    d_x2 = nc.dram_tensor("x2_dram", [8, TOK, 512], F32)
    d_out = nc.dram_tensor("out_c", [TOK, D], F32, kind="ExternalOutput")

    dbg = {}
    if debug >= 1:
        dbg["kvF"] = nc.dram_tensor("dbg_kvF", [576, S], BF16, kind="ExternalOutput")
        dbg["qcF"] = nc.dram_tensor("dbg_qcF", [2, 128, 12 * 512], BF16, kind="ExternalOutput")
    if debug >= 2:
        dbg["K0"] = nc.dram_tensor("dbg_K0", [NOPE, S], BF16, kind="ExternalOutput")
        dbg["K1"] = nc.dram_tensor("dbg_K1", [NOPE, S], BF16, kind="ExternalOutput")
        dbg["V0"] = nc.dram_tensor("dbg_V0", [128, 64 * 256], BF16, kind="ExternalOutput")
        dbg["Q0"] = nc.dram_tensor("dbg_Q0", [128, 3 * TOK], BF16, kind="ExternalOutput")
        dbg["osum"] = nc.dram_tensor("dbg_osum", [VH, TOK], F32, kind="ExternalOutput")
    if debug >= 3:
        dbg["x2"] = nc.dram_tensor("dbg_x2", [8, TOK, 512], F32, kind="ExternalOutput")

    sched = Sched()
    em, wt = sched.emit, sched.wait

    with ExitStack() as ctx:
        E = ctx.enter_context

        sb_id = E(nc.sbuf_tensor("sb_id", [128, 128], F32))
        sb_ones = E(nc.sbuf_tensor("sb_ones", [1, 128], F32))
        sb_onec = E(nc.sbuf_tensor("sb_onec", [128, 1], BF16))
        sb_bqc = E(nc.sbuf_tensor("sb_bqc", [128, 12], F32))
        sb_bqup = E(nc.sbuf_tensor("sb_bqup", [128, 48], F32))
        sb_bkvc = E(nc.sbuf_tensor("sb_bkvc", [128, 5], F32))
        sb_bkK = E(nc.sbuf_tensor("sb_bkK", [128, 32], F32))
        sb_bkV = E(nc.sbuf_tensor("sb_bkV", [128, 32], F32))
        sb_lw = E(nc.sbuf_tensor("sb_lw", [128, 1], F32))
        sb_coef = E(nc.sbuf_tensor("sb_coef", [128, 1], F32))
        sb_bvsum = E(nc.sbuf_tensor("sb_bvsum", [128, 1], F32))
        st_sum = E(nc.sbuf_tensor("st_sum", [128, 2], F32))
        st_sqs = E(nc.sbuf_tensor("st_sqs", [128, 2], F32))
        st_nm = E(nc.sbuf_tensor("st_nm", [128, 2], F32))
        st_m2 = E(nc.sbuf_tensor("st_m2", [128, 2], F32))
        st_var = E(nc.sbuf_tensor("st_var", [128, 2], F32))
        st_sq = E(nc.sbuf_tensor("st_sq", [128, 2], F32))
        st_rs = E(nc.sbuf_tensor("st_rs", [128, 2], F32))
        sb_stab = E(nc.sbuf_tensor("sb_stab", [128, 8], F32))
        sb_qmax = E(nc.sbuf_tensor("sb_qmax", [128, 1], F32))
        sb_kmax = E(nc.sbuf_tensor("sb_kmax", [128, 1], F32))
        sb_qm = E(nc.sbuf_tensor("sb_qm", [1, 1], F32))
        sb_km = E(nc.sbuf_tensor("sb_km", [1, 1], F32))
        sb_cqk = E(nc.sbuf_tensor("sb_cqk", [1, 1], F32))
        sb_negc_col = E(nc.sbuf_tensor("sb_negc_col", [128, 1], F32))
        sb_recip0 = E(nc.sbuf_tensor("sb_recip0", [1, 512], F32))
        sb_recip1 = E(nc.sbuf_tensor("sb_recip1", [1, 512], F32))

        AR_N = 51456
        OS_BASE = AR_N - 1536
        arena = E(nc.sbuf_tensor("arena", [128, AR_N], F32))

        class Carver:
            def __init__(self, base=0, limit=AR_N):
                self.off = base
                self.limit = limit

            def take(self, nfree, dtype):
                ef32 = nfree if dtype == F32 else (nfree + 1) // 2
                ef32 = (ef32 + 15) & ~15
                assert self.off + ef32 <= self.limit, \
                    f"arena overflow {self.off}+{ef32} > {self.limit}"
                ap = arena[:, self.off:self.off + ef32]
                self.off += ef32
                if dtype == BF16:
                    ap = ap.bitcast(BF16)[:, 0:nfree]
                return ap

        ps = E(nc.psum_tensor("ps", [128, 4096], F32))

        def bank(b):
            return ps[:, b * 512:(b + 1) * 512]

        bankfree = [None] * 8

        sm_gc = sched.sem("gc", E(nc.semaphore("sm_gc")))
        sm_t = sched.sem("t", E(nc.semaphore("sm_t")))
        sm_v = sched.sem("v", E(nc.semaphore("sm_v")))
        sm_s = sched.sem("s", E(nc.semaphore("sm_s")))
        dma_sems = {}
        for nm in ("c", "x0", "x1", "x2b", "wkvc", "wq0", "wq1", "kv0", "kv1",
                   "qc0", "qc1", "kvn", "qch0", "qch1", "wkK", "wkV", "wqup",
                   "cw", "cwy", "wf0", "wf1", "x2w", "x2r0", "x2r1", "o0", "o1", "st"):
            dma_sems[nm] = sched.sem(nm, E(nc.semaphore("sm_" + nm)))
        sm_st = dma_sems["st"]

        def dma(e, out, in_, sem):
            if isinstance(sem, str):
                sem = dma_sems[sem]
            return em(e, lambda eng, o=out, i=in_: eng.dma_start(out=o, in_=i), sem, 16)

        tok_consts = None
        for (dst, src) in [
            (sb_id[:, :], d_id.ap()), (sb_ones[:, :], d_ones.ap()),
            (sb_onec[:, :], d_onec.ap()), (sb_bqc[:, :], d_bqc.ap()),
            (sb_bqup[:, :], d_bqup.ap()), (sb_bkvc[:, :], d_bkvc.ap()),
            (sb_bkK[:, :], d_bkK.ap()), (sb_bkV[:, :], d_bkV.ap()),
            (sb_lw[:, :], d_lw.ap()),
        ]:
            tok_consts = dma("y", dst, src, "c")
        wt("v", tok_consts)
        em("v", lambda v: v.tensor_scalar(
            sb_coef[:, :], sb_lw[:, :], scalar1=2.0, scalar2=None, op0=ALU.mult), sm_v)
        tok_bvsum = em("v", lambda v: v.reduce_sum(sb_bvsum[:, :], sb_bkV[:, :], AX.X), sm_v)

        ln_rot = [0]
        last_sq_tok = [None]
        slot_sum_free = [None, None]
        slot_sqs_free = [None, None]
        slot_sq_free = [None, None]

        def emit_ln_stats(xbuf, xsqbuf, tok_loaded_s, extra_s_wait=None):
            r = ln_rot[0] % 2
            ln_rot[0] += 1
            sl = slice(r, r + 1)
            wt("s", tok_loaded_s)
            if extra_s_wait is not None:
                wt("s", extra_s_wait)
            wt("s", slot_sum_free[r])
            wt("s", last_sq_tok[0])
            t1 = em("s", lambda s, sl=sl, xb=xbuf, xo=xsqbuf: s.activation(
                xo, xb, AF.Copy, accum_out=st_sum[:, sl]), sm_s)
            wt("s", slot_sqs_free[r])
            wt("s", t1)
            t2 = em("s", lambda s, sl=sl, xb=xbuf, xq=xsqbuf: s.activation(
                xq, xb, AF.Square, accum_out=st_sqs[:, sl]), sm_s)
            last_sq_tok[0] = t2
            return r, t1, t2

        def emit_ln_finish(xbuf, xnbuf, r, t1, t2, tok_loaded_v, extra_v_wait=None):
            sl = slice(r, r + 1)
            if extra_v_wait is not None:
                wt("v", extra_v_wait)
            wt("v", t1)
            t3 = em("v", lambda v, sl=sl: v.tensor_scalar(
                st_nm[:, sl], st_sum[:, sl], scalar1=-1.0 / D, scalar2=None, op0=ALU.mult), sm_v)
            slot_sum_free[r] = t3
            wt("v", t3)
            t4 = em("v", lambda v, sl=sl: v.tensor_tensor(
                st_m2[:, sl], st_nm[:, sl], st_nm[:, sl], ALU.mult), sm_v)
            wt("v", t2)
            wt("v", t4)
            t5 = em("v", lambda v, sl=sl: v.tensor_scalar(
                st_var[:, sl], st_sqs[:, sl], scalar1=1.0 / D, scalar2=st_m2[:, sl],
                op0=ALU.mult, op1=ALU.subtract), sm_v)
            slot_sqs_free[r] = t5
            wt("v", t5)
            t6 = em("v", lambda v, sl=sl: v.tensor_scalar(
                st_var[:, sl], st_var[:, sl], scalar1=EPS, scalar2=None, op0=ALU.add), sm_v)
            wt("s", t6)
            wt("s", slot_sq_free[r])
            t7 = em("s", lambda s, sl=sl: s.activation(st_sq[:, sl], st_var[:, sl], AF.Sqrt), sm_s)
            wt("v", t7)
            t8 = em("v", lambda v, sl=sl: v.reciprocal(st_rs[:, sl], st_sq[:, sl]), sm_v)
            slot_sq_free[r] = t8
            wt("v", t8)
            wt("v", tok_loaded_v)
            t9 = em("v", lambda v, sl=sl, xb=xbuf, xo=xnbuf: v.tensor_scalar(
                xo, xb, scalar1=st_nm[:, sl], scalar2=st_rs[:, sl],
                op0=ALU.add, op1=ALU.mult), sm_v)
            return t9

        def emit_ln(xbuf, xnbuf, xsqbuf, tok_loaded_v, tok_loaded_s,
                    extra_v_wait=None, extra_s_wait=None):
            r, t1, t2 = emit_ln_stats(xbuf, xsqbuf, tok_loaded_s, extra_s_wait)
            return emit_ln_finish(xbuf, xnbuf, r, t1, t2, tok_loaded_v, extra_v_wait)

        A = Carver()
        a_x = [A.take(D, F32) for _ in range(2)]
        a_xsq = A.take(D, F32)
        a_xn = [A.take(D, F32) for _ in range(2)]
        a_xnF = [[A.take(512, BF16) for _ in range(32)] for _ in range(2)]
        a_wkvc = A.take(32 * 576, BF16).rearrange("p (c f) -> p c f", c=32)
        a_wqct = [A.take(2 * QL, BF16).rearrange("p (c f) -> p c f", c=2) for _ in range(2)]
        a_kvst = [A.take(512, BF16) for _ in range(2)]
        a_qcst = [A.take(512, BF16) for _ in range(2)]

        tok_wkvc = dma("g", a_wkvc, d_wkvc.ap().rearrange("p (c f) -> p c f", c=32), "wkvc")

        tok_x_free = [None, None]
        tok_xld = [None, None]
        stats_tok = [None, None]
        tok_xn_cons = [None, None]
        tok_xnF_ready = [[None] * 32, [None] * 32]
        tok_set_free = [None, None]
        tok_kv_last = [None] * 5
        tok_kvst_free = [None, None]
        tok_qcst_free = [None, None]
        qc_buf_free = [None, None]
        tok_qc_last = [None] * 6

        def gen_kv_mms(tb):
            st = tb % 2
            if tb == 0:
                wt("t", tok_wkvc)
            for dc in range(32):
                wt("t", tok_xnF_ready[st][dc])
            for f in range(5):
                rows = 128 if f < 4 else 64
                wt("t", bankfree[f])
                for dc in range(32):
                    tok_kv_last[f] = em("t", lambda t, f=f, dc=dc, rows=rows, st=st: t.matmul(
                        bank(f)[0:rows, 0:512],
                        a_wkvc[:, dc, f * 128:f * 128 + rows],
                        a_xnF[st][dc],
                        start=(dc == 0), stop=(dc == 31)),
                        sm_t if dc == 31 else None)
                    yield

        def flush_kv(tb):
            for f in range(5):
                rows = 128 if f < 4 else 64
                kb = f % 2
                wt("v", tok_kv_last[f])
                wt("v", tok_kvst_free[kb])
                tok_ev = em("v", lambda v, f=f, kb=kb, rows=rows: v.tensor_scalar(
                    a_kvst[kb][0:rows, :], bank(f)[0:rows, 0:512],
                    scalar1=sb_bkvc[0:rows, f:f + 1], scalar2=None, op0=ALU.add), sm_v)
                bankfree[f] = tok_ev
                wt("y", tok_ev)
                tok_st = dma("y", bass.AP(d_kvF, f * 128 * S + tb * 512,
                                          [[S, rows], [1, 512]]),
                             a_kvst[kb][0:rows, :], "kv0" if kb == 0 else "kv1")
                if debug >= 1:
                    tok_st = dma("y", bass.AP(dbg["kvF"], f * 128 * S + tb * 512,
                                              [[S, rows], [1, 512]]),
                                 a_kvst[kb][0:rows, :], "kv0" if kb == 0 else "kv1")
                tok_kvst_free[kb] = tok_st

        kv_gen = None
        for tb in range(16):
            st = tb % 2
            for tt in range(4):
                tile = tb * 4 + tt
                bj = tile % 2
                if tile == 0:
                    tok_xld[0] = dma("y", a_x[0], d_x[0:128, :], "x0")
                    stats_tok[0] = emit_ln_stats(a_x[0], a_xsq, tok_xld[0])
                if tile + 1 < 64:
                    t1_ = tile + 1
                    wt("y", tok_x_free[t1_ % 2])
                    tok_xld[t1_ % 2] = dma("y", a_x[t1_ % 2],
                                           d_x[t1_ * 128:(t1_ + 1) * 128, :],
                                           "x0" if t1_ % 2 == 0 else "x1")
                    stats_tok[t1_ % 2] = emit_ln_stats(a_x[t1_ % 2], a_xsq,
                                                       tok_xld[t1_ % 2])
                r_, ts1, ts2 = stats_tok[tile % 2]
                tok_xn = emit_ln_finish(a_x[bj], a_xn[bj], r_, ts1, ts2,
                                        tok_xld[bj], extra_v_wait=tok_xn_cons[bj])
                tok_x_free[bj] = tok_xn
                wt("t", tok_xn)
                last_tp = None
                for dc in range(32):
                    b = 5 + (dc % 3)
                    wt("t", bankfree[b])
                    tok_tp = em("t", lambda t, b=b, xb=a_xn[bj], dc=dc: t.transpose(
                        bank(b)[:, 0:128], xb[:, dc * 128:dc * 128 + 128], sb_id[:, :]), sm_t)
                    last_tp = tok_tp
                    if kv_gen is not None:
                        next(kv_gen, None)
                        next(kv_gen, None)
                    eng = "s" if dc % 2 == 0 else "v"
                    wt(eng, tok_tp)
                    if tt == 0 and dc == 0:
                        wt("s", tok_set_free[st])
                        wt("v", tok_set_free[st])
                    if eng == "s":
                        tok_ev = em(eng, lambda e, b=b, dc=dc, tt=tt, st=st: e.copy(
                            a_xnF[st][dc][:, tt * 128:tt * 128 + 128], bank(b)[:, 0:128]), sm_s)
                    else:
                        tok_ev = em(eng, lambda e, b=b, dc=dc, tt=tt, st=st: e.tensor_copy(
                            a_xnF[st][dc][:, tt * 128:tt * 128 + 128], bank(b)[:, 0:128]), sm_v)
                    bankfree[b] = tok_ev
                    if tt == 3:
                        tok_xnF_ready[st][dc] = tok_ev
                tok_xn_cons[bj] = last_tp
            if kv_gen is not None:
                for _ in kv_gen:
                    pass
                flush_kv(tb - 1)
                tok_set_free[(tb - 1) % 2] = tok_kv_last[4]
            if tb < 2:
                for dc in range(32):
                    wt("t", tok_xnF_ready[st][dc])
                for pas in range(2):
                    tok_mm = None
                    for dcg in range(16):
                        gb = dcg % 2
                        wt("g", qc_buf_free[gb])
                        tok_w = dma("g", a_wqct[gb],
                                    bass.AP(d_wqcT, dcg * 2 * 128 * QL,
                                            [[QL, 128], [128 * QL, 2], [1, QL]]),
                                    "wq0" if gb == 0 else "wq1")
                        wt("t", tok_w)
                        for di in range(2):
                            dc = dcg * 2 + di
                            for f6 in range(6):
                                fg = pas * 6 + f6
                                if dc == 0:
                                    wt("t", bankfree[f6])
                                tok_mm = em("t", lambda t, f6=f6, fg=fg, gb=gb, dc=dc, di=di, st=st: t.matmul(
                                    bank(f6)[:, 0:512],
                                    a_wqct[gb][:, di, fg * 128:fg * 128 + 128],
                                    a_xnF[st][dc],
                                    start=(dc == 0), stop=(dc == 31)),
                                    sm_t if (f6 == 5 or dc == 31) else None)
                                if dc == 31:
                                    tok_qc_last[f6] = tok_mm
                        qc_buf_free[gb] = tok_mm
                    for f6 in range(6):
                        fg = pas * 6 + f6
                        qb_ = f6 % 2
                        wt("v", tok_qc_last[f6])
                        wt("v", tok_qcst_free[qb_])
                        tok_ev = em("v", lambda v, f6=f6, fg=fg, qb_=qb_: v.tensor_scalar(
                            a_qcst[qb_], bank(f6)[:, 0:512],
                            scalar1=sb_bqc[:, fg:fg + 1], scalar2=None, op0=ALU.add), sm_v)
                        bankfree[f6] = tok_ev
                        wt("y", tok_ev)
                        tok_st = dma("y", bass.AP(d_qcF, tb * (128 * 12 * 512) + fg * 512,
                                                  [[12 * 512, 128], [1, 512]]),
                                     a_qcst[qb_], "qc0" if qb_ == 0 else "qc1")
                        if debug >= 1:
                            tok_st = dma("y", bass.AP(dbg["qcF"], tb * (128 * 12 * 512) + fg * 512,
                                                      [[12 * 512, 128], [1, 512]]),
                                         a_qcst[qb_], "qc0" if qb_ == 0 else "qc1")
                        tok_qcst_free[qb_] = tok_st
            kv_gen = gen_kv_mms(tb)
        for _ in kv_gen:
            pass
        flush_kv(15)
        tok_set_free[15 % 2] = tok_kv_last[4]

        run_b = _PHASE >= 2
        run_c = _PHASE >= 3
        sched.barrier()
        B = Carver(0, OS_BASE)
        b_kvn = [B.take(S, BF16) for _ in range(4)]
        b_rope = B.take(S, BF16)
        b_K = [B.take(S, BF16) for _ in range(2)]
        b_V = B.take(64 * 256, BF16)
        b_Q = [B.take(TOK, BF16) for _ in range(3)]
        b_P = [[B.take(512, BF16) for _ in range(3)] for _ in range(2)]
        b_qch = B.take(12 * 512, BF16).rearrange("p (c f) -> p c f", c=12)
        b_wqup = B.take(12 * 384, BF16).rearrange("p (c f) -> p c f", c=12)
        b_wkK = B.take(4 * 256, BF16).rearrange("p (c f) -> p c f", c=4)
        b_wkV = B.take(4 * 256, BF16).rearrange("p (c f) -> p c f", c=4)
        b_bcast = B.take(512, F32)
        b_otmp = B.take(512, F32)
        b_Ps = [[B.take(512, BF16) for _ in range(2)] for _ in range(2)]
        b_osum = arena[:, OS_BASE:OS_BASE + 1024]
        b_osbf = arena[:, OS_BASE + 1024:OS_BASE + 1536].bitcast(BF16)

        for lat in range(4):
            dma("y", b_kvn[lat],
                bass.AP(d_kvF, lat * 128 * S, [[S, 128], [1, S]]), "kvn")
        dma("y", b_rope[0:64, :], bass.AP(d_kvF, 512 * S, [[S, 64], [1, S]]), "kvn")
        tok_kvn_all = dma("y", b_rope[64:128, :],
                          bass.AP(d_kvF, 512 * S, [[S, 64], [1, S]]), "kvn")
        tok_rope_l = tok_rope_h = tok_kvn_all
        tok_osum0 = em("v", lambda v: v.memset(b_osum, 0.0), sm_v)

        wt("v", tok_rope_h)
        wt("v", tok_rope_l)
        tok_ropemax = em("v", lambda v: v.tensor_reduce(
            sb_stab[:, 5:6], b_rope, AX.X, ALU.max, apply_absolute_value=True), sm_v)

        tok_w_free = {"q": None, "k": None, "v": None}
        tok_qch_free = [None, None]
        tok_K_free = [None, None]
        tok_V_free = None
        tok_Q_free = None
        tok_osum_last = tok_osum0
        tok_exp_prev_pair = None
        tok_gq_prev = None
        tok_gk_prev = None
        tok_tc1_prev = None
        tok_bcast_prev = None

        pend = [None]
        texp = [(None, None)] * 3
        tps_tok = [[None, None], [None, None]]
        tps_free = [[None, None], [None, None]]
        p_read = [[None] * 3, [None] * 3]
        tok_bc_prev = [None, None]
        tok_osum_chain = [tok_osum0]
        flush_fn = [lambda: None]

        def flush_epi_outer():
            flush_fn[0]()
        npair_run = min(_NPAIR_RUN, NPAIR) if run_b else 0
        for p in range(npair_run):
            wt("g", tok_w_free["k"])
            tok_wk = dma("g", b_wkK, bass.AP(d_wkK, p * (4 * 256),
                                             [[NPAIR * 4 * 256, 128], [256, 4], [1, 256]]), "wkK")
            wt("g", tok_w_free["v"])
            tok_wv = dma("g", b_wkV, bass.AP(d_wkV, p * (4 * 256),
                                             [[NPAIR * 4 * 256, 128], [256, 4], [1, 256]]), "wkV")
            wt("g", tok_w_free["q"])
            tok_wq = dma("g", b_wqup, bass.AP(d_wqup, p * (12 * 384),
                                              [[NPAIR * 12 * 384, 128], [384, 12], [1, 384]]), "wqup")

            prod_rot = [0]

            def prod_bank():
                b = prod_rot[0] % 4
                prod_rot[0] += 1
                return b

            wt("t", tok_wq)
            wt("v", tok_Q_free)
            tok_Q_ready = None
            for half in range(2):
                wt("y", tok_qch_free[1] if half == 0 else tok_qch_free[0])
                tok_qc = dma("y", b_qch, bass.AP(d_qcF, half * (128 * 12 * 512),
                                                 [[12 * 512, 128], [512, 12], [1, 512]]),
                             "qch0" if half == 0 else "qch1")
                wt("t", tok_qc)
                for qch in range(3):
                    b = prod_bank()
                    wt("t", bankfree[b])
                    tok_mm = None
                    for ql in range(12):
                        tok_mm = em("t", lambda t, b=b, qch=qch, ql=ql: t.matmul(
                            bank(b),
                            b_wqup[:, ql, qch * 128:qch * 128 + 128],
                            b_qch[:, ql, :],
                            start=(ql == 0), stop=(ql == 11)),
                            sm_t if ql == 11 else None)
                    tok_qch_free[half] = tok_mm
                    wt("v", tok_mm)
                    tok_Q_ready = em("v", lambda v, b=b, qch=qch, half=half, p=p: v.tensor_scalar(
                        b_Q[qch][:, half * 512:half * 512 + 512], bank(b),
                        scalar1=sb_bqup[:, 3 * p + qch:3 * p + qch + 1], scalar2=ISQ,
                        op0=ALU.add, op1=ALU.mult), sm_v)
                    bankfree[b] = tok_Q_ready

            wt("v", tok_Q_ready)
            wt("v", tok_gq_prev)
            tql = None
            for qch in range(3):
                tql = em("v", lambda v, qch=qch: v.tensor_reduce(
                    sb_stab[:, qch:qch + 1], b_Q[qch], AX.X, ALU.max,
                    apply_absolute_value=True), sm_v)
            wt("v", tql)
            tqm = em("v", lambda v: v.tensor_reduce(
                sb_qmax[:, :], sb_stab[:, 0:3], AX.X, ALU.max), sm_v)
            wt("t", tok_wk)
            if p == 0:
                wt("t", tok_kvn_all)
            tok_K_ready = [None, None]
            for h in range(2):
                wt("v", tok_K_free[h])
                for tk in range(16):
                    b = prod_bank()
                    wt("t", bankfree[b])
                    tok_mm = None
                    for lat in range(4):
                        tok_mm = em("t", lambda t, b=b, h=h, tk=tk, lat=lat: t.matmul(
                            bank(b),
                            b_wkK[:, lat, h * 128:h * 128 + 128],
                            b_kvn[lat][:, tk * 512:tk * 512 + 512],
                            start=(lat == 0), stop=(lat == 3)),
                            sm_t if lat == 3 else None)
                    wt("v", tok_mm)
                    tok_ev = em("v", lambda v, b=b, h=h, tk=tk, p=p: v.tensor_scalar(
                        b_K[h][:, tk * 512:tk * 512 + 512], bank(b),
                        scalar1=sb_bkK[:, 2 * p + h:2 * p + h + 1], scalar2=None,
                        op0=ALU.add), sm_v)
                    bankfree[b] = tok_ev
                tok_K_ready[h] = tok_ev

            wt("v", tok_K_ready[0])
            wt("v", tok_K_ready[1])
            wt("v", tok_gk_prev)
            em("v", lambda v: v.tensor_reduce(
                sb_stab[:, 3:4], b_K[0], AX.X, ALU.max, apply_absolute_value=True), sm_v)
            tk1 = em("v", lambda v: v.tensor_reduce(
                sb_stab[:, 4:5], b_K[1], AX.X, ALU.max, apply_absolute_value=True), sm_v)
            wt("v", tk1)
            wt("v", tok_ropemax)
            tkm = em("v", lambda v: v.tensor_reduce(
                sb_kmax[:, :], sb_stab[:, 3:6], AX.X, ALU.max), sm_v)
            wt("t", tok_wv)
            wt("s", tok_V_free)
            tok_V_ready = None
            for tk in range(64):
                b = prod_bank()
                wt("t", bankfree[b])
                tok_mm = None
                for lat in range(4):
                    tok_mm = em("t", lambda t, b=b, tk=tk, lat=lat: t.matmul(
                        bank(b)[:, 0:256],
                        b_kvn[lat][:, tk * 128:tk * 128 + 128],
                        b_wkV[:, lat, :],
                        start=(lat == 0), stop=(lat == 3)),
                        sm_t if lat == 3 else None)
                wt("s", tok_mm)
                tok_V_ready = em("s", lambda e, b=b, tk=tk: e.copy(
                    b_V[:, tk * 256:tk * 256 + 256], bank(b)[:, 0:256]), sm_s)
                bankfree[b] = tok_V_ready

            wt("g", tqm)
            wt("g", tkm)
            wt("g", tok_tc1_prev)
            tok_gq_prev = em("g", lambda g: g.tensor_reduce(
                sb_qm[:, :], sb_qmax[:, :], AX.C, ALU.max), sm_gc)
            tok_gk_prev = em("g", lambda g: g.tensor_reduce(
                sb_km[:, :], sb_kmax[:, :], AX.C, ALU.max), sm_gc)
            wt("v", tok_gq_prev)
            wt("v", tok_gk_prev)
            tok_tc1_prev = em("v", lambda v: v.tensor_tensor(
                sb_cqk[:, :], sb_qm[:, :], sb_km[:, :], ALU.mult), sm_v)
            wt("v", tok_tc1_prev)
            tc2 = em("v", lambda v: v.tensor_scalar(
                sb_cqk[:, :], sb_cqk[:, :], scalar1=-float(QH), scalar2=-45.0,
                op0=ALU.mult, op1=ALU.max), sm_v)
            flush_epi_outer()
            wt("t", tc2)
            wt("t", bankfree[4])
            tbc = em("t", lambda t: t.matmul(
                bank(4)[:, 0:1], sb_ones[:, :], sb_cqk[:, :], start=True, stop=True), sm_t)
            wt("v", tbc)
            wt("v", tok_exp_prev_pair)
            tnc = em("v", lambda v: v.tensor_copy(sb_negc_col[:, :], bank(4)[:, 0:1]), sm_v)
            bankfree[4] = tnc
            wt("t", tok_V_ready)
            wt("s", tnc)
            tok_score_last = None

            def emit_scores(qb, kc):
                par = kc % 2
                b0, b1 = par, 2 + par
                wt("t", bankfree[b0])
                wt("t", bankfree[b1])
                em("t", lambda t, b0=b0, kc=kc, qb=qb: t.matmul(
                    bank(b0), b_K[0][:, kc * 128:kc * 128 + 128],
                    b_Q[0][:, qb * 512:qb * 512 + 512], start=True, stop=False))
                em("t", lambda t, b1=b1, kc=kc, qb=qb: t.matmul(
                    bank(b1), b_K[1][:, kc * 128:kc * 128 + 128],
                    b_Q[1][:, qb * 512:qb * 512 + 512], start=True, stop=False))
                em("t", lambda t, b0=b0, kc=kc, qb=qb: t.matmul(
                    bank(b0), b_rope[0:64, kc * 128:kc * 128 + 128],
                    b_Q[2][0:64, qb * 512:qb * 512 + 512],
                    start=False, stop=True, tile_position=(0, 0)))
                tok_sc = em("t", lambda t, b1=b1, kc=kc, qb=qb: t.matmul(
                    bank(b1), b_rope[64:128, kc * 128:kc * 128 + 128],
                    b_Q[2][64:128, qb * 512:qb * 512 + 512],
                    start=False, stop=True, tile_position=(64, 0)), sm_t)
                wt("s", tok_sc)
                wt("s", p_read[0][kc % 3])
                wt("s", p_read[1][kc % 3])
                te0 = em("s", lambda s, b0=b0, kc=kc: s.activation(
                    b_P[0][kc % 3], bank(b0), AF.Exp, bias=sb_negc_col[:, :]), sm_s)
                te1 = em("s", lambda s, b1=b1, kc=kc: s.activation(
                    b_P[1][kc % 3], bank(b1), AF.Exp, bias=sb_negc_col[:, :]), sm_s)
                bankfree[b0] = te0
                bankfree[b1] = te1
                texp[kc % 3] = (te0, te1)
                if kc % 2 == 1:
                    for h, te in ((0, te0), (1, te1)):
                        wt("v", te)
                        wt("v", tps_free[h][(kc // 2) % 2])
                        tok_add = em("v", lambda v, h=h, kc=kc: v.tensor_tensor(
                            b_Ps[h][(kc // 2) % 2], b_P[h][(kc - 1) % 3],
                            b_P[h][kc % 3], ALU.add), sm_v)
                        tps_tok[h][(kc // 2) % 2] = tok_add
                        p_read[h][(kc - 1) % 3] = tok_add
                        p_read[h][kc % 3] = tok_add
                return tok_sc, te1

            def emit_pv(qb, kc, epi_banks_wait):
                if epi_banks_wait:
                    for ob in (4, 5):
                        wt("t", bankfree[ob])
                wt("t", texp[kc % 3][0])
                wt("t", texp[kc % 3][1])
                em("t", lambda t, kc=kc: t.matmul(
                    bank(4), b_V[:, kc * 256:kc * 256 + 128], b_P[0][kc % 3],
                    start=(kc == 0), stop=(kc == 63)))
                tok = em("t", lambda t, kc=kc: t.matmul(
                    bank(5), b_V[:, kc * 256 + 128:kc * 256 + 256], b_P[1][kc % 3],
                    start=(kc == 0), stop=(kc == 63)),
                    sm_t if kc == 63 else None)
                if kc % 2 == 1:
                    j = (kc // 2) % 2
                    if kc == 1:
                        for ob in (6, 7):
                            wt("t", bankfree[ob])
                    for h in range(2):
                        wt("t", tps_tok[h][j])
                        tok = em("t", lambda t, kc=kc, h=h, j=j: t.matmul(
                            bank(6 + h)[0:1, :], sb_onec[:, :], b_Ps[h][j],
                            start=(kc == 1), stop=(kc == 63)),
                            sm_t if kc == 63 else None)
                        tps_free[h][j] = tok if kc == 63 else None
                return tok

            def emit_epi_v1(qb, tok_pv):
                trs = []
                for h in range(2):
                    wt("v", tok_pv)
                    wt("v", tok_bc_prev[h])
                    trs.append(em("v", lambda v, h=h: v.reciprocal(
                        (sb_recip0 if h == 0 else sb_recip1)[0:1, :],
                        bank(6 + h)[0:1, 0:512]), sm_v))
                    bankfree[6 + h] = trs[h]
                return trs

            def flush_epi():
                if not pend[0]:
                    return
                qb, trs = pend[0]
                pend[0] = None
                for h in range(2):
                    bcb = 1 + 2 * h
                    wt("t", trs[h])
                    wt("t", bankfree[bcb])
                    tok_bc = em("t", lambda t, bcb=bcb, h=h: t.matmul(
                        bank(bcb), sb_ones[:, :],
                        (sb_recip0 if h == 0 else sb_recip1)[0:1, :],
                        start=True, stop=True), sm_t)
                    tok_bc_prev[h] = tok_bc
                    wt("v", tok_bc)
                    tcp = em("v", lambda v, bcb=bcb: v.tensor_copy(
                        b_bcast, bank(bcb)), sm_v)
                    bankfree[bcb] = tcp
                    wt("v", tcp)
                    tmu = em("v", lambda v, h=h: v.tensor_tensor(
                        b_otmp, bank(4 + h), b_bcast, ALU.mult), sm_v)
                    bankfree[4 + h] = tmu
                    wt("v", tmu)
                    wt("v", tok_osum_chain[0])
                    tok_osum_chain[0] = em("v", lambda v, qb=qb: v.tensor_tensor(
                        b_osum[:, qb * 512:qb * 512 + 512],
                        b_osum[:, qb * 512:qb * 512 + 512], b_otmp, ALU.add), sm_v)

            flush_fn[0] = flush_epi
            for qb in range(2):
                tok_pv = None
                for kc in range(64):
                    tok_sc, te1 = emit_scores(qb, kc)
                    tok_score_last = tok_sc
                    tok_exp_prev_pair = te1
                    if kc == 2:
                        flush_epi()
                    if kc >= 2:
                        tok_pv = emit_pv(qb, kc - 2, epi_banks_wait=(kc == 2))
                tok_pv = emit_pv(qb, 62, False)
                tok_pv = emit_pv(qb, 63, False)
                tok_V_free = tok_pv
                trs = emit_epi_v1(qb, tok_pv)
                pend[0] = (qb, trs)
            tok_K_free = [tok_score_last, tok_score_last]
            tok_Q_free = tok_score_last
            tok_w_free = {"q": tok_Q_ready, "k": tok_K_ready[1], "v": tok_V_ready}

        flush_fn[0]()
        wt("v", tok_bvsum)
        wt("v", tok_osum_chain[0])
        tb1 = em("v", lambda v: v.tensor_scalar(
            b_osum, b_osum, scalar1=sb_bvsum[:, :], scalar2=None, op0=ALU.add), sm_v)
        wt("v", tb1)
        tok_osbf = em("v", lambda v: v.tensor_copy(b_osbf, b_osum), sm_v)

        if debug >= 2:
            wt("y", tok_osbf)
            dma("y", dbg["K0"].ap(), b_K[0], "cwy")
            dma("y", dbg["K1"].ap(), b_K[1], "cwy")
            dma("y", dbg["V0"].ap(), b_V, "cwy")
            for qch in range(3):
                dma("y", bass.AP(dbg["Q0"], qch * TOK, [[3 * TOK, 128], [1, TOK]]), b_Q[qch], "cwy")
            dma("y", dbg["osum"].ap(), b_osum, "cwy")

        def _phase_c():
            sched.barrier()
            C = Carver(0, OS_BASE)
            c_xn2F = [C.take(TOK, BF16) for _ in range(32)]
            c_rows = C.take(D, F32)
            c_la = C.take(32 * 128, BF16).rearrange("p (c f) -> p c f", c=32)
            c_lb = C.take(D, BF16)
            c_hT = C.take(TOK, BF16)
            c_h = C.take(128, F32)
            c_outs = [C.take(512, F32) for _ in range(2)]
            c_woT = C.take(D, BF16)
            c_biasbc = C.take(D, F32)
            C_MAIN_BASE = C.off
            c_x = C.take(D, F32)
            c_x2 = C.take(D, F32)
            c_xn2 = C.take(D, F32)

            dma("g", c_la, d_laT.ap().rearrange("p (c f) -> p c f", c=32), "cw")
            dma("g", c_lb, d_lbT.ap(), "cw")
            tok_woT = dma("g", c_woT, d_woT.ap(), "cw")
            tok_la = tok_lb = tok_woT
            tok_borow = dma("y", c_rows[0:1, :], d_bo.ap(), "cwy")

            tok_bo_bc = None
            wt("t", tok_borow)
            for ob in range(8):
                bb = 6 + (ob % 2)
                wt("t", bankfree[bb])
                tm = em("t", lambda t, bb=bb, ob=ob: t.matmul(
                    bank(bb), sb_ones[:, :], c_rows[0:1, ob * 512:(ob + 1) * 512],
                    start=True, stop=True), sm_t)
                tok_bcast_mm_last = tm
                wt("v", tm)
                tok_bo_bc = em("v", lambda v, bb=bb, ob=ob: v.tensor_copy(
                    c_biasbc[:, ob * 512:(ob + 1) * 512], bank(bb)), sm_v)
                bankfree[bb] = tok_bo_bc

            tok_x_used = None
            tok_x2_dma_last = None
            tok_xn2_cons = None
            tok_x2w = []
            tok_evS_last = None
            tok_evV_last = None
            tok_op_last = None
            for tt in range(8):
                wt("y", tok_x_used)
                tok_xl = dma("y", c_x, d_x[tt * 128:(tt + 1) * 128, :], "x0")
                wt("t", tok_osbf)
                wt("t", tok_woT)
                for ob in range(8):
                    wt("t", bankfree[ob])
                    tm = em("t", lambda t, tt=tt, ob=ob: t.matmul(
                        bank(ob), b_osbf[:, tt * 128:tt * 128 + 128],
                        c_woT[:, ob * 512:(ob + 1) * 512], start=True, stop=True),
                        sm_t if ob == 7 else None)
                tok_op_last = tm
                wt("v", tm)
                wt("v", tok_bo_bc)
                wt("v", tok_x2_dma_last)
                t1 = em("v", lambda v: v.tensor_tensor(
                    c_x2, ps[:, 0:4096], c_biasbc, ALU.add), sm_v)
                for ob in range(8):
                    bankfree[ob] = t1
                wt("v", t1)
                wt("v", tok_xl)
                tok_ev_all = em("v", lambda v: v.tensor_tensor(
                    c_x2, c_x2, c_x, ALU.add), sm_v)
                tok_x_used = tok_ev_all
                wt("y", tok_ev_all)
                tw = dma("y", bass.AP(d_x2, tt * 128 * 512,
                                      [[512, 128], [TOK * 512, 8], [1, 512]]),
                         c_x2.rearrange("p (a b) -> p a b", a=8), "x2w")
                if debug >= 3:
                    tw = dma("y", bass.AP(dbg["x2"], tt * 128 * 512,
                                          [[512, 128], [TOK * 512, 8], [1, 512]]),
                             c_x2.rearrange("p (a b) -> p a b", a=8), "x2w")
                tok_x2w.append(tw)
                tok_x2_dma_last = tok_x2w[-1]
                t9 = emit_ln(c_x2, c_xn2, c_xn2, tok_ev_all, tok_ev_all,
                             extra_s_wait=tok_xn2_cons)
                wt("t", t9)
                last_tp = None
                for dc in range(32):
                    b = 5 + (dc % 3)
                    wt("t", bankfree[b])
                    tok_tp = em("t", lambda t, b=b, dc=dc: t.transpose(
                        bank(b)[:, 0:128], c_xn2[:, dc * 128:dc * 128 + 128], sb_id[:, :]), sm_t)
                    last_tp = tok_tp
                    eng = "s" if dc % 2 == 0 else "v"
                    wt(eng, tok_tp)
                    if eng == "s":
                        tok_ev = em(eng, lambda e, b=b, dc=dc, tt=tt: e.copy(
                            c_xn2F[dc][:, tt * 128:tt * 128 + 128], bank(b)[:, 0:128]), sm_s)
                    else:
                        tok_ev = em(eng, lambda e, b=b, dc=dc, tt=tt: e.tensor_copy(
                            c_xn2F[dc][:, tt * 128:tt * 128 + 128], bank(b)[:, 0:128]), sm_v)
                    bankfree[b] = tok_ev
                    if eng == "s":
                        tok_evS_last = tok_ev
                    else:
                        tok_evV_last = tok_ev
                tok_xn2_cons = last_tp

            wt("y", tok_bcast_mm_last)
            tok_fbrow = dma("y", c_rows[0:1, :], d_fb.ap(), "cwy")
            tok_fb_bc = None
            wt("t", tok_fbrow)
            wt("t", tok_x_used)
            for ob in range(8):
                bb = 6 + (ob % 2)
                wt("t", bankfree[bb])
                tm = em("t", lambda t, bb=bb, ob=ob: t.matmul(
                    bank(bb), sb_ones[:, :], c_rows[0:1, ob * 512:(ob + 1) * 512],
                    start=True, stop=True), sm_t)
                wt("v", tm)
                tok_fb_bc = em("v", lambda v, bb=bb, ob=ob: v.tensor_copy(
                    c_biasbc[:, ob * 512:(ob + 1) * 512], bank(bb)), sm_v)
                bankfree[bb] = tok_fb_bc

            wt("t", tok_la)
            wt("t", tok_evS_last)
            wt("t", tok_evV_last)
            tok_hT_done = None
            tok_h_cons = None
            for tt in range(8):
                b = tt % 2
                wt("t", bankfree[b])
                tok_mm = None
                for dc in range(32):
                    tok_mm = em("t", lambda t, b=b, tt=tt, dc=dc: t.matmul(
                        bank(b)[:, 0:128], c_xn2F[dc][:, tt * 128:tt * 128 + 128],
                        c_la[:, dc, :], start=(dc == 0), stop=(dc == 31)),
                        sm_t if dc == 31 else None)
                wt("v", tok_mm)
                wt("v", tok_h_cons)
                tcp = em("v", lambda v, b=b: v.tensor_copy(c_h, bank(b)[:, 0:128]), sm_v)
                bankfree[b] = tcp
                tb_ = 2 + (tt % 2)
                wt("t", tcp)
                wt("t", bankfree[tb_])
                ttp = em("t", lambda t, tb_=tb_: t.transpose(
                    bank(tb_)[:, 0:128], c_h, sb_id[:, :]), sm_t)
                tok_h_cons = ttp
                wt("v", ttp)
                tok_hT_done = em("v", lambda v, tb_=tb_, tt=tt: v.tensor_scalar(
                    c_hT[:, tt * 128:tt * 128 + 128], bank(tb_)[:, 0:128],
                    scalar1=sb_coef[:, :], scalar2=None, op0=ALU.mult), sm_v)
                bankfree[tb_] = tok_hT_done

            CM = Carver(C_MAIN_BASE, OS_BASE)
            c_wf = [CM.take(16 * 512, BF16).rearrange("p (c f) -> p c f", c=16) for _ in range(2)]
            c_x2s = [CM.take(8 * 512, F32).rearrange("p (c f) -> p c f", c=8) for _ in range(2)]

            tok_wfh_free = [None, None]
            tok_x2s_free = [None, None]
            tok_outs_free = [None, None]
            x2w_all = tok_x2w[-1]
            for ob in range(8):
                wb = ob % 2
                if ob == 0:
                    wt("g", tok_x_used)
                    wt("g", tok_xn2_cons)
                    wt("g", x2w_all)
                    wt("y", tok_x_used)
                    wt("y", tok_xn2_cons)
                tok_wfh = [None, None]
                for hf in range(2):
                    wt("g", tok_wfh_free[hf])
                    wt("g", tok_wfh_free[1])
                    tok_wfh[hf] = dma("g", c_wf[hf],
                                      bass.AP(d_wfT, ob * (32 * 512) + hf * (16 * 512),
                                              [[8 * 32 * 512, 128], [512, 16], [1, 512]]),
                                      "wf0" if hf == 0 else "wf1")
                wt("y", x2w_all)
                wt("y", tok_x2s_free[wb])
                tok_x2s = dma("y", c_x2s[wb], bass.AP(d_x2, ob * TOK * 512,
                                                      [[512, 128], [128 * 512, 8], [1, 512]]),
                              "x2r0" if wb == 0 else "x2r1")
                if ob == 0:
                    wt("t", tok_lb)
                    wt("t", tok_hT_done)
                tok_lora_mm = [None] * 8
                for hf in range(2):
                    wt("t", tok_wfh[hf])
                    for tt in range(8):
                        if hf == 0:
                            wt("t", bankfree[tt])
                        for dci in range(16):
                            dc = hf * 16 + dci
                            em("t", lambda t, tt=tt, hf=hf, dc=dc, dci=dci: t.matmul(
                                bank(tt), c_xn2F[dc][:, tt * 128:tt * 128 + 128],
                                c_wf[hf][:, dci, :], start=(dc == 0), stop=False))
                        if hf == 1:
                            tok_lora_mm[tt] = em("t", lambda t, tt=tt, ob=ob: t.matmul(
                                bank(tt), c_hT[:, tt * 128:tt * 128 + 128],
                                c_lb[:, ob * 512:(ob + 1) * 512],
                                start=False, stop=True), sm_t)
                        else:
                            if tt == 7:
                                tok_wfh_free[0] = em("t", lambda t: t.nop(), sm_t)
                tok_wfh_free[1] = tok_lora_mm[7]
                for tt in range(8):
                    oi = tt % 2
                    wt("v", tok_lora_mm[tt])
                    if ob == 0 and tt == 0:
                        wt("v", tok_fb_bc)
                    wt("v", tok_outs_free[oi])
                    t1 = em("v", lambda v, tt=tt, ob=ob, oi=oi: v.tensor_tensor(
                        c_outs[oi], bank(tt), c_biasbc[:, ob * 512:(ob + 1) * 512], ALU.add), sm_v)
                    bankfree[tt] = t1
                    wt("v", t1)
                    wt("v", tok_x2s)
                    t2 = em("v", lambda v, oi=oi, wb=wb, tt=tt: v.tensor_tensor(
                        c_outs[oi], c_outs[oi], c_x2s[wb][:, tt, :], ALU.add), sm_v)
                    wt("y", t2)
                    tw = dma("y", bass.AP(d_out, tt * 128 * D + ob * 512,
                                          [[D, 128], [1, 512]]), c_outs[oi],
                             "o0" if oi == 0 else "o1")
                    tok_outs_free[oi] = tw
                    if tt == 7:
                        tok_x2s_free[wb] = t2


        if run_c:
            _phase_c()

        for _sm in dma_sems.values():
            if _sm.n > 0:
                sched.wait("y", (_sm, _sm.n))

        with nc.Block() as block:
            block.tensor(lambda eng: [f(eng) for f in sched.ops["t"]])
            block.vector(lambda eng: [f(eng) for f in sched.ops["v"]])
            block.scalar(lambda eng: [f(eng) for f in sched.ops["s"]])
            block.gpsimd(lambda eng: [f(eng) for f in sched.ops["g"]])
            block.sync(lambda eng: [f(eng) for f in sched.ops["y"]])

    return nc


def _stage(inputs):
    g = lambda k: np.asarray(inputs[k], np.float32)
    X = g("L0_attn_hidden_size").reshape(S, D)
    Wq_c = g("L0_attn_Wq_c")
    bq_c = g("L0_attn_bq_c")
    Wq_up = g("L0_attn_Wq_up")
    bq_up = g("L0_attn_bq_up")
    Wkv_c = g("L0_attn_Wkv_c")
    bkv_c = g("L0_attn_bkv_c")
    Wkv_up = g("L0_attn_Wkv_up")
    bkv_up = g("L0_attn_bkv_up")
    Wo = g("L0_attn_Wo")
    bo = g("L0_attn_bo")
    lora_A = g("L0_ffn_lora_A")
    lora_B = g("L0_ffn_lora_B")
    Wf = g("L0_ffn_W")
    fb = g("L0_ffn_bias")
    lw = g("L0_ffn_lora_weights")

    C = np.ascontiguousarray

    PERM, KPERM, VPERM = [], [], []
    for p in range(NPAIR):
        h0, h1 = 2 * p, 2 * p + 1
        PERM += list(range(h0 * QH, h0 * QH + NOPE))
        PERM += list(range(h1 * QH, h1 * QH + NOPE))
        PERM += list(range(h0 * QH + NOPE, h0 * QH + QH))
        PERM += list(range(h1 * QH + NOPE, h1 * QH + QH))
        KPERM += list(range(h0 * 256, h0 * 256 + 128))
        KPERM += list(range(h1 * 256, h1 * 256 + 128))
        VPERM += list(range(h0 * 256 + 128, h0 * 256 + 256))
        VPERM += list(range(h1 * 256 + 128, h1 * 256 + 256))

    base = {}
    base["wqcT"] = C(Wq_c.T)
    base["bqc_t"] = C(bq_c.reshape(12, 128).T)
    wqupT = Wq_up[PERM, :].T
    base["wqup_til"] = C(wqupT.reshape(12, 128, NPAIR, 384).transpose(1, 2, 0, 3)
                         .reshape(128, NPAIR, 12 * 384))
    base["bqup_t"] = C(bq_up[PERM].reshape(48, 128).T)
    wkvcT = Wkv_c.T
    base["wkvc_til"] = C(wkvcT.reshape(32, 128, 576).transpose(1, 0, 2).reshape(128, 32 * 576))
    bkvc_p = np.zeros(640, np.float32)
    bkvc_p[:576] = bkv_c
    base["bkvc_t"] = C(bkvc_p.reshape(5, 128).T)
    wkKT = Wkv_up[KPERM, :].T
    base["wkK_til"] = C(wkKT.reshape(4, 128, NPAIR, 256).transpose(1, 2, 0, 3)
                        .reshape(128, NPAIR, 4 * 256))
    base["bkupK_t"] = C(bkv_up[KPERM].reshape(32, 128).T)
    wkVT = Wkv_up[VPERM, :].T
    base["wkV_til"] = C(wkVT.reshape(4, 128, NPAIR, 256).transpose(1, 2, 0, 3)
                        .reshape(128, NPAIR, 4 * 256))
    base["bkvV_cols"] = C(bkv_up[VPERM].reshape(32, 128).T)
    base["woT"] = C(Wo.T)
    base["bo_row"] = C(bo.reshape(1, D))
    WfT = Wf.T
    base["wffn_til"] = C(WfT.reshape(32, 128, 8, 512).transpose(1, 2, 0, 3)
                         .reshape(128, 8, 32 * 512))
    base["ffnb_row"] = C(fb.reshape(1, D))
    laT = lora_A.reshape(128, D).T
    base["loraA_til"] = C(laT.reshape(32, 128, 128).transpose(1, 0, 2).reshape(128, 32 * 128))
    base["loraBT"] = C(lora_B.transpose(0, 2, 1).reshape(128, D))
    base["lwrep"] = C(np.repeat(lw, 16)[:, None])
    base["ident"] = np.eye(128, dtype=np.float32)
    base["ones_row"] = np.ones((1, 128), np.float32)
    base["onecol_bf"] = np.ones((128, 1), ml_dtypes.bfloat16)

    in_maps = []
    for c in range(NCORE):
        m = dict(base)
        m["x_rot"] = C(np.concatenate([X[TOK * c:], X[:TOK * c]], axis=0))
        in_maps.append(m)
    return in_maps


def kernel(**inputs):
    in_maps = _stage(inputs)
    nc = build_nc(debug=_DEBUG)
    res = run_bass_kernel_spmd(nc, in_maps, core_ids=list(range(NCORE)))
    if _DEBUG:
        kernel._last = res
    out = np.concatenate([np.asarray(res.results[c]["out_c"]) for c in range(NCORE)], axis=0)
    return out.reshape(1, S, D).astype(np.float32)


# version 12 (final)
# speedup vs baseline: 1.0856x; 1.0285x over previous
import os
import math
import numpy as np
import ml_dtypes
import concourse.bass as bass
import concourse.mybir as mybir
from concourse.bass_utils import run_bass_kernel_spmd
from contextlib import ExitStack

F32 = mybir.dt.float32
BF16 = mybir.dt.bfloat16
AF = mybir.ActivationFunctionType
ALU = mybir.AluOpType
AX = mybir.AxisListType

NCORE = 8
S = 8192
D = 4096
TOK = 1024
NH = 32
NPAIR = 16
QL = 1536
KVL = 512
QH = 192
NOPE = 128
VH = 128
ISQ = 1.0 / math.sqrt(QH)
EPS = 1e-5

_DEBUG = int(os.environ.get("MK_DEBUG", "0"))
_NPAIR_RUN = int(os.environ.get("MK_NPAIR", str(NPAIR)))
_PHASE = int(os.environ.get("MK_PHASE", "3"))
_TBS = int(os.environ.get("MK_TBS", "16"))
_SUB = int(os.environ.get("MK_SUB", "15"))
_TPV = int(os.environ.get("MK_TPV", "0"))
_TTS = int(os.environ.get("MK_TTS", "4"))


class Sem:
    def __init__(self, handle, name):
        self.h = handle
        self.name = name
        self.n = 0


class Sched:
    ENGINES = ("t", "v", "s", "g", "y")

    def __init__(self):
        self.ops = {e: [] for e in self.ENGINES}
        self.sems = {}

    def sem(self, name, handle):
        sm = Sem(handle, name)
        self.sems[name] = sm
        return sm

    def emit(self, e, fn, sem=None, k=1):
        if sem is None:
            self.ops[e].append(fn)
            return None
        sem.n += k
        tag = sem.n

        def run(eng, fn=fn, s=sem.h, k=k):
            fn(eng).then_inc(s, k)

        self.ops[e].append(run)
        return (sem, tag)

    def wait(self, e, tok):
        if tok is None:
            return
        sem, tag = tok
        if tag <= 0:
            return
        self.ops[e].append(lambda eng, s=sem.h, t=tag: eng.wait_ge(s, t))

    def barrier(self):
        toks = [(sm, sm.n) for sm in self.sems.values() if sm.n > 0]
        for e in self.ENGINES:
            for tok in toks:
                self.wait(e, tok)


def build_nc(debug=0):
    nc = bass.Bass("TRN2", target_bir_lowering=False)

    d_x = nc.dram_tensor("x_rot", [S, D], F32, kind="ExternalInput")
    d_wqcT = nc.dram_tensor("wqcT", [D, QL], F32, kind="ExternalInput")
    d_bqc = nc.dram_tensor("bqc_t", [128, 12], F32, kind="ExternalInput")
    d_wqup = nc.dram_tensor("wqup_til", [128, NPAIR, 12 * 384], F32, kind="ExternalInput")
    d_bqup = nc.dram_tensor("bqup_t", [128, 48], F32, kind="ExternalInput")
    d_wkvc = nc.dram_tensor("wkvc_til", [128, 32 * 576], F32, kind="ExternalInput")
    d_bkvc = nc.dram_tensor("bkvc_t", [128, 5], F32, kind="ExternalInput")
    d_wkK = nc.dram_tensor("wkK_til", [128, NPAIR, 4 * 256], F32, kind="ExternalInput")
    d_bkK = nc.dram_tensor("bkupK_t", [128, 32], F32, kind="ExternalInput")
    d_wkV = nc.dram_tensor("wkV_til", [128, NPAIR, 4 * 256], F32, kind="ExternalInput")
    d_bkV = nc.dram_tensor("bkvV_cols", [128, 32], F32, kind="ExternalInput")
    d_woT = nc.dram_tensor("woT", [VH, D], F32, kind="ExternalInput")
    d_bo = nc.dram_tensor("bo_row", [1, D], F32, kind="ExternalInput")
    d_wfT = nc.dram_tensor("wffn_til", [128, 8, 32 * 512], F32, kind="ExternalInput")
    d_fb = nc.dram_tensor("ffnb_row", [1, D], F32, kind="ExternalInput")
    d_laT = nc.dram_tensor("loraA_til", [128, 32 * 128], F32, kind="ExternalInput")
    d_lbT = nc.dram_tensor("loraBT", [128, D], F32, kind="ExternalInput")
    d_lw = nc.dram_tensor("lwrep", [128, 1], F32, kind="ExternalInput")
    d_id = nc.dram_tensor("ident", [128, 128], F32, kind="ExternalInput")
    d_ones = nc.dram_tensor("ones_row", [1, 128], F32, kind="ExternalInput")
    d_onec = nc.dram_tensor("onecol_bf", [128, 1], BF16, kind="ExternalInput")

    d_kvF = nc.dram_tensor("kvF_dram", [576, S], BF16)
    d_qcF = nc.dram_tensor("qcF_dram", [2, 128, 12 * 512], BF16)
    d_x2 = nc.dram_tensor("x2_dram", [8, TOK, 512], F32)
    d_out = nc.dram_tensor("out_c", [TOK, D], F32, kind="ExternalOutput")

    dbg = {}
    if debug >= 1:
        dbg["kvF"] = nc.dram_tensor("dbg_kvF", [576, S], BF16, kind="ExternalOutput")
        dbg["qcF"] = nc.dram_tensor("dbg_qcF", [2, 128, 12 * 512], BF16, kind="ExternalOutput")
    if debug >= 2:
        dbg["K0"] = nc.dram_tensor("dbg_K0", [NOPE, S], BF16, kind="ExternalOutput")
        dbg["K1"] = nc.dram_tensor("dbg_K1", [NOPE, S], BF16, kind="ExternalOutput")
        dbg["V0"] = nc.dram_tensor("dbg_V0", [128, 64 * 256], BF16, kind="ExternalOutput")
        dbg["Q0"] = nc.dram_tensor("dbg_Q0", [128, 3 * TOK], BF16, kind="ExternalOutput")
        dbg["osum"] = nc.dram_tensor("dbg_osum", [VH, TOK], F32, kind="ExternalOutput")
    if debug >= 3:
        dbg["x2"] = nc.dram_tensor("dbg_x2", [8, TOK, 512], F32, kind="ExternalOutput")

    sched = Sched()
    em, wt = sched.emit, sched.wait

    with ExitStack() as ctx:
        E = ctx.enter_context

        sb_id = E(nc.sbuf_tensor("sb_id", [128, 128], F32))
        sb_ones = E(nc.sbuf_tensor("sb_ones", [1, 128], F32))
        sb_onec = E(nc.sbuf_tensor("sb_onec", [128, 1], BF16))
        sb_bqc = E(nc.sbuf_tensor("sb_bqc", [128, 12], F32))
        sb_bqup = E(nc.sbuf_tensor("sb_bqup", [128, 48], F32))
        sb_bkvc = E(nc.sbuf_tensor("sb_bkvc", [128, 5], F32))
        sb_bkK = E(nc.sbuf_tensor("sb_bkK", [128, 32], F32))
        sb_bkV = E(nc.sbuf_tensor("sb_bkV", [128, 32], F32))
        sb_lw = E(nc.sbuf_tensor("sb_lw", [128, 1], F32))
        sb_coef = E(nc.sbuf_tensor("sb_coef", [128, 1], F32))
        sb_bvsum = E(nc.sbuf_tensor("sb_bvsum", [128, 1], F32))
        st_sum = E(nc.sbuf_tensor("st_sum", [128, 2], F32))
        st_sqs = E(nc.sbuf_tensor("st_sqs", [128, 2], F32))
        st_nm = E(nc.sbuf_tensor("st_nm", [128, 2], F32))
        st_m2 = E(nc.sbuf_tensor("st_m2", [128, 2], F32))
        st_var = E(nc.sbuf_tensor("st_var", [128, 2], F32))
        st_sq = E(nc.sbuf_tensor("st_sq", [128, 2], F32))
        st_rs = E(nc.sbuf_tensor("st_rs", [128, 2], F32))
        sb_stab = E(nc.sbuf_tensor("sb_stab", [128, 8], F32))
        sb_qmax = E(nc.sbuf_tensor("sb_qmax", [128, 1], F32))
        sb_kmax = E(nc.sbuf_tensor("sb_kmax", [128, 1], F32))
        sb_qm = E(nc.sbuf_tensor("sb_qm", [1, 1], F32))
        sb_km = E(nc.sbuf_tensor("sb_km", [1, 1], F32))
        sb_cqk = E(nc.sbuf_tensor("sb_cqk", [1, 1], F32))
        sb_negc_col = E(nc.sbuf_tensor("sb_negc_col", [128, 1], F32))
        sb_recip0 = E(nc.sbuf_tensor("sb_recip0", [1, 512], F32))
        sb_recip1 = E(nc.sbuf_tensor("sb_recip1", [1, 512], F32))

        AR_N = 51456
        OS_BASE = AR_N - 1536
        arena = E(nc.sbuf_tensor("arena", [128, AR_N], F32))

        class Carver:
            def __init__(self, base=0, limit=AR_N):
                self.off = base
                self.limit = limit

            def take(self, nfree, dtype):
                ef32 = nfree if dtype == F32 else (nfree + 1) // 2
                ef32 = (ef32 + 15) & ~15
                assert self.off + ef32 <= self.limit, \
                    f"arena overflow {self.off}+{ef32} > {self.limit}"
                ap = arena[:, self.off:self.off + ef32]
                self.off += ef32
                if dtype == BF16:
                    ap = ap.bitcast(BF16)[:, 0:nfree]
                return ap

        ps = E(nc.psum_tensor("ps", [128, 4096], F32))

        def bank(b):
            return ps[:, b * 512:(b + 1) * 512]

        bankfree = [None] * 8

        sm_gc = sched.sem("gc", E(nc.semaphore("sm_gc")))
        sm_t = sched.sem("t", E(nc.semaphore("sm_t")))
        sm_v = sched.sem("v", E(nc.semaphore("sm_v")))
        sm_s = sched.sem("s", E(nc.semaphore("sm_s")))
        dma_sems = {}
        for nm in ("c", "x0", "x1", "x2b", "wkvc", "wq0", "wq1", "kv0", "kv1",
                   "qc0", "qc1", "kvn", "qch0", "qch1", "wkK", "wkV", "wqup",
                   "cw", "cwy", "wf0", "wf1", "x2w", "x2r0", "x2r1", "o0", "o1", "st"):
            dma_sems[nm] = sched.sem(nm, E(nc.semaphore("sm_" + nm)))
        sm_st = dma_sems["st"]

        def dma(e, out, in_, sem):
            if isinstance(sem, str):
                sem = dma_sems[sem]
            return em(e, lambda eng, o=out, i=in_: eng.dma_start(out=o, in_=i), sem, 16)

        tok_consts = None
        for (dst, src) in [
            (sb_id[:, :], d_id.ap()), (sb_ones[:, :], d_ones.ap()),
            (sb_onec[:, :], d_onec.ap()), (sb_bqc[:, :], d_bqc.ap()),
            (sb_bqup[:, :], d_bqup.ap()), (sb_bkvc[:, :], d_bkvc.ap()),
            (sb_bkK[:, :], d_bkK.ap()), (sb_bkV[:, :], d_bkV.ap()),
            (sb_lw[:, :], d_lw.ap()),
        ]:
            tok_consts = dma("y", dst, src, "c")
        wt("v", tok_consts)
        em("v", lambda v: v.tensor_scalar(
            sb_coef[:, :], sb_lw[:, :], scalar1=2.0, scalar2=None, op0=ALU.mult), sm_v)
        tok_bvsum = em("v", lambda v: v.reduce_sum(sb_bvsum[:, :], sb_bkV[:, :], AX.X), sm_v)

        ln_rot = [0]
        last_sq_tok = [None]
        slot_sum_free = [None, None]
        slot_sqs_free = [None, None]
        slot_sq_free = [None, None]

        def emit_ln_stats(xbuf, xsqbuf, tok_loaded_s, extra_s_wait=None):
            r = ln_rot[0] % 2
            ln_rot[0] += 1
            sl = slice(r, r + 1)
            wt("s", tok_loaded_s)
            if extra_s_wait is not None:
                wt("s", extra_s_wait)
            wt("s", slot_sum_free[r])
            wt("s", last_sq_tok[0])
            t1 = em("s", lambda s, sl=sl, xb=xbuf, xo=xsqbuf: s.activation(
                xo, xb, AF.Copy, accum_out=st_sum[:, sl]), sm_s)
            wt("s", slot_sqs_free[r])
            wt("s", t1)
            t2 = em("s", lambda s, sl=sl, xb=xbuf, xq=xsqbuf: s.activation(
                xq, xb, AF.Square, accum_out=st_sqs[:, sl]), sm_s)
            last_sq_tok[0] = t2
            return r, t1, t2

        def emit_ln_finish(xbuf, xnbuf, r, t1, t2, tok_loaded_v, extra_v_wait=None):
            sl = slice(r, r + 1)
            if extra_v_wait is not None:
                wt("v", extra_v_wait)
            wt("v", t1)
            t3 = em("v", lambda v, sl=sl: v.tensor_scalar(
                st_nm[:, sl], st_sum[:, sl], scalar1=-1.0 / D, scalar2=None, op0=ALU.mult), sm_v)
            slot_sum_free[r] = t3
            wt("v", t3)
            t4 = em("v", lambda v, sl=sl: v.tensor_tensor(
                st_m2[:, sl], st_nm[:, sl], st_nm[:, sl], ALU.mult), sm_v)
            wt("v", t2)
            wt("v", t4)
            t5 = em("v", lambda v, sl=sl: v.tensor_scalar(
                st_var[:, sl], st_sqs[:, sl], scalar1=1.0 / D, scalar2=st_m2[:, sl],
                op0=ALU.mult, op1=ALU.subtract), sm_v)
            slot_sqs_free[r] = t5
            wt("v", t5)
            t6 = em("v", lambda v, sl=sl: v.tensor_scalar(
                st_var[:, sl], st_var[:, sl], scalar1=EPS, scalar2=None, op0=ALU.add), sm_v)
            wt("s", t6)
            wt("s", slot_sq_free[r])
            t7 = em("s", lambda s, sl=sl: s.activation(st_sq[:, sl], st_var[:, sl], AF.Sqrt), sm_s)
            wt("v", t7)
            t8 = em("v", lambda v, sl=sl: v.reciprocal(st_rs[:, sl], st_sq[:, sl]), sm_v)
            slot_sq_free[r] = t8
            wt("v", t8)
            wt("v", tok_loaded_v)
            t9 = em("v", lambda v, sl=sl, xb=xbuf, xo=xnbuf: v.tensor_scalar(
                xo, xb, scalar1=st_nm[:, sl], scalar2=st_rs[:, sl],
                op0=ALU.add, op1=ALU.mult), sm_v)
            return t9

        def emit_ln(xbuf, xnbuf, xsqbuf, tok_loaded_v, tok_loaded_s,
                    extra_v_wait=None, extra_s_wait=None):
            r, t1, t2 = emit_ln_stats(xbuf, xsqbuf, tok_loaded_s, extra_s_wait)
            return emit_ln_finish(xbuf, xnbuf, r, t1, t2, tok_loaded_v, extra_v_wait)

        A = Carver()
        a_x = [A.take(D, F32) for _ in range(2)]
        a_xsq = A.take(D, F32)
        a_xn = [A.take(D, F32) for _ in range(2)]
        a_xnF = [[A.take(512, BF16) for _ in range(32)] for _ in range(2)]
        a_wkvc = A.take(32 * 576, BF16).rearrange("p (c f) -> p c f", c=32)
        a_wqct = [A.take(2 * QL, BF16).rearrange("p (c f) -> p c f", c=2) for _ in range(2)]
        a_kvst = [A.take(512, BF16) for _ in range(2)]
        a_qcst = [A.take(512, BF16) for _ in range(2)]

        tok_wkvc = dma("g", a_wkvc, d_wkvc.ap().rearrange("p (c f) -> p c f", c=32), "wkvc")

        tok_x_free = [None, None]
        tok_xld = [None, None]
        stats_tok = [None, None]
        tok_xn_cons = [None, None]
        tok_xnF_ready = [[None] * 32, [None] * 32]
        tok_set_free = [None, None]
        tok_kv_last = [None] * 5
        tok_kvst_free = [None, None]
        tok_qcst_free = [None, None]
        qc_buf_free = [None, None]
        tok_qc_last = [None] * 6

        def gen_kv_mms(tb):
            st = tb % 2
            if tb == 0:
                wt("t", tok_wkvc)
            for dc in range(32):
                wt("t", tok_xnF_ready[st][dc])
            for f in range(5):
                rows = 128 if f < 4 else 64
                wt("t", bankfree[f])
                for dc in range(32):
                    tok_kv_last[f] = em("t", lambda t, f=f, dc=dc, rows=rows, st=st: t.matmul(
                        bank(f)[0:rows, 0:512],
                        a_wkvc[:, dc, f * 128:f * 128 + rows],
                        a_xnF[st][dc],
                        start=(dc == 0), stop=(dc == 31)),
                        sm_t if dc == 31 else None)
                    yield

        def flush_kv(tb):
            for f in range(5):
                rows = 128 if f < 4 else 64
                kb = f % 2
                wt("v", tok_kv_last[f])
                wt("v", tok_kvst_free[kb])
                tok_ev = em("v", lambda v, f=f, kb=kb, rows=rows: v.tensor_scalar(
                    a_kvst[kb][0:rows, :], bank(f)[0:rows, 0:512],
                    scalar1=sb_bkvc[0:rows, f:f + 1], scalar2=None, op0=ALU.add), sm_v)
                bankfree[f] = tok_ev
                wt("y", tok_ev)
                tok_st = dma("y", bass.AP(d_kvF, f * 128 * S + tb * 512,
                                          [[S, rows], [1, 512]]),
                             a_kvst[kb][0:rows, :], "kv0" if kb == 0 else "kv1")
                if debug >= 1:
                    tok_st = dma("y", bass.AP(dbg["kvF"], f * 128 * S + tb * 512,
                                              [[S, rows], [1, 512]]),
                                 a_kvst[kb][0:rows, :], "kv0" if kb == 0 else "kv1")
                tok_kvst_free[kb] = tok_st

        kv_gen = None
        for tb in range(16):
            st = tb % 2
            for tt in range(4):
                tile = tb * 4 + tt
                bj = tile % 2
                if tile == 0:
                    tok_xld[0] = dma("y", a_x[0], d_x[0:128, :], "x0")
                    stats_tok[0] = emit_ln_stats(a_x[0], a_xsq, tok_xld[0])
                if tile + 1 < 64:
                    t1_ = tile + 1
                    wt("y", tok_x_free[t1_ % 2])
                    tok_xld[t1_ % 2] = dma("y", a_x[t1_ % 2],
                                           d_x[t1_ * 128:(t1_ + 1) * 128, :],
                                           "x0" if t1_ % 2 == 0 else "x1")
                    stats_tok[t1_ % 2] = emit_ln_stats(a_x[t1_ % 2], a_xsq,
                                                       tok_xld[t1_ % 2])
                r_, ts1, ts2 = stats_tok[tile % 2]
                tok_xn = emit_ln_finish(a_x[bj], a_xn[bj], r_, ts1, ts2,
                                        tok_xld[bj], extra_v_wait=tok_xn_cons[bj])
                tok_x_free[bj] = tok_xn
                wt("t", tok_xn)
                last_tp = None
                for dc in range(32):
                    b = 5 + (dc % 3)
                    wt("t", bankfree[b])
                    tok_tp = em("t", lambda t, b=b, xb=a_xn[bj], dc=dc: t.transpose(
                        bank(b)[:, 0:128], xb[:, dc * 128:dc * 128 + 128], sb_id[:, :]), sm_t)
                    last_tp = tok_tp
                    if kv_gen is not None:
                        next(kv_gen, None)
                        next(kv_gen, None)
                    eng = "s" if dc % 2 == 0 else "v"
                    wt(eng, tok_tp)
                    if tt == 0 and dc == 0:
                        wt("s", tok_set_free[st])
                        wt("v", tok_set_free[st])
                    if eng == "s":
                        tok_ev = em(eng, lambda e, b=b, dc=dc, tt=tt, st=st: e.copy(
                            a_xnF[st][dc][:, tt * 128:tt * 128 + 128], bank(b)[:, 0:128]), sm_s)
                    else:
                        tok_ev = em(eng, lambda e, b=b, dc=dc, tt=tt, st=st: e.tensor_copy(
                            a_xnF[st][dc][:, tt * 128:tt * 128 + 128], bank(b)[:, 0:128]), sm_v)
                    bankfree[b] = tok_ev
                    if tt == 3:
                        tok_xnF_ready[st][dc] = tok_ev
                tok_xn_cons[bj] = last_tp
            if kv_gen is not None:
                for _ in kv_gen:
                    pass
                flush_kv(tb - 1)
                tok_set_free[(tb - 1) % 2] = tok_kv_last[4]
            if tb < 2:
                for dc in range(32):
                    wt("t", tok_xnF_ready[st][dc])
                for pas in range(2):
                    tok_mm = None
                    for dcg in range(16):
                        gb = dcg % 2
                        wt("g", qc_buf_free[gb])
                        tok_w = dma("g", a_wqct[gb],
                                    bass.AP(d_wqcT, dcg * 2 * 128 * QL,
                                            [[QL, 128], [128 * QL, 2], [1, QL]]),
                                    "wq0" if gb == 0 else "wq1")
                        wt("t", tok_w)
                        for di in range(2):
                            dc = dcg * 2 + di
                            for f6 in range(6):
                                fg = pas * 6 + f6
                                if dc == 0:
                                    wt("t", bankfree[f6])
                                tok_mm = em("t", lambda t, f6=f6, fg=fg, gb=gb, dc=dc, di=di, st=st: t.matmul(
                                    bank(f6)[:, 0:512],
                                    a_wqct[gb][:, di, fg * 128:fg * 128 + 128],
                                    a_xnF[st][dc],
                                    start=(dc == 0), stop=(dc == 31)),
                                    sm_t if (f6 == 5 or dc == 31) else None)
                                if dc == 31:
                                    tok_qc_last[f6] = tok_mm
                        qc_buf_free[gb] = tok_mm
                    for f6 in range(6):
                        fg = pas * 6 + f6
                        qb_ = f6 % 2
                        wt("v", tok_qc_last[f6])
                        wt("v", tok_qcst_free[qb_])
                        tok_ev = em("v", lambda v, f6=f6, fg=fg, qb_=qb_: v.tensor_scalar(
                            a_qcst[qb_], bank(f6)[:, 0:512],
                            scalar1=sb_bqc[:, fg:fg + 1], scalar2=None, op0=ALU.add), sm_v)
                        bankfree[f6] = tok_ev
                        wt("y", tok_ev)
                        tok_st = dma("y", bass.AP(d_qcF, tb * (128 * 12 * 512) + fg * 512,
                                                  [[12 * 512, 128], [1, 512]]),
                                     a_qcst[qb_], "qc0" if qb_ == 0 else "qc1")
                        if debug >= 1:
                            tok_st = dma("y", bass.AP(dbg["qcF"], tb * (128 * 12 * 512) + fg * 512,
                                                      [[12 * 512, 128], [1, 512]]),
                                         a_qcst[qb_], "qc0" if qb_ == 0 else "qc1")
                        tok_qcst_free[qb_] = tok_st
            kv_gen = gen_kv_mms(tb)
        for _ in kv_gen:
            pass
        flush_kv(15)
        tok_set_free[15 % 2] = tok_kv_last[4]

        run_b = _PHASE >= 2
        run_c = _PHASE >= 3
        sched.barrier()
        B = Carver(0, OS_BASE)
        b_kvn = [B.take(S, BF16) for _ in range(4)]
        b_rope = B.take(S, BF16)
        b_K = [B.take(S, BF16) for _ in range(2)]
        b_V = B.take(64 * 256, BF16)
        b_Q = [B.take(TOK, BF16) for _ in range(3)]
        b_P = [[B.take(512, BF16) for _ in range(3)] for _ in range(2)]
        b_qch = B.take(12 * 512, BF16).rearrange("p (c f) -> p c f", c=12)
        b_wqup = B.take(12 * 384, BF16).rearrange("p (c f) -> p c f", c=12)
        b_wkK = B.take(4 * 256, BF16).rearrange("p (c f) -> p c f", c=4)
        b_wkV = B.take(4 * 256, BF16).rearrange("p (c f) -> p c f", c=4)
        b_bcast = B.take(512, F32)
        b_otmp = B.take(512, F32)
        b_Ps = [[B.take(512, BF16) for _ in range(2)] for _ in range(2)]
        b_Pq = [B.take(512, BF16) for _ in range(2)]
        b_osum = arena[:, OS_BASE:OS_BASE + 1024]
        b_osbf = arena[:, OS_BASE + 1024:OS_BASE + 1536].bitcast(BF16)

        for lat in range(4):
            dma("y", b_kvn[lat],
                bass.AP(d_kvF, lat * 128 * S, [[S, 128], [1, S]]), "kvn")
        dma("y", b_rope[0:64, :], bass.AP(d_kvF, 512 * S, [[S, 64], [1, S]]), "kvn")
        tok_kvn_all = dma("y", b_rope[64:128, :],
                          bass.AP(d_kvF, 512 * S, [[S, 64], [1, S]]), "kvn")
        tok_rope_l = tok_rope_h = tok_kvn_all
        tok_osum0 = em("v", lambda v: v.memset(b_osum, 0.0), sm_v)

        wt("v", tok_rope_h)
        wt("v", tok_rope_l)
        tok_ropemax = em("v", lambda v: v.tensor_reduce(
            sb_stab[:, 5:6], b_rope, AX.X, ALU.max, apply_absolute_value=True), sm_v)

        tok_w_free = {"q": None, "k": None, "v": None}
        tok_qch_free = [None, None]
        tok_K_free = [None, None]
        tok_V_free = None
        tok_Q_free = None
        tok_osum_last = tok_osum0
        tok_exp_prev_pair = None
        tok_gq_prev = None
        tok_gk_prev = None
        tok_tc1_prev = None
        tok_bcast_prev = None

        pend = [None]
        texp = [(None, None)] * 3
        tps_tok = [[None, None], [None, None]]
        tps_free = [[None, None], [None, None]]
        pq_tok = [None, None]
        pq_free = [None, None]
        p_read = [[None] * 3, [None] * 3]
        tok_bc_prev = [None, None]
        tok_osum_chain = [tok_osum0]
        flush_fn = [lambda: None]

        def flush_epi_outer():
            flush_fn[0]()
        npair_run = min(_NPAIR_RUN, NPAIR) if run_b else 0
        for p in range(npair_run):
            wt("g", tok_w_free["k"])
            tok_wk = dma("g", b_wkK, bass.AP(d_wkK, p * (4 * 256),
                                             [[NPAIR * 4 * 256, 128], [256, 4], [1, 256]]), "wkK")
            wt("g", tok_w_free["v"])
            tok_wv = dma("g", b_wkV, bass.AP(d_wkV, p * (4 * 256),
                                             [[NPAIR * 4 * 256, 128], [256, 4], [1, 256]]), "wkV")
            wt("g", tok_w_free["q"])
            tok_wq = dma("g", b_wqup, bass.AP(d_wqup, p * (12 * 384),
                                              [[NPAIR * 12 * 384, 128], [384, 12], [1, 384]]), "wqup")

            prod_rot = [0]

            def prod_bank():
                b = prod_rot[0] % 4
                prod_rot[0] += 1
                return b

            wt("t", tok_wq)
            wt("v", tok_Q_free)
            tok_Q_ready = None
            for half in range(2):
                wt("y", tok_qch_free[1] if half == 0 else tok_qch_free[0])
                tok_qc = dma("y", b_qch, bass.AP(d_qcF, half * (128 * 12 * 512),
                                                 [[12 * 512, 128], [512, 12], [1, 512]]),
                             "qch0" if half == 0 else "qch1")
                wt("t", tok_qc)
                for qch in range(3):
                    b = prod_bank()
                    wt("t", bankfree[b])
                    tok_mm = None
                    for ql in range(12):
                        tok_mm = em("t", lambda t, b=b, qch=qch, ql=ql: t.matmul(
                            bank(b),
                            b_wqup[:, ql, qch * 128:qch * 128 + 128],
                            b_qch[:, ql, :],
                            start=(ql == 0), stop=(ql == 11)),
                            sm_t if ql == 11 else None)
                    tok_qch_free[half] = tok_mm
                    wt("v", tok_mm)
                    tok_Q_ready = em("v", lambda v, b=b, qch=qch, half=half, p=p: v.tensor_scalar(
                        b_Q[qch][:, half * 512:half * 512 + 512], bank(b),
                        scalar1=sb_bqup[:, 3 * p + qch:3 * p + qch + 1], scalar2=ISQ,
                        op0=ALU.add, op1=ALU.mult), sm_v)
                    bankfree[b] = tok_Q_ready

            wt("v", tok_Q_ready)
            wt("v", tok_gq_prev)
            tql = None
            for qch in range(3):
                tql = em("v", lambda v, qch=qch: v.tensor_reduce(
                    sb_stab[:, qch:qch + 1], b_Q[qch], AX.X, ALU.max,
                    apply_absolute_value=True), sm_v)
            wt("v", tql)
            tqm = em("v", lambda v: v.tensor_reduce(
                sb_qmax[:, :], sb_stab[:, 0:3], AX.X, ALU.max), sm_v)
            wt("t", tok_wk)
            if p == 0:
                wt("t", tok_kvn_all)
            tok_K_ready = [None, None]
            for h in range(2):
                wt("v", tok_K_free[h])
                for tk in range(16):
                    b = prod_bank()
                    wt("t", bankfree[b])
                    tok_mm = None
                    for lat in range(4):
                        tok_mm = em("t", lambda t, b=b, h=h, tk=tk, lat=lat: t.matmul(
                            bank(b),
                            b_wkK[:, lat, h * 128:h * 128 + 128],
                            b_kvn[lat][:, tk * 512:tk * 512 + 512],
                            start=(lat == 0), stop=(lat == 3)),
                            sm_t if lat == 3 else None)
                    wt("v", tok_mm)
                    tok_ev = em("v", lambda v, b=b, h=h, tk=tk, p=p: v.tensor_scalar(
                        b_K[h][:, tk * 512:tk * 512 + 512], bank(b),
                        scalar1=sb_bkK[:, 2 * p + h:2 * p + h + 1], scalar2=None,
                        op0=ALU.add), sm_v)
                    bankfree[b] = tok_ev
                tok_K_ready[h] = tok_ev

            wt("v", tok_K_ready[0])
            wt("v", tok_K_ready[1])
            wt("v", tok_gk_prev)
            em("v", lambda v: v.tensor_reduce(
                sb_stab[:, 3:4], b_K[0], AX.X, ALU.max, apply_absolute_value=True), sm_v)
            tk1 = em("v", lambda v: v.tensor_reduce(
                sb_stab[:, 4:5], b_K[1], AX.X, ALU.max, apply_absolute_value=True), sm_v)
            wt("v", tk1)
            wt("v", tok_ropemax)
            tkm = em("v", lambda v: v.tensor_reduce(
                sb_kmax[:, :], sb_stab[:, 3:6], AX.X, ALU.max), sm_v)
            wt("t", tok_wv)
            wt("s", tok_V_free)
            tok_V_ready = None
            for tk in range(64):
                b = prod_bank()
                wt("t", bankfree[b])
                tok_mm = None
                for lat in range(4):
                    tok_mm = em("t", lambda t, b=b, tk=tk, lat=lat: t.matmul(
                        bank(b)[:, 0:256],
                        b_kvn[lat][:, tk * 128:tk * 128 + 128],
                        b_wkV[:, lat, :],
                        start=(lat == 0), stop=(lat == 3)),
                        sm_t if lat == 3 else None)
                wt("s", tok_mm)
                tok_V_ready = em("s", lambda e, b=b, tk=tk: e.copy(
                    b_V[:, tk * 256:tk * 256 + 256], bank(b)[:, 0:256]), sm_s)
                bankfree[b] = tok_V_ready

            wt("g", tqm)
            wt("g", tkm)
            wt("g", tok_tc1_prev)
            tok_gq_prev = em("g", lambda g: g.tensor_reduce(
                sb_qm[:, :], sb_qmax[:, :], AX.C, ALU.max), sm_gc)
            tok_gk_prev = em("g", lambda g: g.tensor_reduce(
                sb_km[:, :], sb_kmax[:, :], AX.C, ALU.max), sm_gc)
            wt("v", tok_gq_prev)
            wt("v", tok_gk_prev)
            tok_tc1_prev = em("v", lambda v: v.tensor_tensor(
                sb_cqk[:, :], sb_qm[:, :], sb_km[:, :], ALU.mult), sm_v)
            wt("v", tok_tc1_prev)
            tc2 = em("v", lambda v: v.tensor_scalar(
                sb_cqk[:, :], sb_cqk[:, :], scalar1=-float(QH), scalar2=-45.0,
                op0=ALU.mult, op1=ALU.max), sm_v)
            flush_epi_outer()
            wt("t", tc2)
            wt("t", bankfree[4])
            tbc = em("t", lambda t: t.matmul(
                bank(4)[:, 0:1], sb_ones[:, :], sb_cqk[:, :], start=True, stop=True), sm_t)
            wt("v", tbc)
            wt("v", tok_exp_prev_pair)
            tnc = em("v", lambda v: v.tensor_copy(sb_negc_col[:, :], bank(4)[:, 0:1]), sm_v)
            bankfree[4] = tnc
            wt("t", tok_V_ready)
            wt("s", tnc)
            tok_score_last = None

            def emit_scores(qb, kc):
                par = kc % 2
                b0, b1 = par, 2 + par
                wt("t", bankfree[b0])
                wt("t", bankfree[b1])
                em("t", lambda t, b0=b0, kc=kc, qb=qb: t.matmul(
                    bank(b0), b_K[0][:, kc * 128:kc * 128 + 128],
                    b_Q[0][:, qb * 512:qb * 512 + 512], start=True, stop=False))
                em("t", lambda t, b1=b1, kc=kc, qb=qb: t.matmul(
                    bank(b1), b_K[1][:, kc * 128:kc * 128 + 128],
                    b_Q[1][:, qb * 512:qb * 512 + 512], start=True, stop=False))
                em("t", lambda t, b0=b0, kc=kc, qb=qb: t.matmul(
                    bank(b0), b_rope[0:64, kc * 128:kc * 128 + 128],
                    b_Q[2][0:64, qb * 512:qb * 512 + 512],
                    start=False, stop=True, tile_position=(0, 0)))
                tok_sc = em("t", lambda t, b1=b1, kc=kc, qb=qb: t.matmul(
                    bank(b1), b_rope[64:128, kc * 128:kc * 128 + 128],
                    b_Q[2][64:128, qb * 512:qb * 512 + 512],
                    start=False, stop=True, tile_position=(64, 0)), sm_t)
                wt("s", tok_sc)
                wt("s", p_read[0][kc % 3])
                wt("s", p_read[1][kc % 3])
                te0 = em("s", lambda s, b0=b0, kc=kc: s.activation(
                    b_P[0][kc % 3], bank(b0), AF.Exp, bias=sb_negc_col[:, :]), sm_s)
                te1 = em("s", lambda s, b1=b1, kc=kc: s.activation(
                    b_P[1][kc % 3], bank(b1), AF.Exp, bias=sb_negc_col[:, :]), sm_s)
                bankfree[b0] = te0
                bankfree[b1] = te1
                texp[kc % 3] = (te0, te1)
                if kc % 2 == 1:
                    for h, te in ((0, te0), (1, te1)):
                        wt("v", te)
                        wt("v", tps_free[h][(kc // 2) % 2])
                        tok_add = em("v", lambda v, h=h, kc=kc: v.tensor_tensor(
                            b_Ps[h][(kc // 2) % 2], b_P[h][(kc - 1) % 3],
                            b_P[h][kc % 3], ALU.add), sm_v)
                        tps_tok[h][(kc // 2) % 2] = tok_add
                        p_read[h][(kc - 1) % 3] = tok_add
                        p_read[h][kc % 3] = tok_add
                        if kc % 4 == 3:
                            wt("v", tps_tok[h][0])
                            wt("v", tps_tok[h][1])
                            wt("v", pq_free[h])
                            pq_tok[h] = em("v", lambda v, h=h: v.tensor_tensor(
                                b_Pq[h], b_Ps[h][0], b_Ps[h][1], ALU.add), sm_v)
                            tps_free[h][0] = pq_tok[h]
                            tps_free[h][1] = pq_tok[h]
                return tok_sc, te1

            def emit_pv(qb, kc, epi_banks_wait):
                if epi_banks_wait:
                    for ob in (4, 5):
                        wt("t", bankfree[ob])
                wt("t", texp[kc % 3][0])
                wt("t", texp[kc % 3][1])
                em("t", lambda t, kc=kc: t.matmul(
                    bank(4), b_V[:, kc * 256:kc * 256 + 128], b_P[0][kc % 3],
                    start=(kc == 0), stop=(kc == 63)))
                tok = em("t", lambda t, kc=kc: t.matmul(
                    bank(5), b_V[:, kc * 256 + 128:kc * 256 + 256], b_P[1][kc % 3],
                    start=(kc == 0), stop=(kc == 63)),
                    sm_t if kc == 63 else None)
                if kc % 4 == 3:
                    if kc == 3:
                        for ob in (6, 7):
                            wt("t", bankfree[ob])
                    for h in range(2):
                        wt("t", pq_tok[h])
                        tok = em("t", lambda t, kc=kc, h=h: t.matmul(
                            bank(6 + h)[0:1, :], sb_onec[:, :], b_Pq[h],
                            start=(kc == 3), stop=(kc == 63)), sm_t)
                        pq_free[h] = tok
                return tok

            def emit_epi_v1(qb, tok_pv):
                trs = []
                for h in range(2):
                    wt("v", tok_pv)
                    wt("v", tok_bc_prev[h])
                    trs.append(em("v", lambda v, h=h: v.reciprocal(
                        (sb_recip0 if h == 0 else sb_recip1)[0:1, :],
                        bank(6 + h)[0:1, 0:512]), sm_v))
                    bankfree[6 + h] = trs[h]
                return trs

            def flush_epi():
                if not pend[0]:
                    return
                qb, trs = pend[0]
                pend[0] = None
                for h in range(2):
                    bcb = 1 + 2 * h
                    wt("t", trs[h])
                    wt("t", bankfree[bcb])
                    tok_bc = em("t", lambda t, bcb=bcb, h=h: t.matmul(
                        bank(bcb), sb_ones[:, :],
                        (sb_recip0 if h == 0 else sb_recip1)[0:1, :],
                        start=True, stop=True), sm_t)
                    tok_bc_prev[h] = tok_bc
                    wt("v", tok_bc)
                    tcp = em("v", lambda v, bcb=bcb: v.tensor_copy(
                        b_bcast, bank(bcb)), sm_v)
                    bankfree[bcb] = tcp
                    wt("v", tcp)
                    tmu = em("v", lambda v, h=h: v.tensor_tensor(
                        b_otmp, bank(4 + h), b_bcast, ALU.mult), sm_v)
                    bankfree[4 + h] = tmu
                    wt("v", tmu)
                    wt("v", tok_osum_chain[0])
                    tok_osum_chain[0] = em("v", lambda v, qb=qb: v.tensor_tensor(
                        b_osum[:, qb * 512:qb * 512 + 512],
                        b_osum[:, qb * 512:qb * 512 + 512], b_otmp, ALU.add), sm_v)

            flush_fn[0] = flush_epi
            for qb in range(2):
                tok_pv = None
                for kc in range(64):
                    tok_sc, te1 = emit_scores(qb, kc)
                    tok_score_last = tok_sc
                    tok_exp_prev_pair = te1
                    if kc == 2:
                        flush_epi()
                    if kc >= 2:
                        tok_pv = emit_pv(qb, kc - 2, epi_banks_wait=(kc == 2))
                tok_pv = emit_pv(qb, 62, False)
                tok_pv = emit_pv(qb, 63, False)
                tok_V_free = tok_pv
                trs = emit_epi_v1(qb, tok_pv)
                pend[0] = (qb, trs)
            tok_K_free = [tok_score_last, tok_score_last]
            tok_Q_free = tok_score_last
            tok_w_free = {"q": tok_Q_ready, "k": tok_K_ready[1], "v": tok_V_ready}

        flush_fn[0]()
        wt("v", tok_bvsum)
        wt("v", tok_osum_chain[0])
        tb1 = em("v", lambda v: v.tensor_scalar(
            b_osum, b_osum, scalar1=sb_bvsum[:, :], scalar2=None, op0=ALU.add), sm_v)
        wt("v", tb1)
        tok_osbf = em("v", lambda v: v.tensor_copy(b_osbf, b_osum), sm_v)

        if debug >= 2:
            wt("y", tok_osbf)
            dma("y", dbg["K0"].ap(), b_K[0], "cwy")
            dma("y", dbg["K1"].ap(), b_K[1], "cwy")
            dma("y", dbg["V0"].ap(), b_V, "cwy")
            for qch in range(3):
                dma("y", bass.AP(dbg["Q0"], qch * TOK, [[3 * TOK, 128], [1, TOK]]), b_Q[qch], "cwy")
            dma("y", dbg["osum"].ap(), b_osum, "cwy")

        def _phase_c():
            sched.barrier()
            C = Carver(0, OS_BASE)
            c_xn2F = [C.take(TOK, BF16) for _ in range(32)]
            c_rows = C.take(D, F32)
            c_la = C.take(32 * 128, BF16).rearrange("p (c f) -> p c f", c=32)
            c_lb = C.take(D, BF16)
            c_hT = C.take(TOK, BF16)
            c_h = C.take(128, F32)
            c_outs = [C.take(512, F32) for _ in range(2)]
            c_woT = C.take(D, BF16)
            c_biasbc = C.take(D, F32)
            C_MAIN_BASE = C.off
            c_x = C.take(D, F32)
            c_x2 = C.take(D, F32)
            c_xn2 = C.take(D, F32)

            dma("g", c_la, d_laT.ap().rearrange("p (c f) -> p c f", c=32), "cw")
            dma("g", c_lb, d_lbT.ap(), "cw")
            tok_woT = dma("g", c_woT, d_woT.ap(), "cw")
            tok_la = tok_lb = tok_woT
            tok_borow = dma("y", c_rows[0:1, :], d_bo.ap(), "cwy")

            tok_bo_bc = None
            wt("t", tok_borow)
            for ob in range(8):
                bb = 6 + (ob % 2)
                wt("t", bankfree[bb])
                tm = em("t", lambda t, bb=bb, ob=ob: t.matmul(
                    bank(bb), sb_ones[:, :], c_rows[0:1, ob * 512:(ob + 1) * 512],
                    start=True, stop=True), sm_t)
                tok_bcast_mm_last = tm
                wt("v", tm)
                tok_bo_bc = em("v", lambda v, bb=bb, ob=ob: v.tensor_copy(
                    c_biasbc[:, ob * 512:(ob + 1) * 512], bank(bb)), sm_v)
                bankfree[bb] = tok_bo_bc

            tok_x_used = None
            tok_x2_dma_last = None
            tok_xn2_cons = None
            tok_x2w = []
            tok_evS_last = None
            tok_evV_last = None
            tok_op_last = None
            for tt in range(8):
                wt("y", tok_x_used)
                tok_xl = dma("y", c_x, d_x[tt * 128:(tt + 1) * 128, :], "x0")
                wt("t", tok_osbf)
                wt("t", tok_woT)
                for ob in range(8):
                    wt("t", bankfree[ob])
                    tm = em("t", lambda t, tt=tt, ob=ob: t.matmul(
                        bank(ob), b_osbf[:, tt * 128:tt * 128 + 128],
                        c_woT[:, ob * 512:(ob + 1) * 512], start=True, stop=True),
                        sm_t if ob == 7 else None)
                tok_op_last = tm
                wt("v", tm)
                wt("v", tok_bo_bc)
                wt("v", tok_x2_dma_last)
                t1 = em("v", lambda v: v.tensor_tensor(
                    c_x2, ps[:, 0:4096], c_biasbc, ALU.add), sm_v)
                for ob in range(8):
                    bankfree[ob] = t1
                wt("v", t1)
                wt("v", tok_xl)
                tok_ev_all = em("v", lambda v: v.tensor_tensor(
                    c_x2, c_x2, c_x, ALU.add), sm_v)
                tok_x_used = tok_ev_all
                wt("y", tok_ev_all)
                tw = dma("y", bass.AP(d_x2, tt * 128 * 512,
                                      [[512, 128], [TOK * 512, 8], [1, 512]]),
                         c_x2.rearrange("p (a b) -> p a b", a=8), "x2w")
                if debug >= 3:
                    tw = dma("y", bass.AP(dbg["x2"], tt * 128 * 512,
                                          [[512, 128], [TOK * 512, 8], [1, 512]]),
                             c_x2.rearrange("p (a b) -> p a b", a=8), "x2w")
                tok_x2w.append(tw)
                tok_x2_dma_last = tok_x2w[-1]
                t9 = emit_ln(c_x2, c_xn2, c_xn2, tok_ev_all, tok_ev_all,
                             extra_s_wait=tok_xn2_cons)
                wt("t", t9)
                last_tp = None
                for dc in range(32):
                    b = 5 + (dc % 3)
                    wt("t", bankfree[b])
                    tok_tp = em("t", lambda t, b=b, dc=dc: t.transpose(
                        bank(b)[:, 0:128], c_xn2[:, dc * 128:dc * 128 + 128], sb_id[:, :]), sm_t)
                    last_tp = tok_tp
                    eng = "s" if dc % 2 == 0 else "v"
                    wt(eng, tok_tp)
                    if eng == "s":
                        tok_ev = em(eng, lambda e, b=b, dc=dc, tt=tt: e.copy(
                            c_xn2F[dc][:, tt * 128:tt * 128 + 128], bank(b)[:, 0:128]), sm_s)
                    else:
                        tok_ev = em(eng, lambda e, b=b, dc=dc, tt=tt: e.tensor_copy(
                            c_xn2F[dc][:, tt * 128:tt * 128 + 128], bank(b)[:, 0:128]), sm_v)
                    bankfree[b] = tok_ev
                    if eng == "s":
                        tok_evS_last = tok_ev
                    else:
                        tok_evV_last = tok_ev
                tok_xn2_cons = last_tp

            wt("y", tok_bcast_mm_last)
            tok_fbrow = dma("y", c_rows[0:1, :], d_fb.ap(), "cwy")
            tok_fb_bc = None
            wt("t", tok_fbrow)
            wt("t", tok_x_used)
            for ob in range(8):
                bb = 6 + (ob % 2)
                wt("t", bankfree[bb])
                tm = em("t", lambda t, bb=bb, ob=ob: t.matmul(
                    bank(bb), sb_ones[:, :], c_rows[0:1, ob * 512:(ob + 1) * 512],
                    start=True, stop=True), sm_t)
                wt("v", tm)
                tok_fb_bc = em("v", lambda v, bb=bb, ob=ob: v.tensor_copy(
                    c_biasbc[:, ob * 512:(ob + 1) * 512], bank(bb)), sm_v)
                bankfree[bb] = tok_fb_bc

            wt("t", tok_la)
            wt("t", tok_evS_last)
            wt("t", tok_evV_last)
            tok_hT_done = None
            tok_h_cons = None
            for tt in range(8):
                b = tt % 2
                wt("t", bankfree[b])
                tok_mm = None
                for dc in range(32):
                    tok_mm = em("t", lambda t, b=b, tt=tt, dc=dc: t.matmul(
                        bank(b)[:, 0:128], c_xn2F[dc][:, tt * 128:tt * 128 + 128],
                        c_la[:, dc, :], start=(dc == 0), stop=(dc == 31)),
                        sm_t if dc == 31 else None)
                wt("v", tok_mm)
                wt("v", tok_h_cons)
                tcp = em("v", lambda v, b=b: v.tensor_copy(c_h, bank(b)[:, 0:128]), sm_v)
                bankfree[b] = tcp
                tb_ = 2 + (tt % 2)
                wt("t", tcp)
                wt("t", bankfree[tb_])
                ttp = em("t", lambda t, tb_=tb_: t.transpose(
                    bank(tb_)[:, 0:128], c_h, sb_id[:, :]), sm_t)
                tok_h_cons = ttp
                wt("v", ttp)
                tok_hT_done = em("v", lambda v, tb_=tb_, tt=tt: v.tensor_scalar(
                    c_hT[:, tt * 128:tt * 128 + 128], bank(tb_)[:, 0:128],
                    scalar1=sb_coef[:, :], scalar2=None, op0=ALU.mult), sm_v)
                bankfree[tb_] = tok_hT_done

            CM = Carver(C_MAIN_BASE, OS_BASE)
            c_wf = [CM.take(16 * 512, BF16).rearrange("p (c f) -> p c f", c=16) for _ in range(2)]
            c_x2s = [CM.take(8 * 512, F32).rearrange("p (c f) -> p c f", c=8) for _ in range(2)]

            tok_wfh_free = [None, None]
            tok_x2s_free = [None, None]
            tok_outs_free = [None, None]
            x2w_all = tok_x2w[-1]
            for ob in range(8):
                wb = ob % 2
                if ob == 0:
                    wt("g", tok_x_used)
                    wt("g", tok_xn2_cons)
                    wt("g", x2w_all)
                    wt("y", tok_x_used)
                    wt("y", tok_xn2_cons)
                tok_wfh = [None, None]
                for hf in range(2):
                    wt("g", tok_wfh_free[hf])
                    wt("g", tok_wfh_free[1])
                    tok_wfh[hf] = dma("g", c_wf[hf],
                                      bass.AP(d_wfT, ob * (32 * 512) + hf * (16 * 512),
                                              [[8 * 32 * 512, 128], [512, 16], [1, 512]]),
                                      "wf0" if hf == 0 else "wf1")
                wt("y", x2w_all)
                wt("y", tok_x2s_free[wb])
                tok_x2s = dma("y", c_x2s[wb], bass.AP(d_x2, ob * TOK * 512,
                                                      [[512, 128], [128 * 512, 8], [1, 512]]),
                              "x2r0" if wb == 0 else "x2r1")
                if ob == 0:
                    wt("t", tok_lb)
                    wt("t", tok_hT_done)
                tok_lora_mm = [None] * 8
                for hf in range(2):
                    wt("t", tok_wfh[hf])
                    for tt in range(8):
                        if hf == 0:
                            wt("t", bankfree[tt])
                        for dci in range(16):
                            dc = hf * 16 + dci
                            em("t", lambda t, tt=tt, hf=hf, dc=dc, dci=dci: t.matmul(
                                bank(tt), c_xn2F[dc][:, tt * 128:tt * 128 + 128],
                                c_wf[hf][:, dci, :], start=(dc == 0), stop=False))
                        if hf == 1:
                            tok_lora_mm[tt] = em("t", lambda t, tt=tt, ob=ob: t.matmul(
                                bank(tt), c_hT[:, tt * 128:tt * 128 + 128],
                                c_lb[:, ob * 512:(ob + 1) * 512],
                                start=False, stop=True), sm_t)
                        else:
                            if tt == 7:
                                tok_wfh_free[0] = em("t", lambda t: t.nop(), sm_t)
                tok_wfh_free[1] = tok_lora_mm[7]
                for tt in range(8):
                    oi = tt % 2
                    wt("v", tok_lora_mm[tt])
                    if ob == 0 and tt == 0:
                        wt("v", tok_fb_bc)
                    wt("v", tok_outs_free[oi])
                    t1 = em("v", lambda v, tt=tt, ob=ob, oi=oi: v.tensor_tensor(
                        c_outs[oi], bank(tt), c_biasbc[:, ob * 512:(ob + 1) * 512], ALU.add), sm_v)
                    bankfree[tt] = t1
                    wt("v", t1)
                    wt("v", tok_x2s)
                    t2 = em("v", lambda v, oi=oi, wb=wb, tt=tt: v.tensor_tensor(
                        c_outs[oi], c_outs[oi], c_x2s[wb][:, tt, :], ALU.add), sm_v)
                    wt("y", t2)
                    tw = dma("y", bass.AP(d_out, tt * 128 * D + ob * 512,
                                          [[D, 128], [1, 512]]), c_outs[oi],
                             "o0" if oi == 0 else "o1")
                    tok_outs_free[oi] = tw
                    if tt == 7:
                        tok_x2s_free[wb] = t2


        if run_c:
            _phase_c()

        for _sm in dma_sems.values():
            if _sm.n > 0:
                sched.wait("y", (_sm, _sm.n))

        with nc.Block() as block:
            block.tensor(lambda eng: [f(eng) for f in sched.ops["t"]])
            block.vector(lambda eng: [f(eng) for f in sched.ops["v"]])
            block.scalar(lambda eng: [f(eng) for f in sched.ops["s"]])
            block.gpsimd(lambda eng: [f(eng) for f in sched.ops["g"]])
            block.sync(lambda eng: [f(eng) for f in sched.ops["y"]])

    return nc


def _stage(inputs):
    g = lambda k: np.asarray(inputs[k], np.float32)
    X = g("L0_attn_hidden_size").reshape(S, D)
    Wq_c = g("L0_attn_Wq_c")
    bq_c = g("L0_attn_bq_c")
    Wq_up = g("L0_attn_Wq_up")
    bq_up = g("L0_attn_bq_up")
    Wkv_c = g("L0_attn_Wkv_c")
    bkv_c = g("L0_attn_bkv_c")
    Wkv_up = g("L0_attn_Wkv_up")
    bkv_up = g("L0_attn_bkv_up")
    Wo = g("L0_attn_Wo")
    bo = g("L0_attn_bo")
    lora_A = g("L0_ffn_lora_A")
    lora_B = g("L0_ffn_lora_B")
    Wf = g("L0_ffn_W")
    fb = g("L0_ffn_bias")
    lw = g("L0_ffn_lora_weights")

    C = np.ascontiguousarray

    PERM, KPERM, VPERM = [], [], []
    for p in range(NPAIR):
        h0, h1 = 2 * p, 2 * p + 1
        PERM += list(range(h0 * QH, h0 * QH + NOPE))
        PERM += list(range(h1 * QH, h1 * QH + NOPE))
        PERM += list(range(h0 * QH + NOPE, h0 * QH + QH))
        PERM += list(range(h1 * QH + NOPE, h1 * QH + QH))
        KPERM += list(range(h0 * 256, h0 * 256 + 128))
        KPERM += list(range(h1 * 256, h1 * 256 + 128))
        VPERM += list(range(h0 * 256 + 128, h0 * 256 + 256))
        VPERM += list(range(h1 * 256 + 128, h1 * 256 + 256))

    base = {}
    base["wqcT"] = C(Wq_c.T)
    base["bqc_t"] = C(bq_c.reshape(12, 128).T)
    wqupT = Wq_up[PERM, :].T
    base["wqup_til"] = C(wqupT.reshape(12, 128, NPAIR, 384).transpose(1, 2, 0, 3)
                         .reshape(128, NPAIR, 12 * 384))
    base["bqup_t"] = C(bq_up[PERM].reshape(48, 128).T)
    wkvcT = Wkv_c.T
    base["wkvc_til"] = C(wkvcT.reshape(32, 128, 576).transpose(1, 0, 2).reshape(128, 32 * 576))
    bkvc_p = np.zeros(640, np.float32)
    bkvc_p[:576] = bkv_c
    base["bkvc_t"] = C(bkvc_p.reshape(5, 128).T)
    wkKT = Wkv_up[KPERM, :].T
    base["wkK_til"] = C(wkKT.reshape(4, 128, NPAIR, 256).transpose(1, 2, 0, 3)
                        .reshape(128, NPAIR, 4 * 256))
    base["bkupK_t"] = C(bkv_up[KPERM].reshape(32, 128).T)
    wkVT = Wkv_up[VPERM, :].T
    base["wkV_til"] = C(wkVT.reshape(4, 128, NPAIR, 256).transpose(1, 2, 0, 3)
                        .reshape(128, NPAIR, 4 * 256))
    base["bkvV_cols"] = C(bkv_up[VPERM].reshape(32, 128).T)
    base["woT"] = C(Wo.T)
    base["bo_row"] = C(bo.reshape(1, D))
    WfT = Wf.T
    base["wffn_til"] = C(WfT.reshape(32, 128, 8, 512).transpose(1, 2, 0, 3)
                         .reshape(128, 8, 32 * 512))
    base["ffnb_row"] = C(fb.reshape(1, D))
    laT = lora_A.reshape(128, D).T
    base["loraA_til"] = C(laT.reshape(32, 128, 128).transpose(1, 0, 2).reshape(128, 32 * 128))
    base["loraBT"] = C(lora_B.transpose(0, 2, 1).reshape(128, D))
    base["lwrep"] = C(np.repeat(lw, 16)[:, None])
    base["ident"] = np.eye(128, dtype=np.float32)
    base["ones_row"] = np.ones((1, 128), np.float32)
    base["onecol_bf"] = np.ones((128, 1), ml_dtypes.bfloat16)

    in_maps = []
    for c in range(NCORE):
        m = dict(base)
        m["x_rot"] = C(np.concatenate([X[TOK * c:], X[:TOK * c]], axis=0))
        in_maps.append(m)
    return in_maps


def kernel(**inputs):
    in_maps = _stage(inputs)
    nc = build_nc(debug=_DEBUG)
    res = run_bass_kernel_spmd(nc, in_maps, core_ids=list(range(NCORE)))
    if _DEBUG:
        kernel._last = res
    out = np.concatenate([np.asarray(res.results[c]["out_c"]) for c in range(NCORE)], axis=0)
    return out.reshape(1, S, D).astype(np.float32)
